# version 02
# baseline (speedup 1.0000x reference)
import numpy as np
import ml_dtypes
from contextlib import ExitStack
import concourse.bass as bass
import concourse.mybir as mybir
from concourse.bass_utils import run_bass_kernel_spmd

F32 = mybir.dt.float32
BF16 = mybir.dt.bfloat16
U32 = mybir.dt.uint32
AF = mybir.ActivationFunctionType
ALU = mybir.AluOpType
AX = mybir.AxisListType
IOA = bass.IndirectOffsetOnAxis

NB, NSEQ, D = 4, 8192, 1024
NCORE = 8
NLOC = NSEQ // NCORE
ROWS = NB * NLOC
LNEPS = 1e-5
SCAP = 30.0


class Sem:
    def __init__(self, es, nc, name):
        self.h = es.enter_context(nc.semaphore(name))
        self.v = 0

    def bump(self, instr, n=1):
        instr.then_inc(self.h, n)
        self.v += n
        return self.v


def build_stage_a():
    nc = bass.Bass("TRN2", target_bir_lowering=False)
    x0_h = nc.dram_tensor("x0", [ROWS, D], F32, kind="ExternalInput").ap()
    wq_h = nc.dram_tensor("wq", [D, D], F32, kind="ExternalInput").ap()
    wk_h = nc.dram_tensor("wk", [D, D], F32, kind="ExternalInput").ap()
    wv_h = nc.dram_tensor("wv", [D, D], F32, kind="ExternalInput").ap()
    x1_h = nc.dram_tensor("x1", [ROWS, D], F32, kind="ExternalOutput").ap()
    ysc = nc.dram_tensor("ysc", [ROWS, D], F32).ap()
    cc_in = nc.dram_tensor("cc_in", [1, 512], F32).ap()
    cc_out = nc.dram_tensor("cc_out", [1, 512], F32).ap()

    es = ExitStack()
    t = lambda name, shape, dt: es.enter_context(nc.sbuf_tensor(name, shape, dt))
    wq = t("wqs", [128, 8 * D], BF16)
    wk = t("wks", [128, 8 * D], BF16)
    wv = t("wvs", [128, 8 * D], BF16)
    X = t("X", [128, 4096], F32)
    H = t("H", [128, 4096], F32)
    Hb = t("Hb", [128, 4096], BF16)
    hfm = t("hfm", [128, 4096], BF16)
    Qfm = t("Qfm", [128, 4096], BF16)
    Kfm = t("Kfm", [128, 4096], BF16)
    Qdup = t("Qdup", [128, 8192], BF16)
    Vtm = t("Vtm", [128, 4096], BF16)
    Vaug = t("Vaug", [128, 64 * 65], BF16)
    PT = [t(f"PT{i}", [128, 1024], BF16) for i in range(2)]
    onsb = t("onsb", [64, 512], F32)
    outTsb = t("outTsb", [64, 512], BF16)
    rec = t("rec", [1, 512], F32)
    Yslab = t("Yslab", [128, 4096], F32)
    scr = t("scr", [128, 1024], F32)
    sqd = t("sqd", [128, 1024], BF16)
    stat = t("stat", [128, 4], F32)
    SS = t("SS", [128, 8], F32)
    MR = t("MR", [128, 8], F32)
    stg = t("stg", [1, 8], F32)
    stg512 = t("stg512", [1, 512], F32)
    idm = t("idm", [128, 128], F32)
    ident = t("ident", [128, 128], BF16)
    ones1_64 = t("ones1_64", [1, 64], F32)
    ones1_128 = t("ones1_128", [1, 128], F32)
    ones128_1 = t("ones128_1", [128, 1], F32)
    cb_eps = t("cb_eps", [128, 1], F32)
    cb_cap = t("cb_cap", [128, 1], F32)
    XA = [t(f"XA{i}", [128, 1024], F32) for i in range(2)]
    YA = [t(f"YA{i}", [128, 1024], F32) for i in range(2)]
    X1T = [t(f"X1T{i}", [128, 1024], F32) for i in range(2)]

    pSpair = [es.enter_context(nc.psum_tensor(f"pSp{i}", [128, 1024], F32)) for i in range(2)]
    pAV = es.enter_context(nc.psum_tensor("pAV", [128, 512], F32))
    pMisc = es.enter_context(nc.psum_tensor("pMisc", [128, 512], F32))
    pTp = [es.enter_context(nc.psum_tensor(f"pTp{i}", [128, 128], BF16)) for i in range(2)]

    S = {}
    for name in ["s_w", "s_ld", "s_pre", "s_hb", "s_tp", "s_te", "s_mm", "s_ev",
                 "s_qkv", "s_S", "s_exp", "s_av", "s_rec", "s_bc", "s_outT",
                 "s_tp2", "s_yadd", "s_ysd", "s_yst", "s_st", "s_cc", "s_mr",
                 "s_x1l", "s_x1c", "s_x1s", "s_c", "s_r", "s_ccr", "s_ccc"]:
        S[name] = Sem(es, nc, name)

    with es:
        block = es.enter_context(nc.Block())

        @block.sync
        def _(sync):
            for g in range(8):
                b, sb2 = g // 2, g % 2
                row0 = b * 1024 + sb2 * 512
                if g >= 1:
                    sync.wait_ge(S["s_hb"].h, g)
                for tt_ in range(4):
                    S["s_ld"].bump(sync.dma_start(
                        out=X[:, tt_ * 1024:(tt_ + 1) * 1024],
                        in_=x0_h[row0 + 128 * tt_: row0 + 128 * (tt_ + 1), :]), 16)
                sync.wait_ge(S["s_yst"].h, g + 1)
                for tt_ in range(4):
                    S["s_ysd"].bump(sync.dma_start(
                        out=ysc[row0 + 128 * tt_: row0 + 128 * (tt_ + 1), :],
                        in_=Yslab[:, tt_ * 1024:(tt_ + 1) * 1024]), 16)
            sync.wait_ge(S["s_ysd"].h, 16 * 32)
            for i in range(32):
                p = i % 2
                if i >= 2:
                    sync.wait_ge(S["s_x1c"].h, i - 1)
                S["s_x1l"].bump(sync.dma_start(
                    out=XA[p][:], in_=x0_h[128 * i:128 * (i + 1), :]), 16)
                S["s_x1l"].bump(sync.dma_start(
                    out=YA[p][:], in_=ysc[128 * i:128 * (i + 1), :]), 16)
                sync.wait_ge(S["s_x1c"].h, i + 1)
                S["s_x1s"].bump(sync.dma_start(
                    out=x1_h[128 * i:128 * (i + 1), :], in_=X1T[p][:]), 16)
            sync.wait_ge(S["s_x1s"].h, 16 * 32)

        @block.gpsimd
        def _(gpsimd):
            S["s_w"].bump(gpsimd.dma_start(
                out=wq[:].rearrange("p (dc c) -> p dc c", dc=8),
                in_=wq_h.rearrange("(dc p) c -> p dc c", p=128)), 16)
            S["s_w"].bump(gpsimd.dma_start(
                out=wk[:].rearrange("p (dc c) -> p dc c", dc=8),
                in_=wk_h.rearrange("(dc p) c -> p dc c", p=128)), 16)
            S["s_w"].bump(gpsimd.dma_start(
                out=wv[:].rearrange("p (dc c) -> p dc c", dc=8),
                in_=wv_h.rearrange("(dc p) c -> p dc c", p=128)), 16)
            gpsimd.iota(out=idm[:], pattern=[[1, 128]], base=0, channel_multiplier=-1,
                        allow_small_or_imprecise_dtypes=True)
            S["s_pre"].bump(gpsimd.memset(Vaug[:], 1.0))
            gpsimd.wait_ge(S["s_st"].h, 3)
            S["s_cc"].bump(gpsimd.dma_start(out=cc_in[:], in_=stg512[:]), 16)
            gpsimd.wait_ge(S["s_cc"].h, 16)
            gpsimd.collective_compute(
                "AllReduce", ALU.add, replica_groups=[list(range(NCORE))],
                ins=[cc_in[:]], outs=[cc_out[:]]).then_inc(S["s_ccc"].h, 1)
            gpsimd.wait_ge(S["s_ccc"].h, 1)
            S["s_ccr"].bump(gpsimd.dma_start(out=stg512[:], in_=cc_out[:]), 16)

        @block.vector
        def _(vector):
            vector.wait_ge(S["s_pre"].h, 1)
            vector.tensor_scalar(out=ident[:], in0=idm[:], scalar1=0.0, scalar2=None,
                                 op0=ALU.is_equal)
            vector.memset(ones1_64[:], 1.0)
            vector.memset(cb_eps[:], LNEPS)
            vector.memset(cb_cap[:], -SCAP)
            vector.memset(ones1_128[:], 1.0)
            vector.memset(stg512[:], 0.0)
            vector.memset(ones128_1[:], 1.0)
            S["s_pre"].bump(vector.memset(SS[:], 0.0))

            for g in range(8):
                b = g // 2
                if g >= 1:
                    vector.wait_ge(S["s_ysd"].h, 64 * g)
                for tt_ in range(4):
                    vector.wait_ge(S["s_ld"].h, 16 * (4 * g + tt_ + 1))
                    xap = X[:, tt_ * 1024:(tt_ + 1) * 1024]
                    vector.tensor_reduce(out=stat[:, 0:1], in_=xap, axis=AX.X, op=ALU.add)
                    vector.drain()
                    vector.tensor_scalar(out=stat[:, 1:2], in0=stat[:, 0:1],
                                         scalar1=1.0 / 1024, scalar2=None, op0=ALU.mult)
                    vector.drain()
                    v = S["s_c"].bump(vector.tensor_scalar(
                        out=scr[:], in0=xap, scalar1=stat[:, 1:2], scalar2=None,
                        op0=ALU.subtract))
                    vector.wait_ge(S["s_r"].h, v)
                    vector.tensor_scalar(out=H[:, tt_ * 1024:(tt_ + 1) * 1024],
                                         in0=scr[:], scalar1=stat[:, 3:4], scalar2=None,
                                         op0=ALU.mult)
                S["s_hb"].bump(vector.tensor_copy(out=Hb[:], in_=H[:]))
                for i in range(32):
                    rt_, dc = i // 8, i % 8
                    vector.wait_ge(S["s_tp"].h, 32 * g + i + 1)
                    S["s_te"].bump(vector.tensor_copy(
                        out=hfm[:, dc * 512 + rt_ * 128: dc * 512 + (rt_ + 1) * 128],
                        in_=pTp[i % 2][:]))
                for i in range(24):
                    vector.wait_ge(S["s_mm"].h, 24 * g + i + 1)
                    pp = pSpair[i % 2][:, 0:512]
                    if i < 8:
                        dst = Qfm[:, i * 512:(i + 1) * 512]
                    elif i < 16:
                        dst = Kfm[:, (i - 8) * 512:(i - 7) * 512]
                    else:
                        k = i - 16
                        rt_, ch = k // 2, k % 2
                        dst = Vtm[:, rt_ * 1024 + ch * 512: rt_ * 1024 + (ch + 1) * 512]
                    S["s_ev"].bump(vector.tensor_copy(out=dst, in_=pp))
                for sq in range(16):
                    src = Qfm[(sq % 2) * 64:(sq % 2) * 64 + 64,
                              (sq // 2) * 512:(sq // 2 + 1) * 512]
                    vector.tensor_copy(out=Qdup[0:64, sq * 512:(sq + 1) * 512], in_=src)
                    vector.tensor_copy(out=Qdup[64:128, sq * 512:(sq + 1) * 512], in_=src)
                for s_ in range(16):
                    for rt_ in range(4):
                        cid = s_ * 4 + rt_
                        vector.tensor_copy(
                            out=Vaug[:, cid * 65: cid * 65 + 64],
                            in_=Vtm[:, rt_ * 1024 + s_ * 64: rt_ * 1024 + (s_ + 1) * 64])
                S["s_qkv"].bump(vector.engine_nop())
                for sq in range(16):
                    qi = 16 * g + sq
                    vector.wait_ge(S["s_av"].h, qi + 1)
                    vector.reciprocal(out=rec[:], in_=pAV[64:65, :])
                    S["s_rec"].bump(vector.tensor_copy(out=onsb[:], in_=pAV[0:64, :]))
                    vector.wait_ge(S["s_bc"].h, qi + 1)
                    vector.drain()
                    S["s_outT"].bump(vector.tensor_tensor(
                        out=outTsb[:], in0=onsb[:], in1=pMisc[0:64, :], op=ALU.mult))
                    for rt_ in range(4):
                        vector.wait_ge(S["s_tp2"].h, 4 * qi + rt_ + 1)
                        col = rt_ * 1024 + sq * 64
                        S["s_yadd"].bump(vector.tensor_tensor(
                            out=Yslab[:, col: col + 64],
                            in0=pTp[rt_ % 2][:, 0:64], in1=H[:, col: col + 64],
                            op=ALU.add))
                for tt_ in range(4):
                    yap = Yslab[:, tt_ * 1024:(tt_ + 1) * 1024]
                    vector.tensor_reduce(out=stat[:, 0:1], in_=yap, axis=AX.X, op=ALU.add)
                    vector.tensor_tensor(out=scr[:], in0=yap, in1=yap, op=ALU.mult)
                    vector.tensor_reduce(out=stat[:, 1:2], in_=scr[:], axis=AX.X, op=ALU.add)
                    vector.tensor_tensor(out=SS[:, b:b + 1], in0=SS[:, b:b + 1],
                                         in1=stat[:, 0:1], op=ALU.add)
                    vector.tensor_tensor(out=SS[:, 4 + b:5 + b], in0=SS[:, 4 + b:5 + b],
                                         in1=stat[:, 1:2], op=ALU.add)
                    vector.drain()
                S["s_yst"].bump(vector.engine_nop())

            vector.wait_ge(S["s_st"].h, 2)
            vector.tensor_copy(out=stg[:], in_=pMisc[0:1, 0:8])
            vector.drain()
            S["s_st"].bump(vector.tensor_copy(out=stg512[:, 0:8], in_=stg[:]))
            vector.wait_ge(S["s_ccr"].h, 16)
            vector.tensor_copy(out=stg[:], in_=stg512[:, 0:8])
            ninv = 1.0 / (NSEQ * D)
            vector.tensor_scalar(out=stg[:, 0:4], in0=stg[:, 0:4], scalar1=ninv,
                                 scalar2=None, op0=ALU.mult)
            vector.tensor_scalar(out=stg[:, 4:8], in0=stg[:, 4:8], scalar1=ninv,
                                 scalar2=None, op0=ALU.mult)
            vector.drain()
            vector.tensor_tensor(out=scr[0:1, 0:4], in0=stg[:, 0:4], in1=stg[:, 0:4],
                                 op=ALU.mult)
            vector.drain()
            S["s_st"].bump(vector.tensor_tensor(
                out=stg[:, 4:8], in0=stg[:, 4:8], in1=scr[0:1, 0:4],
                op=ALU.subtract))
            vector.wait_ge(S["s_mr"].h, 2)
            S["s_mr"].bump(vector.tensor_copy(out=MR[:], in_=pMisc[0:128, 0:8]))
            vector.drain()
            for i in range(32):
                p = i % 2
                b = i // 8
                vector.wait_ge(S["s_x1l"].h, 32 * (i + 1))
                if i >= 2:
                    vector.wait_ge(S["s_x1s"].h, 16 * (i - 1))
                vector.tensor_scalar(out=X1T[p][:], in0=YA[p][:],
                                     scalar1=MR[:, b:b + 1], scalar2=MR[:, 4 + b:5 + b],
                                     op0=ALU.subtract, op1=ALU.mult)
                S["s_x1c"].bump(vector.tensor_tensor(
                    out=X1T[p][:], in0=X1T[p][:], in1=XA[p][:], op=ALU.add))

        @block.scalar
        def _(scalar):
            for g in range(8):
                for tt_ in range(4):
                    scalar.wait_ge(S["s_c"].h, 4 * g + tt_ + 1)
                    scalar.activation(out=sqd[:], in_=scr[:], func=AF.Square,
                                      accum_out=stat[:, 2:3])
                    scalar.drain()
                    scalar.activation(out=stat[:, 3:4], in_=stat[:, 2:3], func=AF.Ln,
                                      scale=1.0 / 1024, bias=cb_eps[:])
                    scalar.drain()
                    S["s_r"].bump(scalar.activation(out=stat[:, 3:4], in_=stat[:, 3:4],
                                                    func=AF.Exp, scale=-0.5))
                for sq in range(16):
                    for p_ in range(32):
                        idx = (16 * g + sq) * 32 + p_
                        scalar.wait_ge(S["s_S"].h, idx + 1)
                        S["s_exp"].bump(scalar.activation(
                            out=PT[p_ % 2][:], in_=pSpair[p_ % 2][:], func=AF.Exp,
                            bias=cb_cap[:]))
            scalar.wait_ge(S["s_st"].h, 4)
            scalar.activation(out=stg[:, 4:8], in_=stg[:, 4:8], func=AF.Ln,
                              bias=cb_eps[0:1, 0:1])
            scalar.drain()
            S["s_st"].bump(scalar.activation(out=stg[:, 4:8], in_=stg[:, 4:8],
                                             func=AF.Exp, scale=-0.5))

        @block.tensor
        def _(tensor):
            tensor.wait_ge(S["s_w"].h, 48)
            tensor.wait_ge(S["s_pre"].h, 2)
            for g in range(8):
                tensor.wait_ge(S["s_hb"].h, g + 1)
                if g >= 1:
                    tensor.wait_ge(S["s_yadd"].h, 64 * g)
                    tensor.wait_ge(S["s_exp"].h, 512 * g)
                for i in range(32):
                    rt_, dc = i // 8, i % 8
                    if 32 * g + i >= 2:
                        tensor.wait_ge(S["s_te"].h, 32 * g + i - 1)
                    S["s_tp"].bump(tensor.transpose(
                        out=pTp[i % 2][:],
                        in_=Hb[:, rt_ * 1024 + dc * 128: rt_ * 1024 + (dc + 1) * 128],
                        identity=ident[:]))
                tensor.wait_ge(S["s_te"].h, 32 * (g + 1))
                for i in range(24):
                    mmi = 24 * g + i
                    if mmi >= 2:
                        tensor.wait_ge(S["s_ev"].h, mmi - 1)
                    pdst = pSpair[i % 2][:, 0:512]
                    for dc in range(8):
                        if i < 8:
                            lh = wq[:, dc * 1024 + i * 128: dc * 1024 + (i + 1) * 128]
                            rh = hfm[:, dc * 512:(dc + 1) * 512]
                        elif i < 16:
                            cc = i - 8
                            lh = wk[:, dc * 1024 + cc * 128: dc * 1024 + (cc + 1) * 128]
                            rh = hfm[:, dc * 512:(dc + 1) * 512]
                        else:
                            k = i - 16
                            rt_, ch = k // 2, k % 2
                            lh = hfm[:, dc * 512 + rt_ * 128: dc * 512 + (rt_ + 1) * 128]
                            rh = wv[:, dc * 1024 + ch * 512: dc * 1024 + ch * 512 + 512]
                        mm = tensor.matmul(out=pdst, lhsT=lh, rhs=rh,
                                           start=(dc == 0), stop=(dc == 7))
                    S["s_mm"].bump(mm)
                tensor.wait_ge(S["s_qkv"].h, g + 1)
                for sq in range(16):
                    qi = 16 * g + sq
                    if qi >= 1:
                        tensor.wait_ge(S["s_rec"].h, qi)
                        tensor.wait_ge(S["s_outT"].h, qi)
                    for p_ in range(32):
                        cc, rt_ = p_ // 4, p_ % 4
                        idx = qi * 32 + p_
                        if idx >= 2:
                            tensor.wait_ge(S["s_exp"].h, idx - 1)
                        kA = Kfm[0:64, cc * 512 + rt_ * 128: cc * 512 + (rt_ + 1) * 128]
                        kB = Kfm[64:128, cc * 512 + rt_ * 128: cc * 512 + (rt_ + 1) * 128]
                        tensor.matmul(out=pSpair[p_ % 2][:, 0:512], lhsT=kA,
                                      rhs=Qdup[0:64, sq * 512:(sq + 1) * 512],
                                      start=True, stop=True, tile_position=(0, 0))
                        S["s_S"].bump(tensor.matmul(
                            out=pSpair[p_ % 2][:, 512:1024], lhsT=kB,
                            rhs=Qdup[64:128, sq * 512:(sq + 1) * 512],
                            start=True, stop=True, tile_position=(64, 0)))
                        tensor.wait_ge(S["s_exp"].h, idx + 1)
                        cidA = (2 * cc) * 4 + rt_
                        cidB = (2 * cc + 1) * 4 + rt_
                        tensor.matmul(out=pAV[0:65, :],
                                      lhsT=Vaug[:, cidA * 65: cidA * 65 + 65],
                                      rhs=PT[p_ % 2][:, 0:512],
                                      start=(p_ == 0), stop=False)
                        mm = tensor.matmul(out=pAV[0:65, :],
                                           lhsT=Vaug[:, cidB * 65: cidB * 65 + 65],
                                           rhs=PT[p_ % 2][:, 512:1024],
                                           start=False, stop=(p_ == 31))
                    S["s_av"].bump(mm)
                    tensor.wait_ge(S["s_rec"].h, qi + 1)
                    S["s_bc"].bump(tensor.matmul(
                        out=pMisc[0:64, :], lhsT=ones1_64[:], rhs=rec[:],
                        start=True, stop=True))
                    tensor.wait_ge(S["s_outT"].h, qi + 1)
                    for rt_ in range(4):
                        if 4 * qi + rt_ >= 2:
                            tensor.wait_ge(S["s_yadd"].h, 4 * qi + rt_ - 1)
                        S["s_tp2"].bump(tensor.transpose(
                            out=pTp[rt_ % 2][:, 0:64],
                            in_=outTsb[:, rt_ * 128:(rt_ + 1) * 128],
                            identity=ident[0:64, 0:64]))
            tensor.wait_ge(S["s_yst"].h, 8)
            S["s_st"].bump(tensor.matmul(out=pMisc[0:1, 0:8], lhsT=ones128_1[:],
                                         rhs=SS[:], start=True, stop=True), 2)
            tensor.wait_ge(S["s_st"].h, 5)
            S["s_mr"].bump(tensor.matmul(out=pMisc[0:128, 0:8], lhsT=ones1_128[:],
                                         rhs=stg[:], start=True, stop=True), 2)

    return nc


def run_stage_a(inputs, x0_full=None):
    x0 = np.ascontiguousarray((x0_full if x0_full is not None
                               else inputs["L0_attn_x"]).reshape(NB, NSEQ, D))
    wqT = np.ascontiguousarray(inputs["L0_attn_Wq"].T)
    wkT = np.ascontiguousarray(inputs["L0_attn_Wk"].T)
    wvT = np.ascontiguousarray(inputs["L0_attn_Wv"].T)
    in_maps = []
    for j in range(NCORE):
        xs = np.ascontiguousarray(x0[:, j * NLOC:(j + 1) * NLOC, :]).reshape(ROWS, D)
        in_maps.append({"x0": xs, "wq": wqT, "wk": wkT, "wv": wvT})
    nc = build_stage_a()
    res = run_bass_kernel_spmd(nc, in_maps, core_ids=list(range(NCORE)))
    x1 = np.zeros((NB, NSEQ, D), np.float32)
    for j in range(NCORE):
        x1[:, j * NLOC:(j + 1) * NLOC, :] = res.results[j]["x1"].reshape(NB, NLOC, D)
    return x1


AGR = 4104

def build_stage_b():
    nc = bass.Bass("TRN2", target_bir_lowering=False)
    x1_h = nc.dram_tensor("x1", [ROWS, D], F32, kind="ExternalInput").ap()
    eid_h = nc.dram_tensor("eid", [128, 1], F32, kind="ExternalInput").ap()
    wg_h = nc.dram_tensor("wg", [D, 8], F32, kind="ExternalInput").ap()
    w1_h = nc.dram_tensor("w1", [D, 4096], F32, kind="ExternalInput").ap()
    w2_h = nc.dram_tensor("w2", [4096, D], F32, kind="ExternalInput").ap()
    x2_h = nc.dram_tensor("x2", [ROWS, D], F32, kind="ExternalOutput").ap()
    agin = nc.dram_tensor("agin", [AGR, D], BF16).ap()
    agout = nc.dram_tensor("agout", [8 * AGR, D], BF16, addr_space="Shared").ap()
    ei_h = nc.dram_tensor("ei", [4096, D], BF16).ap()
    eo_h = nc.dram_tensor("eo", [4096, D], BF16).ap()
    egout = nc.dram_tensor("egout", [8 * 4096, D], BF16, addr_space="Shared").ap()
    offsc = nc.dram_tensor("offsc", [256, 8], F32).ap()

    es = ExitStack()
    t = lambda name, shape, dt: es.enter_context(nc.sbuf_tensor(name, shape, dt))
    w1s = t("w1s", [128, 8 * 4096], BF16)
    w2s = t("w2s", [128, 32 * 1024], BF16)
    wgs = t("wgs", [128, 64], F32)
    XT = [t(f"XT{i}", [128, 1024], F32) for i in range(2)]
    T1 = [t(f"T1{i}", [128, 1024], F32) for i in range(2)]
    T1b = [t(f"T1b{i}", [128, 1024], BF16) for i in range(2)]
    t1fm_s = t("t1fm", [128, 1024], F32)
    t1fm = [t1fm_s, t1fm_s]
    Lg = [t(f"Lg{i}", [128, 8], F32) for i in range(2)]
    EGsb = [t(f"EGsb{i}", [128, 2], BF16) for i in range(2)]
    EIDXGATE = t("EIDXGATE", [2, 4096], BF16)
    scr = t("scrB", [128, 1024], F32)
    stat = t("statB", [128, 4], F32)
    m8 = t("m8", [128, 2], F32)
    ohr = t("ohrB", [128, 8], F32)
    e8 = t("e8B", [128, 8], F32)
    eidxf = t("eidxf", [128, 1], F32)
    gatef = t("gatef", [128, 1], F32)
    iota8 = t("iota8", [128, 8], F32)
    IOTA64 = t("IOTA64", [128, 64], F32)
    lane6 = t("lane6", [128, 1], F32)
    lane7 = t("lane7", [128, 1], F32)
    idm = t("idmB", [128, 128], F32)
    identf = t("identf", [128, 128], F32)
    identb = t("identb", [128, 128], BF16)
    lstrb = t("lstrb", [128, 128], BF16)
    lstrf = t("lstrf", [128, 128], F32)
    ones1_128f = t("o1_128f", [1, 128], F32)
    ones128_1b = t("o128_1b", [128, 1], BF16)
    cb_eps = t("cb_epsB", [128, 1], F32)
    eid = t("eid_sb", [128, 1], F32)
    mg8 = t("mg8", [128, 1], F32)
    KEEP8 = t("KEEP8", [128, 8], F32)
    IDX2f = t("IDX2f", [128, 32], F32)
    GAT2f = t("GAT2f", [128, 32], F32)
    EIDXall = t("EIDXall", [128, 1024], BF16)
    GATEall = t("GATEall", [128, 1024], BF16)
    OHall = t("OHall", [128, 2048], BF16)
    CNT = t("CNT", [8, 256], F32)
    Ctile = t("Ctile", [64, 8], F32)
    OFFA4 = t("OFFA4", [64, 8], F32)
    OFFROW = t("OFFROW", [1, 512], F32)
    TRr = [t(f"TRr{i}", [128, 1024], BF16) for i in range(2)]
    IDXu = [t(f"IDXu{i}", [128, 8], U32) for i in range(2)]
    POSF64 = t("POSF64", [128, 64], F32)
    OHF = t("OHF", [128, 64], F32)
    PS8 = t("PS8", [128, 8], F32)
    KK8 = t("KK8", [128, 8], F32)
    AA8 = t("AA8", [128, 8], F32)
    BB8 = t("BB8", [128, 8], F32)
    IDX2u = t("IDX2u", [128, 32], U32)
    GAT2 = t("GAT2", [128, 32], F32)
    EGL = t("EGL", [8, 1024], BF16)
    EIT = [t(f"EIT{i}", [128, 1024], BF16) for i in range(2)]
    zeros = EIT[0]
    eifm = [t(f"eifm{i}", [128, 1024], BF16) for i in range(2)]
    HBt = t("HBt", [128, 2048], BF16)
    hfmB = t("hfmB", [128, 2048], BF16)
    EOT = T1b
    GOT = TRr
    X2T = T1

    pA = es.enter_context(nc.psum_tensor("pA", [128, 512], F32))
    pB = es.enter_context(nc.psum_tensor("pB", [128, 512], F32))
    pC = es.enter_context(nc.psum_tensor("pC", [128, 512], F32))
    pD = es.enter_context(nc.psum_tensor("pD", [128, 512], F32))
    pG = es.enter_context(nc.psum_tensor("pG", [128, 512], F32))
    pT1 = es.enter_context(nc.psum_tensor("pT1B", [128, 512], BF16))
    pT2 = es.enter_context(nc.psum_tensor("pT2B", [128, 512], BF16))

    S = {}
    for name in ["s_w", "s_pre", "s_x1ld", "s_c", "s_r", "s_t1c", "s_m8", "s_rt",
                 "s_tp", "s_tpe", "s_lg", "s_lge", "s_eg", "s_egt", "s_ege",
                 "s_agw", "s_ag1", "s_eiz", "s_egl", "s_egx", "s_egxe",
                 "s_oh", "s_cnt", "s_cnte", "s_ctp", "s_ctpe", "s_off", "s_offe", "s_offd", "s_offl",
                 "s_pos", "s_pose", "s_idx", "s_sc", "s_trl",
                 "s_eild", "s_eitp", "s_eitpe", "s_h", "s_hg", "s_htp", "s_htpe",
                 "s_eo", "s_eoe", "s_eos", "s_ag2", "s_gz", "s_ga",
                 "s_x2l", "s_x2c", "s_x2s"]:
        S[name] = Sem(es, nc, name)

    NT = 32

    with es:
        block = es.enter_context(nc.Block())

        @block.sync
        def _(sync):
            sync.dma_start(out=wgs[:].rearrange("p (dc e) -> p dc e", dc=8),
                           in_=wg_h.rearrange("(dc p) e -> p dc e", p=128)
                           ).then_inc(S["s_w"].h, 16)
            sync.dma_start(out=eid[:], in_=eid_h[:]).then_inc(S["s_w"].h, 16)
            for i in range(NT):
                p = i % 2
                if i >= 2:
                    sync.wait_ge(S["s_t1c"].h, i - 1)
                S["s_x1ld"].bump(sync.dma_start(
                    out=XT[p][:], in_=x1_h[128 * i:128 * (i + 1), :]), 16)
                sync.wait_ge(S["s_t1c"].h, i + 1)
                S["s_agw"].bump(sync.dma_start(
                    out=agin[128 * i:128 * (i + 1), :], in_=T1b[p][:]), 16)
            sync.wait_ge(S["s_ege"].h, NT)
            S["s_agw"].bump(sync.dma_start(
                out=agin[4096:4104, :].rearrange("(a b) c -> a (b c)", a=2),
                in_=EIDXGATE[:]), 16)
            sync.wait_ge(S["s_ag1"].h, 1)
            for r in range(8):
                if r >= 1:
                    sync.wait_ge(S["s_egxe"].h, 8 * r)
                S["s_egl"].bump(sync.dma_start(
                    out=EGL[:], in_=agout[r * AGR + 4096: r * AGR + 4104, :]), 16)
            for b_ in range(4):
                sync.wait_ge(S["s_offe"].h, b_ + 1)
                S["s_offd"].bump(sync.dma_start(
                    out=offsc[b_ * 64:(b_ + 1) * 64, :], in_=OFFA4[:]), 16)
            for gi in range(32):
                b, g8 = gi // 8, gi % 8
                if g8 == 0:
                    sync.wait_ge(S["s_offd"].h, 16 * (b + 1))
                    if b >= 1:
                        sync.wait_ge(S["s_pose"].h, 8 * b)
                    S["s_offl"].bump(sync.dma_start(
                        out=OFFROW[:],
                        in_=offsc[b * 64:(b + 1) * 64, :].rearrange(
                            "r c -> (r c)").rearrange("(a f) -> a f", a=1)), 16)
                for tloc in range(8):
                    kk_ = gi * 8 + tloc
                    if kk_ >= 2:
                        sync.wait_ge(S["s_sc"].h, 16 * (kk_ - 1))
                    S["s_trl"].bump(sync.dma_start(
                        out=TRr[kk_ % 2][:],
                        in_=agout[g8 * AGR + b * 1024 + tloc * 128:
                                  g8 * AGR + b * 1024 + (tloc + 1) * 128, :]), 16)
            sync.wait_ge(S["s_sc"].h, 16 * 256)
            for st in range(32):
                p = st % 2
                if st >= 2:
                    sync.wait_ge(S["s_eitpe"].h, 8 * (st - 1))
                S["s_eild"].bump(sync.dma_start(
                    out=EIT[p][:], in_=ei_h[128 * st:128 * (st + 1), :]), 16)
                sync.wait_ge(S["s_eoe"].h, 2 * (st + 1))
                S["s_eos"].bump(sync.dma_start(
                    out=eo_h[128 * st:128 * (st + 1), :], in_=EOT[p][:]), 16)
            sync.wait_ge(S["s_ag2"].h, 1)
            for i in range(NT):
                p = i % 2
                if i >= 2:
                    sync.wait_ge(S["s_x2c"].h, i - 1)
                S["s_x2l"].bump(sync.dma_start(
                    out=XT[p][:], in_=x1_h[128 * i:128 * (i + 1), :]), 16)
                sync.wait_ge(S["s_x2c"].h, i + 1)
                S["s_x2s"].bump(sync.dma_start(
                    out=x2_h[128 * i:128 * (i + 1), :], in_=X2T[p][:]), 16)
            sync.wait_ge(S["s_x2s"].h, 16 * NT)

        @block.gpsimd
        def _(gpsimd):
            S["s_w"].bump(gpsimd.dma_start(
                out=w1s[:].rearrange("p (dc h) -> p dc h", dc=8),
                in_=w1_h.rearrange("(dc p) h -> p dc h", p=128)), 16)
            S["s_w"].bump(gpsimd.dma_start(
                out=w2s[:].rearrange("p (fc d) -> p fc d", fc=32),
                in_=w2_h.rearrange("(fc p) d -> p fc d", p=128)), 16)
            gpsimd.iota(out=idm[:], pattern=[[1, 128]], base=0, channel_multiplier=-1,
                        allow_small_or_imprecise_dtypes=True)
            gpsimd.iota(out=iota8[:], pattern=[[1, 8]], base=0, channel_multiplier=0,
                        allow_small_or_imprecise_dtypes=True)
            gpsimd.iota(out=IOTA64[:], pattern=[[0, 8], [1, 8]], base=0,
                        channel_multiplier=0, allow_small_or_imprecise_dtypes=True)
            gpsimd.iota(out=lane6[:], pattern=[[0, 1]], base=1000000,
                        channel_multiplier=1, allow_small_or_imprecise_dtypes=True)
            S["s_pre"].bump(gpsimd.iota(
                out=lane7[:], pattern=[[0, 1]], base=10000000, channel_multiplier=1,
                allow_small_or_imprecise_dtypes=True))
            gpsimd.wait_ge(S["s_pre"].h, 2)
            for st in range(32):
                S["s_eiz"].bump(gpsimd.dma_start(
                    out=ei_h[128 * st:128 * (st + 1), :], in_=zeros[:]), 16)
            gpsimd.wait_ge(S["s_agw"].h, 16 * (NT + 1))
            gpsimd.collective_compute(
                "AllGather", ALU.bypass, replica_groups=[list(range(NCORE))],
                ins=[agin[:]], outs=[agout[:]]).then_inc(S["s_ag1"].h, 1)
            gpsimd.wait_ge(S["s_eiz"].h, 16 * 32)
            for gi in range(32):
                for tloc in range(8):
                    kk_ = gi * 8 + tloc
                    gpsimd.wait_ge(S["s_trl"].h, 16 * (kk_ + 1))
                    gpsimd.wait_ge(S["s_idx"].h, gi + 1)
                    S["s_sc"].bump(gpsimd.indirect_dma_start(
                        out=ei_h[:], out_offset=IOA(IDXu[gi % 2][:, tloc:tloc + 1], 0),
                        in_=TRr[kk_ % 2][:], in_offset=None, oob_is_err=False), 16)
            gpsimd.wait_ge(S["s_eos"].h, 16 * 32)
            gpsimd.collective_compute(
                "AllGather", ALU.bypass, replica_groups=[list(range(NCORE))],
                ins=[eo_h[:]], outs=[egout[:]]).then_inc(S["s_ag2"].h, 1)
            gpsimd.wait_ge(S["s_ag2"].h, 1)
            for i in range(NT):
                p = i % 2
                gpsimd.wait_ge(S["s_gz"].h, i + 1)
                S["s_ga"].bump(gpsimd.indirect_dma_start(
                    out=GOT[p][:], out_offset=None, in_=egout[:],
                    in_offset=IOA(IDX2u[:, i:i + 1], 0), oob_is_err=False), 16)

        @block.scalar
        def _(scalar):
            for i in range(NT):
                p = i % 2
                scalar.wait_ge(S["s_c"].h, i + 1)
                scalar.activation(out=OHall[:, 0:1024], in_=scr[:], func=AF.Square,
                                  accum_out=stat[:, 2:3])
                scalar.drain()
                scalar.activation(out=stat[:, 3:4], in_=stat[:, 2:3], func=AF.Ln,
                                  scale=1.0 / 1024, bias=cb_eps[:])
                scalar.drain()
                S["s_r"].bump(scalar.activation(out=stat[:, 3:4], in_=stat[:, 3:4],
                                                func=AF.Exp, scale=-0.5))
                scalar.wait_ge(S["s_m8"].h, i + 1)
                S["s_rt"].bump(scalar.activation(out=e8[:], in_=Lg[p][:], func=AF.Exp,
                                                 bias=m8[:, 1:2]))
            for st in range(32):
                for hc in range(8):
                    scalar.wait_ge(S["s_h"].h, 8 * st + hc + 1)
                    S["s_hg"].bump(scalar.activation(
                        out=HBt[:, (hc % 4) * 512:(hc % 4 + 1) * 512],
                        in_=(pA if hc % 2 == 0 else pB)[:],
                        func=AF.Gelu_apprx_tanh))

        @block.vector
        def _(vector):
            vector.wait_ge(S["s_pre"].h, 1)
            vector.tensor_scalar(out=identf[:], in0=idm[:], scalar1=0.0, scalar2=None,
                                 op0=ALU.is_equal)
            vector.tensor_scalar(out=identb[:], in0=idm[:], scalar1=0.0, scalar2=None,
                                 op0=ALU.is_equal)
            vector.tensor_scalar(out=lstrb[:], in0=idm[:], scalar1=0.0, scalar2=None,
                                 op0=ALU.is_gt)
            vector.tensor_scalar(out=lstrf[:], in0=idm[:], scalar1=0.0, scalar2=None,
                                 op0=ALU.is_gt)
            vector.memset(ones1_128f[:], 1.0)
            vector.memset(ones128_1b[:], 1.0)
            vector.memset(cb_eps[:], LNEPS)
            S["s_pre"].bump(vector.memset(zeros[:], 0.0))
            for i in range(NT):
                p = i % 2
                vector.wait_ge(S["s_x1ld"].h, 16 * (i + 1))
                xap = XT[p][:]
                vector.tensor_reduce(out=stat[:, 0:1], in_=xap, axis=AX.X, op=ALU.add)
                vector.drain()
                vector.tensor_scalar(out=stat[:, 1:2], in0=stat[:, 0:1],
                                     scalar1=1.0 / 1024, scalar2=None, op0=ALU.mult)
                vector.drain()
                v = S["s_c"].bump(vector.tensor_scalar(
                    out=scr[:], in0=xap, scalar1=stat[:, 1:2], scalar2=None,
                    op0=ALU.subtract))
                vector.wait_ge(S["s_r"].h, v)
                vector.tensor_scalar(out=T1[p][:], in0=scr[:], scalar1=stat[:, 3:4],
                                     scalar2=None, op0=ALU.mult)
                S["s_t1c"].bump(vector.tensor_copy(out=T1b[p][:], in_=T1[p][:]))
                for dc in range(8):
                    vector.wait_ge(S["s_tp"].h, 8 * i + dc + 1)
                    S["s_tpe"].bump(vector.tensor_copy(
                        out=t1fm[p][:, dc * 128:(dc + 1) * 128],
                        in_=(pA if dc % 2 == 0 else pB)[:, 0:128]))
                vector.wait_ge(S["s_lg"].h, i + 1)
                S["s_lge"].bump(vector.tensor_copy(out=Lg[p][:],
                                                   in_=(pC if p == 0 else pD)[:, 0:8]))
                vector.drain()
                vector.tensor_reduce(out=m8[:, 0:1], in_=Lg[p][:], axis=AX.X, op=ALU.max)
                S["s_m8"].bump(vector.tensor_reduce(out=m8[:, 1:2], in_=Lg[p][:],
                                                    axis=AX.X, op=ALU.max, negate=True))
                vector.drain()
                vector.tensor_scalar(out=ohr[:], in0=Lg[p][:], scalar1=m8[:, 0:1],
                                     scalar2=None, op0=ALU.is_equal)
                vector.drain()
                vector.tensor_scalar(out=ohr[:], in0=ohr[:], scalar1=1000.0,
                                     scalar2=None, op0=ALU.mult)
                vector.drain()
                vector.tensor_tensor(out=ohr[:], in0=iota8[:], in1=ohr[:],
                                     op=ALU.subtract)
                vector.drain()
                vector.tensor_reduce(out=eidxf[:], in_=ohr[:], axis=AX.X, op=ALU.min)
                vector.drain()
                vector.tensor_scalar(out=eidxf[:], in0=eidxf[:], scalar1=1000.0,
                                     scalar2=None, op0=ALU.add)
                vector.drain()
                vector.wait_ge(S["s_rt"].h, i + 1)
                vector.tensor_reduce(out=gatef[:], in_=e8[:], axis=AX.X, op=ALU.add)
                vector.drain()
                vector.reciprocal(out=gatef[:], in_=gatef[:])
                vector.drain()
                if i >= 2:
                    vector.wait_ge(S["s_egt"].h, i - 1)
                vector.tensor_copy(out=EGsb[p][:, 0:1], in_=eidxf[:])
                S["s_eg"].bump(vector.tensor_copy(out=EGsb[p][:, 1:2], in_=gatef[:]))
                vector.wait_ge(S["s_egt"].h, i + 1)
                S["s_ege"].bump(vector.tensor_copy(
                    out=EIDXGATE[:, i * 128:(i + 1) * 128], in_=pT1[0:2, 0:128]))
            for r in range(8):
                for tloc in range(8):
                    vector.wait_ge(S["s_egx"].h, 8 * r + tloc + 1)
                    pt = pT1 if tloc % 2 == 0 else pT2
                    k = r * 8 + tloc
                    vector.tensor_copy(out=EIDXall[:, k * 4:(k + 1) * 4],
                                       in_=pt[:, 0:4])
                    S["s_egxe"].bump(vector.tensor_copy(
                        out=GATEall[:, k * 4:(k + 1) * 4], in_=pt[:, 4:8]))
            EIv = EIDXall[:].rearrange("p (k four) -> p k four", four=4)
            GAv = GATEall[:].rearrange("p (k four) -> p k four", four=4)
            for gi in range(32):
                b, g8 = gi // 8, gi % 8
                eg8 = EIv[:, g8 * 8:(g8 + 1) * 8, b]
                vector.tensor_tensor(
                    out=OHall[:, gi * 64:(gi + 1) * 64].rearrange(
                        "p (t e) -> p t e", t=8),
                    in0=IOTA64[:].rearrange("p (t e) -> p t e", t=8),
                    in1=eg8.unsqueeze(2).broadcast_to([128, 8, 8]),
                    op=ALU.is_equal)
                S["s_oh"].bump(vector.engine_nop())
            vector.wait_ge(S["s_cnt"].h, 256)
            S["s_cnte"].bump(vector.tensor_copy(out=CNT[:], in_=pG[0:8, 0:256]))
            for b in range(4):
                vector.wait_ge(S["s_ctp"].h, b + 1)
                S["s_ctpe"].bump(vector.tensor_copy(out=Ctile[:],
                                                    in_=pG[0:64, 256:264]))
                vector.wait_ge(S["s_off"].h, b + 1)
                if b >= 1:
                    vector.wait_ge(S["s_offd"].h, 16 * b)
                S["s_offe"].bump(vector.tensor_copy(
                    out=OFFA4[:], in_=pG[0:64, 264:272]))
            for gi in range(32):
                b, g8 = gi // 8, gi % 8
                vector.wait_ge(S["s_pos"].h, gi + 1)
                ppos = pA if gi % 2 == 0 else pB
                vector.tensor_copy(out=POSF64[:], in_=ppos[:, 0:64])
                vector.tensor_copy(out=OHF[:], in_=OHall[:, gi * 64:(gi + 1) * 64])
                vector.drain()
                vector.tensor_tensor(out=OHF[:], in0=OHF[:], in1=POSF64[:],
                                     op=ALU.mult)
                vector.drain()
                S["s_pose"].bump(vector.tensor_reduce(
                    out=PS8[:], in_=OHF[:].rearrange("p (t e) -> p t e", t=8),
                    axis=AX.X, op=ALU.add))
                vector.drain()
                vector.tensor_scalar(out=KEEP8[:], in0=PS8[:], scalar1=1024.0,
                                     scalar2=None, op0=ALU.is_lt)
                eg8 = EIv[:, g8 * 8:(g8 + 1) * 8, b]
                vector.tensor_scalar(out=AA8[:], in0=eg8, scalar1=eid[:],
                                     scalar2=None, op0=ALU.is_equal)
                vector.drain()
                vector.tensor_tensor(out=KK8[:], in0=KEEP8[:], in1=AA8[:], op=ALU.mult)
                vector.drain()
                vector.tensor_scalar(out=AA8[:], in0=PS8[:], scalar1=1.0,
                                     scalar2=float(b * 1024), op0=ALU.mult,
                                     op1=ALU.add)
                vector.drain()
                vector.tensor_tensor(out=BB8[:], in0=AA8[:],
                                     in1=lane6[:].broadcast_to([128, 8]),
                                     op=ALU.subtract)
                vector.drain()
                vector.tensor_tensor(out=BB8[:], in0=BB8[:], in1=KK8[:], op=ALU.mult)
                vector.drain()
                vector.tensor_tensor(out=BB8[:], in0=BB8[:],
                                     in1=lane6[:].broadcast_to([128, 8]), op=ALU.add)
                vector.drain()
                if gi >= 2:
                    vector.wait_ge(S["s_sc"].h, 16 * 8 * (gi - 1))
                S["s_idx"].bump(vector.tensor_copy(out=IDXu[gi % 2][:], in_=BB8[:]))
                vector.tensor_scalar(out=mg8[:], in0=eid[:], scalar1=float(g8),
                                     scalar2=None, op0=ALU.is_equal)
                vector.tensor_scalar(out=BB8[:], in0=eg8, scalar1=4096.0,
                                     scalar2=None, op0=ALU.mult)
                vector.drain()
                vector.tensor_tensor(out=AA8[:], in0=AA8[:], in1=BB8[:], op=ALU.add)
                vector.drain()
                vector.tensor_tensor(out=BB8[:], in0=AA8[:],
                                     in1=lane7[:].broadcast_to([128, 8]),
                                     op=ALU.subtract)
                vector.drain()
                vector.tensor_tensor(out=BB8[:], in0=BB8[:], in1=KEEP8[:], op=ALU.mult)
                vector.drain()
                vector.tensor_tensor(out=BB8[:], in0=BB8[:],
                                     in1=lane7[:].broadcast_to([128, 8]), op=ALU.add)
                vector.drain()
                vector.tensor_scalar(out=BB8[:], in0=BB8[:], scalar1=mg8[:],
                                     scalar2=None, op0=ALU.mult)
                vector.drain()
                if g8 == 0:
                    vector.tensor_copy(out=IDX2f[:, b * 8:(b + 1) * 8], in_=BB8[:])
                else:
                    vector.tensor_tensor(out=IDX2f[:, b * 8:(b + 1) * 8],
                                         in0=IDX2f[:, b * 8:(b + 1) * 8], in1=BB8[:],
                                         op=ALU.add)
                vector.tensor_scalar(out=AA8[:], in0=GAv[:, g8 * 8:(g8 + 1) * 8, b],
                                     scalar1=mg8[:], scalar2=None, op0=ALU.mult)
                vector.drain()
                if g8 == 0:
                    vector.tensor_copy(out=GAT2f[:, b * 8:(b + 1) * 8], in_=AA8[:])
                else:
                    vector.tensor_tensor(out=GAT2f[:, b * 8:(b + 1) * 8],
                                         in0=GAT2f[:, b * 8:(b + 1) * 8], in1=AA8[:],
                                         op=ALU.add)
                vector.drain()
            vector.tensor_copy(out=IDX2u[:], in_=IDX2f[:])
            vector.tensor_copy(out=GAT2[:], in_=GAT2f[:])
            vector.drain()
            for st in range(32):
                p = st % 2
                for dc in range(8):
                    vector.wait_ge(S["s_eitp"].h, 8 * st + dc + 1)
                    S["s_eitpe"].bump(vector.tensor_copy(
                        out=eifm[p][:, dc * 128:(dc + 1) * 128],
                        in_=(pT1 if dc % 2 == 0 else pT2)[:, 0:128]))
                for fc in range(32):
                    vector.wait_ge(S["s_htp"].h, 32 * st + fc + 1)
                    S["s_htpe"].bump(vector.tensor_copy(
                        out=hfmB[:, (fc % 16) * 128:(fc % 16 + 1) * 128],
                        in_=(pT1 if fc % 2 == 0 else pT2)[:, 0:128]))
                if st >= 2:
                    vector.wait_ge(S["s_eos"].h, 16 * (st - 1))
                for dh in range(2):
                    vector.wait_ge(S["s_eo"].h, 2 * st + dh + 1)
                    S["s_eoe"].bump(vector.tensor_copy(
                        out=EOT[p][:, dh * 512:(dh + 1) * 512],
                        in_=(pC if dh == 0 else pD)[:]))
            for i in range(NT):
                p = i % 2
                S["s_gz"].bump(vector.memset(GOT[p][:], 0.0))
                vector.wait_ge(S["s_ga"].h, 16 * (i + 1))
                vector.wait_ge(S["s_x2l"].h, 16 * (i + 1))
                if i >= 2:
                    vector.wait_ge(S["s_x2s"].h, 16 * (i - 1))
                S["s_x2c"].bump(vector.scalar_tensor_tensor(
                    out=X2T[p][:], in0=GOT[p][:], scalar=GAT2[:, i:i + 1],
                    in1=XT[p][:], op0=ALU.mult, op1=ALU.add))

        @block.tensor
        def _(tensor):
            tensor.wait_ge(S["s_pre"].h, 2)
            tensor.wait_ge(S["s_w"].h, 64)
            for i in range(NT):
                p = i % 2
                tensor.wait_ge(S["s_t1c"].h, i + 1)
                for dc in range(8):
                    cnt = 8 * i + dc
                    if cnt >= 2:
                        tensor.wait_ge(S["s_tpe"].h, cnt - 1)
                    S["s_tp"].bump(tensor.transpose(
                        out=(pA if dc % 2 == 0 else pB)[:, 0:128],
                        in_=T1[p][:, dc * 128:(dc + 1) * 128], identity=identf[:]))
                tensor.wait_ge(S["s_tpe"].h, 8 * (i + 1))
                if i >= 2:
                    tensor.wait_ge(S["s_lge"].h, i - 1)
                for dc in range(8):
                    mm = tensor.matmul(out=(pC if p == 0 else pD)[:, 0:8],
                                       lhsT=t1fm[p][:, dc * 128:(dc + 1) * 128],
                                       rhs=wgs[:, dc * 8:(dc + 1) * 8],
                                       start=(dc == 0), stop=(dc == 7))
                S["s_lg"].bump(mm)
                tensor.wait_ge(S["s_eg"].h, i + 1)
                if i >= 1:
                    tensor.wait_ge(S["s_ege"].h, i)
                S["s_egt"].bump(tensor.transpose(
                    out=pT1[0:2, 0:128], in_=EGsb[p][:], identity=identb[:]))
            for r in range(8):
                tensor.wait_ge(S["s_egl"].h, 16 * (r + 1))
                for tloc in range(8):
                    n_ = 8 * r + tloc
                    if n_ >= 2:
                        tensor.wait_ge(S["s_egxe"].h, n_ - 1)
                    S["s_egx"].bump(tensor.transpose(
                        out=(pT1 if tloc % 2 == 0 else pT2)[0:128, 0:8],
                        in_=EGL[:, tloc * 128:(tloc + 1) * 128],
                        identity=identb[0:8, 0:8]))
            for gi in range(32):
                b, g8 = gi // 8, gi % 8
                tensor.wait_ge(S["s_oh"].h, gi + 1)
                for tloc in range(8):
                    k = g8 * 8 + tloc
                    col = b * 64 + k
                    S["s_cnt"].bump(tensor.matmul(
                        out=pG[0:8, col:col + 1],
                        lhsT=OHall[:, gi * 64 + tloc * 8: gi * 64 + (tloc + 1) * 8],
                        rhs=ones128_1b[:], start=True, stop=True))
            tensor.wait_ge(S["s_cnte"].h, 1)
            for b in range(4):
                if b >= 1:
                    tensor.wait_ge(S["s_ctpe"].h, b)
                S["s_ctp"].bump(tensor.transpose(
                    out=pG[0:64, 256:264], in_=CNT[:, b * 64:(b + 1) * 64],
                    identity=identf[0:8, 0:8]))
                tensor.wait_ge(S["s_ctpe"].h, b + 1)
                if b >= 1:
                    tensor.wait_ge(S["s_offe"].h, b)
                S["s_off"].bump(tensor.matmul(
                    out=pG[0:64, 264:272], lhsT=lstrf[0:64, 0:64], rhs=Ctile[:],
                    start=True, stop=True))
            for gi in range(32):
                b, g8 = gi // 8, gi % 8
                ppos = pA if gi % 2 == 0 else pB
                tensor.wait_ge(S["s_offl"].h, 16 * (b + 1))
                if gi >= 2:
                    tensor.wait_ge(S["s_pose"].h, gi - 1)
                for tloc in range(8):
                    tensor.matmul(out=ppos[:, tloc * 8:(tloc + 1) * 8],
                                  lhsT=lstrb[:],
                                  rhs=OHall[:, gi * 64 + tloc * 8:
                                            gi * 64 + (tloc + 1) * 8],
                                  start=(tloc == 0), stop=False)
                mm = tensor.matmul(out=ppos[:, 0:64], lhsT=ones1_128f[:],
                                   rhs=OFFROW[0:1, g8 * 64:(g8 + 1) * 64],
                                   start=False, stop=True)
                S["s_pos"].bump(mm)
            for st in range(32):
                p = st % 2
                tensor.wait_ge(S["s_eild"].h, 16 * (st + 1))
                for dc in range(8):
                    n_ = 8 * st + dc
                    if n_ >= 2:
                        tensor.wait_ge(S["s_eitpe"].h, n_ - 1)
                    S["s_eitp"].bump(tensor.transpose(
                        out=(pT1 if dc % 2 == 0 else pT2)[:, 0:128],
                        in_=EIT[p][:, dc * 128:(dc + 1) * 128], identity=identb[:]))
                tensor.wait_ge(S["s_eitpe"].h, 8 * (st + 1))
                if st >= 1:
                    tensor.wait_ge(S["s_eoe"].h, 2 * st)
                for half in range(2):
                    for hcl in range(4):
                        hc = half * 4 + hcl
                        n_ = 8 * st + hc
                        if n_ >= 2:
                            tensor.wait_ge(S["s_hg"].h, n_ - 1)
                        if half == 1 and hcl == 0:
                            pass
                        for dc in range(8):
                            mm = tensor.matmul(
                                out=(pA if hc % 2 == 0 else pB)[:],
                                lhsT=eifm[p][:, dc * 128:(dc + 1) * 128],
                                rhs=w1s[:, dc * 4096 + hc * 512: dc * 4096 + (hc + 1) * 512],
                                start=(dc == 0), stop=(dc == 7))
                        S["s_h"].bump(mm)
                    tensor.wait_ge(S["s_hg"].h, 8 * st + 4 * (half + 1))
                    for fcl in range(16):
                        fc = half * 16 + fcl
                        n_ = 32 * st + fc
                        if n_ >= 2:
                            tensor.wait_ge(S["s_htpe"].h, n_ - 1)
                        S["s_htp"].bump(tensor.transpose(
                            out=(pT1 if fc % 2 == 0 else pT2)[:, 0:128],
                            in_=HBt[:, fcl * 128:(fcl + 1) * 128], identity=identb[:]))
                    tensor.wait_ge(S["s_htpe"].h, 32 * st + 16 * (half + 1))
                    for dh in range(2):
                        for fcl in range(16):
                            fc = half * 16 + fcl
                            mm = tensor.matmul(
                                out=(pC if dh == 0 else pD)[:],
                                lhsT=hfmB[:, fcl * 128:(fcl + 1) * 128],
                                rhs=w2s[:, fc * 1024 + dh * 512: fc * 1024 + (dh + 1) * 512],
                                start=(half == 0 and fcl == 0),
                                stop=(half == 1 and fcl == 15))
                        if half == 1:
                            S["s_eo"].bump(mm)

    return nc


def run_stage_b(inputs, x1_full):
    x1 = np.ascontiguousarray(x1_full.reshape(NB, NSEQ, D))
    in_maps = []
    for j in range(NCORE):
        xs = np.ascontiguousarray(x1[:, j * NLOC:(j + 1) * NLOC, :]).reshape(ROWS, D)
        in_maps.append({"x1": xs,
                        "eid": np.full((128, 1), float(j), np.float32),
                        "wg": np.ascontiguousarray(inputs["L0_ffn_Wg"]),
                        "w1": np.ascontiguousarray(inputs["L0_ffn_W1"][j]),
                        "w2": np.ascontiguousarray(inputs["L0_ffn_W2"][j])})
    nc = build_stage_b()
    res = run_bass_kernel_spmd(nc, in_maps, core_ids=list(range(NCORE)))
    x2 = np.zeros((NB, NSEQ, D), np.float32)
    for j in range(NCORE):
        x2[:, j * NLOC:(j + 1) * NLOC, :] = res.results[j]["x2"].reshape(NB, NLOC, D)
    return x2



def build_stage_c():
    nc = bass.Bass("TRN2", target_bir_lowering=False)
    x2_h = nc.dram_tensor("x2", [ROWS, D], F32, kind="ExternalInput").ap()
    wts_h = {}
    for w in ["wr", "ww", "wk", "wv", "wa", "wg", "wo", "wf"]:
        wts_h[w] = nc.dram_tensor(w, [D, D], F32, kind="ExternalInput").ap()
    vec_h = {}
    for vname in ["mu_r", "mu_w", "mu_k", "mu_v", "mu_a", "mu_g", "w0", "a0",
                  "k_k", "k_a", "r_k", "lnx_w", "lnx_b", "bf", "norm_w",
                  "norm_b", "lnf_w", "lnf_b"]:
        vec_h[vname] = nc.dram_tensor(vname, [D], F32, kind="ExternalInput").ap()
    muarr_h = nc.dram_tensor("muarr", [128, 48], F32, kind="ExternalInput").ap()
    x3_h = nc.dram_tensor("x3", [ROWS, D], F32, kind="ExternalOutput").ap()
    psc = nc.dram_tensor("psc", [6 * ROWS, D], BF16).ap()
    att_sc = nc.dram_tensor("att_sc", [ROWS, D], BF16).ap()

    es = ExitStack()
    t = lambda name, shape, dt: es.enter_context(nc.sbuf_tensor(name, shape, dt))
    WS = [t(f"WSc{i}", [128, 8192], BF16) for i in range(2)]
    ufm4 = [t(f"ufm{i}", [128, 8192], BF16) for i in range(4)]
    PB = [t(f"PBc{i}", [128, 1024], F32) for i in range(7)]
    mu6 = t("mu6", [128, 48], F32)
    om6 = t("om6", [128, 48], F32)
    prow = t("prowC", [1, 512], F32)
    XCH = t("XCHc", [128, 8192], BF16)
    XT2 = [t(f"XT2c{i}", [128, 1024], F32) for i in range(2)]
    UT = [t(f"UTc{i}", [128, 1024], F32) for i in range(2)]
    UB = [t(f"UBc{i}", [128, 1024], BF16) for i in range(2)]
    PJT = [t(f"PJTc{i}", [128, 1024], BF16) for i in range(2)]
    WZ = t("WZc", [128, 512], F32)
    scr = t("scrC", [128, 1024], F32)
    scr2 = t("scr2C", [128, 1024], F32)
    stat = t("statC", [128, 4], F32)
    _v = lambda tn, a, b: tn[:][:, a:b]
    _vf = lambda tn, a, b: tn[:][:, a:b].bitcast(F32)
    KKN = type("A", (), {})()
    class _W:
        def __init__(self, ap): self._ap = ap
        def __getitem__(self, key): return self._ap if key == slice(None) else self._ap[key]
    KKN = _W(_vf(ufm4[0], 0, 2048))
    KP = _W(_vf(ufm4[0], 2048, 4096))
    KKA = _W(_vf(ufm4[0], 4096, 6144))
    Y = _W(_vf(ufm4[0], 6144, 8192))
    P3 = [_W(_vf(ufm4[1], 0, 2048)), _W(_vf(ufm4[1], 2048, 4096)),
          _W(_vf(ufm4[1], 4096, 6144))]
    ATTb = _W(_v(ufm4[1], 6144, 7168))
    Rb = [_W(_v(ufm4[2], 0, 1024)), _W(_v(ufm4[2], 1024, 2048))]
    Wb = [_W(_v(ufm4[2], 2048, 3072)), _W(_v(ufm4[2], 3072, 4096))]
    Kb = [_W(_v(ufm4[2], 4096, 5120)), _W(_v(ufm4[2], 5120, 6144))]
    Ab = [_W(_v(ufm4[2], 6144, 7168)), _W(_v(ufm4[2], 7168, 8192))]
    Gb = [_W(_v(ufm4[3], 0, 1024)), _W(_v(ufm4[3], 1024, 2048))]
    VT = [_W(_v(ufm4[3], 2048 + 1024 * k, 3072 + 1024 * k)) for k in range(4)]
    YGb = _W(_v(ufm4[3], 6144, 7168))
    ygfm = _W(_v(ufm4[3], 7168, 8192))
    H1b = _W(_v(XCH, 2048, 3072))
    h1fm = _W(_v(XCH, 3072, 4096))
    ATTl = [_W(_v(XCH, 4096, 5120)), _W(_v(XCH, 5120, 6144))]
    ST = t("STc", [128, 64], F32)
    CT = t("CTc", [128, 64], F32)
    X3T = [t(f"X3Tc{i}", [128, 1024], F32) for i in range(2)]
    idm = t("idmC", [128, 128], F32)
    identb = t("identbC", [128, 128], BF16)
    ones1_128f = t("o1_128fC", [1, 128], F32)
    cb_eps = t("cb_epsC", [128, 1], F32)
    cb_gn = t("cb_gnC", [128, 1], F32)

    pA = es.enter_context(nc.psum_tensor("pAc", [128, 512], F32))
    pB = es.enter_context(nc.psum_tensor("pBc", [128, 512], F32))
    pC = es.enter_context(nc.psum_tensor("pCc", [128, 512], F32))
    pD = es.enter_context(nc.psum_tensor("pDc", [128, 512], F32))
    pT1 = es.enter_context(nc.psum_tensor("pT1c", [128, 512], BF16))
    pT2 = es.enter_context(nc.psum_tensor("pT2c", [128, 512], BF16))
    pP = es.enter_context(nc.psum_tensor("pPc", [128, 512], F32))

    S = {}
    for name in ["c_mu", "c_pr", "c_pb", "c_pbe", "c_pre", "c_ld", "c_c", "c_r",
                 "c_ucast", "c_ut", "c_ute", "c_ws", "c_xch", "c_pj", "c_pje",
                 "c_pjs", "c_wz", "c_wr", "c_sld", "c_kst", "c_ksr", "c_gn1",
                 "c_gn2", "c_yg", "c_ygt", "c_ygte", "c_wo", "c_att", "c_atts",
                 "c_wol", "c_x2l4", "c_attl", "c_h1c", "c_h1t", "c_h1te",
                 "c_wf", "c_wfe", "c_x3c", "c_x3s"]:
        S[name] = Sem(es, nc, name)

    PBUILD1 = [(0, "w0"), (1, "a0"), (2, "k_k"), (3, "k_a"), (4, "r_k"),
               (5, "lnx_w"), (6, "lnx_b")]
    PBUILD2 = [(0, "bf"), (1, "norm_w"), (2, "norm_b"), (3, "lnf_w"), (4, "lnf_b")]
    CHI = ["wr", "ww", "wk", "wv", "wa", "wg"]

    with es:
        block = es.enter_context(nc.Block())

        @block.sync
        def _(sync):
            S["c_mu"].bump(sync.dma_start(out=mu6[:], in_=muarr_h[:]), 16)
            for bi, (slot, pn) in enumerate(PBUILD1):
                for hf in range(2):
                    sync.wait_ge(S["c_pb"].h, 2 * bi + hf)
                    S["c_pr"].bump(sync.dma_start(
                        out=prow[:], in_=vec_h[pn][hf * 512:(hf + 1) * 512]
                        .rearrange("(a f) -> a f", a=1)), 16)
            for i in range(32):
                p = i % 2
                if i >= 2:
                    sync.wait_ge(S["c_ucast"].h, i - 1)
                S["c_ld"].bump(sync.dma_start(
                    out=XT2[p][:], in_=x2_h[128 * i:128 * (i + 1), :]), 16)
            for ci in range(6):
                for tt_ in range(4):
                    for nt in range(8):
                        for hf in range(2):
                            n_ = ((ci * 4 + tt_) * 8 + nt) * 2 + hf
                            sync.wait_ge(S["c_pje"].h, n_ + 1)
                            row = ci * ROWS + tt_ * 1024 + nt * 128
                            S["c_pjs"].bump(sync.dma_start(
                                out=psc[row:row + 128, hf * 512:(hf + 1) * 512],
                                in_=PJT[n_ % 2][:, hf * 512:(hf + 1) * 512]), 16)
            for i in range(8):
                for tt_ in range(4):
                    it = i * 4 + tt_
                    pp = tt_ % 2
                    if it >= 1:
                        sync.wait_ge(S["c_yg"].h, it)
                    for ci, dst in [(0, Rb[pp]), (1, Wb[pp]), (2, Kb[pp]),
                                    (4, Ab[pp]), (5, Gb[pp])]:
                        S["c_sld"].bump(sync.dma_start(
                            out=dst[:],
                            in_=psc[ci * ROWS + tt_ * 1024 + i * 128:
                                    ci * ROWS + tt_ * 1024 + i * 128 + 128, :]), 16)
                    S["c_sld"].bump(sync.dma_start(
                        out=VT[tt_][:],
                        in_=psc[3 * ROWS + tt_ * 1024 + i * 128:
                                3 * ROWS + tt_ * 1024 + i * 128 + 128, :]), 16)
                    sync.wait_ge(S["c_att"].h, it + 1)
                    S["c_atts"].bump(sync.dma_start(
                        out=att_sc[tt_ * 1024 + i * 128: tt_ * 1024 + i * 128 + 128, :],
                        in_=ATTb[:]), 16)
            for bi, (slot, pn) in enumerate(PBUILD2):
                for hf in range(2):
                    sync.wait_ge(S["c_pb"].h, 14 + 2 * bi + hf)
                    S["c_pr"].bump(sync.dma_start(
                        out=prow[:], in_=vec_h[pn][hf * 512:(hf + 1) * 512]
                        .rearrange("(a f) -> a f", a=1)), 16)
            for i in range(32):
                p = i % 2
                if i >= 2:
                    sync.wait_ge(S["c_x3c"].h, i - 1)
                S["c_x2l4"].bump(sync.dma_start(
                    out=XT2[p][:], in_=x2_h[128 * i:128 * (i + 1), :]), 16)
                S["c_attl"].bump(sync.dma_start(
                    out=ATTl[p][:], in_=att_sc[128 * i:128 * (i + 1), :]), 16)
                sync.wait_ge(S["c_x3c"].h, i + 1)
                S["c_x3s"].bump(sync.dma_start(
                    out=x3_h[128 * i:128 * (i + 1), :], in_=X3T[p][:]), 16)
            sync.wait_ge(S["c_x3s"].h, 16 * 32)

        @block.gpsimd
        def _(gpsimd):
            gpsimd.iota(out=idm[:], pattern=[[1, 128]], base=0, channel_multiplier=-1,
                        allow_small_or_imprecise_dtypes=True)
            S["c_pre"].bump(gpsimd.engine_nop())
            for ci, wname in enumerate(CHI):
                if ci >= 2:
                    gpsimd.wait_ge(S["c_pj"].h, (ci - 1) * 64)
                S["c_ws"].bump(gpsimd.dma_start(
                    out=WS[ci % 2][:].rearrange("p (dc c) -> p dc c", dc=8),
                    in_=wts_h[wname].rearrange("(dc p) c -> p dc c", p=128)), 16)
            gpsimd.wait_ge(S["c_pj"].h, 6 * 64)
            S["c_wol"].bump(gpsimd.dma_start(
                out=WS[0][:].rearrange("p (dc c) -> p dc c", dc=8),
                in_=wts_h["wo"].rearrange("(dc p) c -> p dc c", p=128)), 16)
            S["c_wol"].bump(gpsimd.dma_start(
                out=WS[1][:].rearrange("p (dc c) -> p dc c", dc=8),
                in_=wts_h["wf"].rearrange("(dc p) c -> p dc c", p=128)), 16)

        @block.tensor
        def _(tensor):
            tensor.wait_ge(S["c_pre"].h, 2)
            for bi in range(len(PBUILD1)):
                for hf in range(2):
                    n_ = 2 * bi + hf
                    tensor.wait_ge(S["c_pr"].h, 16 * (n_ + 1))
                    if n_ >= 1:
                        tensor.wait_ge(S["c_pbe"].h, n_)
                    S["c_pb"].bump(tensor.matmul(
                        out=pP[:], lhsT=ones1_128f[:], rhs=prow[:],
                        start=True, stop=True))
            for i in range(32):
                p = i % 2
                tensor.wait_ge(S["c_ucast"].h, i + 1)
                for dc in range(8):
                    n_ = 8 * i + dc
                    if n_ >= 2:
                        tensor.wait_ge(S["c_ute"].h, n_ - 1)
                    S["c_ut"].bump(tensor.transpose(
                        out=(pT1 if dc % 2 == 0 else pT2)[:, 0:128],
                        in_=UB[p][:, dc * 128:(dc + 1) * 128], identity=identb[:]))
            for ci in range(6):
                tensor.wait_ge(S["c_ws"].h, 16 * (ci + 1))
                for tt_ in range(4):
                    tensor.wait_ge(S["c_xch"].h, 4 * ci + tt_ + 1)
                    for nt in range(8):
                        for hf in range(2):
                            n_ = ((ci * 4 + tt_) * 8 + nt) * 2 + hf
                            if n_ >= 2:
                                tensor.wait_ge(S["c_pje"].h, n_ - 1)
                            for dc in range(8):
                                mm = tensor.matmul(
                                    out=(pA if n_ % 2 == 0 else pB)[:],
                                    lhsT=XCH[:, dc * 1024 + nt * 128:
                                             dc * 1024 + (nt + 1) * 128],
                                    rhs=WS[ci % 2][:, dc * 1024 + hf * 512:
                                                   dc * 1024 + (hf + 1) * 512],
                                    start=(dc == 0), stop=(dc == 7))
                            S["c_pj"].bump(mm)
            tensor.wait_ge(S["c_wol"].h, 32)
            for i in range(8):
                for tt_ in range(4):
                    it = i * 4 + tt_
                    tensor.wait_ge(S["c_yg"].h, it + 1)
                    for dc in range(8):
                        n_ = 8 * it + dc
                        if n_ >= 2:
                            tensor.wait_ge(S["c_ygte"].h, n_ - 1)
                        S["c_ygt"].bump(tensor.transpose(
                            out=(pT1 if dc % 2 == 0 else pT2)[:, 0:128],
                            in_=YGb[:, dc * 128:(dc + 1) * 128], identity=identb[:]))
                    tensor.wait_ge(S["c_ygte"].h, 8 * (it + 1))
                    if it >= 1:
                        tensor.wait_ge(S["c_att"].h, it)
                    for hf in range(2):
                        for dc in range(8):
                            mm = tensor.matmul(
                                out=(pC if hf == 0 else pD)[:],
                                lhsT=ygfm[:, dc * 128:(dc + 1) * 128],
                                rhs=WS[0][:, dc * 1024 + hf * 512:
                                          dc * 1024 + (hf + 1) * 512],
                                start=(dc == 0), stop=(dc == 7))
                        S["c_wo"].bump(mm)
            for bi in range(len(PBUILD2)):
                for hf in range(2):
                    n_ = 14 + 2 * bi + hf
                    tensor.wait_ge(S["c_pr"].h, 16 * (n_ + 1))
                    tensor.wait_ge(S["c_pbe"].h, n_)
                    S["c_pb"].bump(tensor.matmul(
                        out=pP[:], lhsT=ones1_128f[:], rhs=prow[:],
                        start=True, stop=True))
            for i in range(32):
                tensor.wait_ge(S["c_h1c"].h, i + 1)
                for dc in range(8):
                    n_ = 8 * i + dc
                    if n_ >= 2:
                        tensor.wait_ge(S["c_h1te"].h, n_ - 1)
                    S["c_h1t"].bump(tensor.transpose(
                        out=(pT1 if dc % 2 == 0 else pT2)[:, 0:128],
                        in_=H1b[:, dc * 128:(dc + 1) * 128], identity=identb[:]))
                tensor.wait_ge(S["c_h1te"].h, 8 * (i + 1))
                if i >= 1:
                    tensor.wait_ge(S["c_wfe"].h, 2 * i)
                for hf in range(2):
                    for dc in range(8):
                        mm = tensor.matmul(
                            out=(pC if hf == 0 else pD)[:],
                            lhsT=h1fm[:, dc * 128:(dc + 1) * 128],
                            rhs=WS[1][:, dc * 1024 + hf * 512:
                                      dc * 1024 + (hf + 1) * 512],
                            start=(dc == 0), stop=(dc == 7))
                    S["c_wf"].bump(mm)

        @block.scalar
        def _(scalar):
            def ln_scalar(k):
                scalar.wait_ge(S["c_c"].h, k + 1)
                scalar.activation(out=XCH[:, 0:1024], in_=scr[:], func=AF.Square,
                                  accum_out=stat[:, 2:3])
                scalar.drain()
                scalar.activation(out=stat[:, 3:4], in_=stat[:, 2:3], func=AF.Ln,
                                  scale=1.0 / 1024, bias=cb_eps[:])
                scalar.drain()
                S["c_r"].bump(scalar.activation(out=stat[:, 3:4], in_=stat[:, 3:4],
                                                func=AF.Exp, scale=-0.5))
            for i in range(32):
                ln_scalar(i)
            nwz = 0
            for ci in range(6):
                if ci not in (1, 4, 5):
                    continue
                for tt_ in range(4):
                    for nt in range(8):
                        for hf in range(2):
                            n_ = ((ci * 4 + tt_) * 8 + nt) * 2 + hf
                            if ci == 1:
                                nwz += 1
                                scalar.wait_ge(S["c_wz"].h, nwz)
                                if n_ >= 2:
                                    scalar.wait_ge(S["c_pjs"].h, 16 * (n_ - 1))
                                scalar.activation(out=WZ[:], in_=WZ[:], func=AF.Sigmoid)
                                scalar.drain()
                                S["c_pje"].bump(scalar.activation(
                                    out=PJT[n_ % 2][:, hf * 512:(hf + 1) * 512],
                                    in_=WZ[:], func=AF.Exp, scale=-0.60653065971))
                            elif ci == 4:
                                nwz += 1
                                scalar.wait_ge(S["c_wz"].h, nwz)
                                if n_ >= 2:
                                    scalar.wait_ge(S["c_pjs"].h, 16 * (n_ - 1))
                                S["c_pje"].bump(scalar.activation(
                                    out=PJT[n_ % 2][:, hf * 512:(hf + 1) * 512],
                                    in_=WZ[:], func=AF.Sigmoid))
                            else:
                                scalar.wait_ge(S["c_pj"].h,
                                               ci * 64 + tt_ * 16 + nt * 2 + hf + 1)
                                if n_ >= 2:
                                    scalar.wait_ge(S["c_pjs"].h, 16 * (n_ - 1))
                                S["c_pje"].bump(scalar.activation(
                                    out=PJT[n_ % 2][:, hf * 512:(hf + 1) * 512],
                                    in_=(pA if n_ % 2 == 0 else pB)[:], func=AF.Sigmoid))
            for it in range(32):
                scalar.wait_ge(S["c_kst"].h, it + 1)
                S["c_ksr"].bump(scalar.activation(out=ST[:, 16:32], in_=ST[:, 0:16],
                                                  func=AF.Sqrt))
                scalar.wait_ge(S["c_gn1"].h, it + 1)
                scalar.activation(out=ST[:, 32:48], in_=ST[:, 16:32], func=AF.Ln,
                                  scale=1.0 / 64, bias=cb_gn[:])
                scalar.drain()
                S["c_gn2"].bump(scalar.activation(out=ST[:, 32:48], in_=ST[:, 32:48],
                                                  func=AF.Exp, scale=-0.5))
            for i in range(32):
                ln_scalar(32 + 3 * i)
                ln_scalar(33 + 3 * i)
                scalar.wait_ge(S["c_wz"].h, 128 + i + 1)
                scalar.activation(out=scr2[:, 0:512], in_=scr[:, 0:512], func=AF.Silu)
                S["c_wr"].bump(scalar.activation(out=scr2[:, 512:1024],
                                                 in_=scr[:, 512:1024], func=AF.Silu))
                ln_scalar(34 + 3 * i)

        @block.vector
        def _(vector):
            vector.wait_ge(S["c_pre"].h, 1)
            vector.tensor_scalar(out=identb[:], in0=idm[:], scalar1=0.0, scalar2=None,
                                 op0=ALU.is_equal)
            vector.memset(ones1_128f[:], 1.0)
            vector.memset(cb_eps[:], LNEPS)
            vector.memset(cb_gn[:], 0.00064)
            vector.wait_ge(S["c_mu"].h, 16)
            vector.tensor_scalar(out=om6[:], in0=mu6[:], scalar1=-1.0, scalar2=1.0,
                                 op0=ALU.mult, op1=ALU.add)
            S["c_pre"].bump(vector.engine_nop())
            for bi, (slot, pn) in enumerate(PBUILD1):
                for hf in range(2):
                    n_ = 2 * bi + hf
                    vector.wait_ge(S["c_pb"].h, n_ + 1)
                    S["c_pbe"].bump(vector.tensor_copy(
                        out=PB[slot][:, hf * 512:(hf + 1) * 512], in_=pP[:]))

            def ln_event(xin_ap, out_ap, scale_w=None, bias_w=None):
                vector.tensor_reduce(out=stat[:, 0:1], in_=xin_ap, axis=AX.X, op=ALU.add)
                vector.drain()
                vector.tensor_scalar(out=stat[:, 1:2], in0=stat[:, 0:1],
                                     scalar1=1.0 / 1024, scalar2=None, op0=ALU.mult)
                vector.drain()
                v = S["c_c"].bump(vector.tensor_scalar(
                    out=scr[:], in0=xin_ap, scalar1=stat[:, 1:2], scalar2=None,
                    op0=ALU.subtract))
                vector.wait_ge(S["c_r"].h, v)
                vector.tensor_scalar(out=out_ap, in0=scr[:], scalar1=stat[:, 3:4],
                                     scalar2=None, op0=ALU.mult)
                if scale_w is not None:
                    vector.tensor_tensor(out=out_ap, in0=out_ap, in1=scale_w, op=ALU.mult)
                    vector.tensor_tensor(out=out_ap, in0=out_ap, in1=bias_w, op=ALU.add)

            for i in range(32):
                p = i % 2
                vector.wait_ge(S["c_ld"].h, 16 * (i + 1))
                ln_event(XT2[p][:], UT[p][:])
                S["c_ucast"].bump(vector.tensor_copy(out=UB[p][:], in_=UT[p][:]))
                tti, nt = i // 8, i % 8
                for dc in range(8):
                    vector.wait_ge(S["c_ut"].h, 8 * i + dc + 1)
                    S["c_ute"].bump(vector.tensor_copy(
                        out=ufm4[tti][:, dc * 1024 + nt * 128:
                                       dc * 1024 + (nt + 1) * 128],
                        in_=(pT1 if dc % 2 == 0 else pT2)[:, 0:128]))
            for ci in range(6):
                for tt_ in range(4):
                    if ci * 4 + tt_ >= 1:
                        vector.wait_ge(S["c_pj"].h, (ci * 4 + tt_) * 16)
                    for dc in range(8):
                        mcol = ci * 8 + dc
                        if tt_ == 0:
                            vector.tensor_scalar(
                                out=XCH[:, dc * 1024:(dc + 1) * 1024],
                                in0=ufm4[0][:, dc * 1024:(dc + 1) * 1024],
                                scalar1=om6[:, mcol:mcol + 1], scalar2=None,
                                op0=ALU.mult)
                        else:
                            vector.tensor_scalar(
                                out=scr[:].bitcast(BF16)[:, 0:1024],
                                in0=ufm4[tt_ - 1][:, dc * 1024:(dc + 1) * 1024],
                                scalar1=mu6[:, mcol:mcol + 1], scalar2=None,
                                op0=ALU.mult)
                            vector.scalar_tensor_tensor(
                                out=XCH[:, dc * 1024:(dc + 1) * 1024],
                                in0=ufm4[tt_][:, dc * 1024:(dc + 1) * 1024],
                                scalar=om6[:, mcol:mcol + 1],
                                in1=scr[:].bitcast(BF16)[:, 0:1024],
                                op0=ALU.mult, op1=ALU.add)
                    S["c_xch"].bump(vector.engine_nop())
                    for nt in range(8):
                        for hf in range(2):
                            n_ = ((ci * 4 + tt_) * 8 + nt) * 2 + hf
                            ps = (pA if n_ % 2 == 0 else pB)
                            pjv = ci * 64 + tt_ * 16 + nt * 2 + hf + 1
                            if ci in (0, 2, 3):
                                vector.wait_ge(S["c_pj"].h, pjv)
                                if n_ >= 2:
                                    vector.wait_ge(S["c_pjs"].h, 16 * (n_ - 1))
                                S["c_pje"].bump(vector.tensor_copy(
                                    out=PJT[n_ % 2][:, hf * 512:(hf + 1) * 512],
                                    in_=ps[:]))
                            elif ci in (1, 4):
                                vector.wait_ge(S["c_pj"].h, pjv)
                                vector.wait_ge(S["c_pje"].h, n_)
                                S["c_wz"].bump(vector.tensor_tensor(
                                    out=WZ[:], in0=ps[:],
                                    in1=PB[0 if ci == 1 else 1][:, hf * 512:(hf + 1) * 512],
                                    op=ALU.add))
            b3 = lambda ap16: ap16.unsqueeze(2).broadcast_to([128, 16, 64])
            r3 = lambda ap: ap.rearrange("p (h w) -> p h w", h=16)
            nsld = 0
            for i in range(8):
                for tt_ in range(4):
                    it = i * 4 + tt_
                    pp = tt_ % 2
                    nsld += 6
                    vector.wait_ge(S["c_sld"].h, 16 * nsld)
                    vector.tensor_tensor(out=scr[:], in0=Kb[pp][:], in1=PB[2][:],
                                         op=ALU.mult)
                    vector.tensor_tensor(out=scr2[:], in0=scr[:], in1=scr[:], op=ALU.mult)
                    S["c_kst"].bump(vector.tensor_reduce(
                        out=ST[:, 0:16], in_=r3(scr2[:]), axis=AX.X, op=ALU.add))
                    vector.wait_ge(S["c_ksr"].h, it + 1)
                    vector.tensor_scalar(out=ST[:, 16:32], in0=ST[:, 16:32],
                                         scalar1=1e-12, scalar2=None, op0=ALU.max)
                    vector.drain()
                    vector.reciprocal(out=ST[:, 48:64], in_=ST[:, 16:32])
                    vector.drain()
                    vector.tensor_tensor(out=r3(KKN[:]), in0=r3(scr[:]),
                                         in1=b3(ST[:, 48:64]), op=ALU.mult)
                    vector.tensor_scalar(out=scr[:], in0=Ab[pp][:], scalar1=1.0,
                                         scalar2=None, op0=ALU.subtract)
                    vector.tensor_tensor(out=scr[:], in0=scr[:], in1=PB[3][:], op=ALU.mult)
                    vector.tensor_tensor(out=scr[:], in0=Kb[pp][:], in1=scr[:], op=ALU.mult)
                    vector.tensor_tensor(out=KP[:], in0=Kb[pp][:], in1=scr[:], op=ALU.add)
                    vector.tensor_tensor(out=KKA[:], in0=KKN[:], in1=Ab[pp][:], op=ALU.mult)
                    for tau in range(tt_):
                        vector.tensor_tensor(out=scr[:], in0=KKN[:], in1=P3[tau][:],
                                             op=ALU.mult)
                        vector.tensor_reduce(out=ST[:, 0:16], in_=r3(scr[:]), axis=AX.X,
                                             op=ALU.add)
                        vector.drain()
                        vector.tensor_tensor(out=P3[tau][:], in0=P3[tau][:],
                                             in1=Wb[pp][:], op=ALU.mult)
                        vector.tensor_tensor(out=r3(scr[:]), in0=r3(KKA[:]),
                                             in1=b3(ST[:, 0:16]), op=ALU.mult)
                        vector.tensor_tensor(out=P3[tau][:], in0=P3[tau][:], in1=scr[:],
                                             op=ALU.subtract)
                    for tau in range(tt_):
                        vector.tensor_tensor(out=scr[:], in0=P3[tau][:], in1=Rb[pp][:],
                                             op=ALU.mult)
                        vector.tensor_reduce(out=CT[:, tau * 16:(tau + 1) * 16],
                                             in_=r3(scr[:]), axis=AX.X, op=ALU.add)
                    vector.tensor_tensor(out=scr[:], in0=KP[:], in1=Rb[pp][:], op=ALU.mult)
                    vector.tensor_reduce(out=CT[:, tt_ * 16:(tt_ + 1) * 16],
                                         in_=r3(scr[:]), axis=AX.X, op=ALU.add)
                    vector.drain()
                    if tt_ < 3:
                        vector.tensor_copy(out=P3[tt_][:], in_=KP[:])
                    vector.tensor_tensor(out=r3(Y[:]),
                                         in0=b3(CT[:, tt_ * 16:(tt_ + 1) * 16]),
                                         in1=r3(VT[tt_][:]), op=ALU.mult)
                    for tau in range(tt_):
                        vector.tensor_tensor(out=r3(scr[:]),
                                             in0=b3(CT[:, tau * 16:(tau + 1) * 16]),
                                             in1=r3(VT[tau][:]), op=ALU.mult)
                        vector.tensor_tensor(out=Y[:], in0=Y[:], in1=scr[:], op=ALU.add)
                    vector.tensor_reduce(out=ST[:, 0:16], in_=r3(Y[:]), axis=AX.X,
                                         op=ALU.add)
                    vector.drain()
                    vector.tensor_scalar(out=ST[:, 0:16], in0=ST[:, 0:16],
                                         scalar1=1.0 / 64, scalar2=None, op0=ALU.mult)
                    vector.drain()
                    vector.tensor_tensor(out=r3(scr[:]), in0=r3(Y[:]),
                                         in1=b3(ST[:, 0:16]), op=ALU.subtract)
                    vector.tensor_tensor(out=scr2[:], in0=scr[:], in1=scr[:], op=ALU.mult)
                    S["c_gn1"].bump(vector.tensor_reduce(
                        out=ST[:, 16:32], in_=r3(scr2[:]), axis=AX.X, op=ALU.add))
                    vector.wait_ge(S["c_gn2"].h, it + 1)
                    vector.tensor_tensor(out=r3(scr2[:]), in0=b3(ST[:, 32:48]),
                                         in1=r3(PB[5][:]), op=ALU.mult)
                    vector.drain()
                    vector.tensor_tensor(out=Y[:], in0=scr[:], in1=scr2[:], op=ALU.mult)
                    vector.tensor_tensor(out=Y[:], in0=Y[:], in1=PB[6][:], op=ALU.add)
                    vector.tensor_tensor(out=scr[:], in0=Rb[pp][:], in1=KP[:], op=ALU.mult)
                    vector.tensor_tensor(out=scr[:], in0=scr[:], in1=PB[4][:], op=ALU.mult)
                    vector.tensor_reduce(out=ST[:, 0:16], in_=r3(scr[:]), axis=AX.X,
                                         op=ALU.add)
                    vector.drain()
                    vector.tensor_tensor(out=r3(scr[:]), in0=b3(ST[:, 0:16]),
                                         in1=r3(VT[tt_][:]), op=ALU.mult)
                    vector.tensor_tensor(out=Y[:], in0=Y[:], in1=scr[:], op=ALU.add)
                    S["c_yg"].bump(vector.tensor_tensor(out=YGb[:], in0=Y[:],
                                                        in1=Gb[pp][:], op=ALU.mult))
                    for dc in range(8):
                        vector.wait_ge(S["c_ygt"].h, 8 * it + dc + 1)
                        S["c_ygte"].bump(vector.tensor_copy(
                            out=ygfm[:, dc * 128:(dc + 1) * 128],
                            in_=(pT1 if dc % 2 == 0 else pT2)[:, 0:128]))
                    vector.wait_ge(S["c_wo"].h, 2 * (it + 1))
                    if it >= 1:
                        vector.wait_ge(S["c_atts"].h, 16 * it)
                    vector.tensor_copy(out=ATTb[:, 0:512], in_=pC[:])
                    S["c_att"].bump(vector.tensor_copy(out=ATTb[:, 512:1024], in_=pD[:]))
            for bi, (slot, pn) in enumerate(PBUILD2):
                for hf in range(2):
                    n_ = 14 + 2 * bi + hf
                    vector.wait_ge(S["c_pb"].h, n_ + 1)
                    S["c_pbe"].bump(vector.tensor_copy(
                        out=PB[slot][:, hf * 512:(hf + 1) * 512], in_=pP[:]))
            for i in range(32):
                p = i % 2
                vector.wait_ge(S["c_x2l4"].h, 16 * (i + 1))
                ln_event(XT2[p][:], UT[0][:])
                vector.wait_ge(S["c_attl"].h, 16 * (i + 1))
                vector.tensor_tensor(out=scr2[:], in0=UT[0][:], in1=ATTl[p][:],
                                     op=ALU.add)
                ln_event(scr2[:], UT[1][:], scale_w=PB[1][:], bias_w=PB[2][:])
                S["c_h1c"].bump(vector.tensor_copy(out=H1b[:], in_=UT[1][:]))
                for dc in range(8):
                    vector.wait_ge(S["c_h1t"].h, 8 * i + dc + 1)
                    S["c_h1te"].bump(vector.tensor_copy(
                        out=h1fm[:, dc * 128:(dc + 1) * 128],
                        in_=(pT1 if dc % 2 == 0 else pT2)[:, 0:128]))
                vector.wait_ge(S["c_wf"].h, 2 * (i + 1))
                vector.tensor_tensor(out=scr[:, 0:512], in0=pC[:], in1=PB[0][:, 0:512],
                                     op=ALU.add)
                S["c_wz"].bump(vector.tensor_tensor(
                    out=scr[:, 512:1024], in0=pD[:], in1=PB[0][:, 512:1024], op=ALU.add))
                S["c_wfe"].bump(vector.engine_nop(), 2)
                vector.wait_ge(S["c_wr"].h, i + 1)
                ln_event(scr2[:], UT[1][:], scale_w=PB[3][:], bias_w=PB[4][:])
                if i >= 2:
                    vector.wait_ge(S["c_x3s"].h, 16 * (i - 1))
                S["c_x3c"].bump(vector.tensor_tensor(
                    out=X3T[p][:], in0=XT2[p][:], in1=UT[1][:], op=ALU.add))

    return nc


def run_stage_c(inputs, x2_full):
    x2 = np.ascontiguousarray(x2_full.reshape(NB, NSEQ, D))
    wmap = {"wr": "L1_attn_Wr", "ww": "L1_attn_Ww", "wk": "L1_attn_Wk",
            "wv": "L1_attn_Wv", "wa": "L1_attn_Wa", "wg": "L1_attn_Wg",
            "wo": "L1_attn_Wo", "wf": "L1_attn_Wf"}
    vmap = {"mu_r": "L1_attn_mu_r", "mu_w": "L1_attn_mu_w", "mu_k": "L1_attn_mu_k",
            "mu_v": "L1_attn_mu_v", "mu_a": "L1_attn_mu_a", "mu_g": "L1_attn_mu_g",
            "w0": "L1_attn_w0", "a0": "L1_attn_a0", "k_k": "L1_attn_k_k",
            "k_a": "L1_attn_k_a", "r_k": "L1_attn_r_k", "lnx_w": "L1_attn_lnx_w",
            "lnx_b": "L1_attn_lnx_b", "bf": "L1_attn_bf", "norm_w": "L1_attn_norm_w",
            "norm_b": "L1_attn_norm_b", "lnf_w": "L1_attn_lnf_w",
            "lnf_b": "L1_attn_lnf_b"}
    common = {}
    mu = np.stack([inputs[f"L1_attn_mu_{m}"] for m in
                   ["r", "w", "k", "v", "a", "g"]])
    common["muarr"] = np.ascontiguousarray(
        mu.reshape(6, 8, 128).transpose(2, 0, 1).reshape(128, 48))
    for k, src in wmap.items():
        common[k] = np.ascontiguousarray(inputs[src])
    for k, src in vmap.items():
        common[k] = np.ascontiguousarray(inputs[src])
    in_maps = []
    for j in range(NCORE):
        m = dict(common)
        m["x2"] = np.ascontiguousarray(
            x2[:, j * NLOC:(j + 1) * NLOC, :]).reshape(ROWS, D)
        in_maps.append(m)
    nc = build_stage_c()
    res = run_bass_kernel_spmd(nc, in_maps, core_ids=list(range(NCORE)))
    x3 = np.zeros((NB, NSEQ, D), np.float32)
    for j in range(NCORE):
        x3[:, j * NLOC:(j + 1) * NLOC, :] = res.results[j]["x3"].reshape(NB, NLOC, D)
    return x3


RSR = 8 * 513

def build_stage_d():
    nc = bass.Bass("TRN2", target_bir_lowering=False)
    x3_h = nc.dram_tensor("x3", [ROWS, D], F32, kind="ExternalInput").ap()
    semb_h = nc.dram_tensor("semb", [1024, D], F32, kind="ExternalInput").ap()
    wg_h = nc.dram_tensor("wgd", [2 * D, 2048], F32, kind="ExternalInput").ap()
    wu_h = nc.dram_tensor("wud", [2 * D, 2048], F32, kind="ExternalInput").ap()
    wd_h = nc.dram_tensor("wdd", [2 * 2048, D], F32, kind="ExternalInput").ap()
    pb_h = nc.dram_tensor("pball", [128, 4096], F32, kind="ExternalInput").ap()
    bgb_h = nc.dram_tensor("bgball", [128, 4096], F32, kind="ExternalInput").ap()
    bub_h = nc.dram_tensor("buball", [128, 4096], F32, kind="ExternalInput").ap()
    bdb_h = nc.dram_tensor("bdball", [128, 2048], F32, kind="ExternalInput").ap()
    scal_h = nc.dram_tensor("sclb", [128, 1], F32, kind="ExternalInput").ap()
    x4_h = nc.dram_tensor("x4", [ROWS, D], F32, kind="ExternalOutput").ap()
    xnsc = nc.dram_tensor("xnsc", [ROWS, D], BF16).ap()
    lgsc = nc.dram_tensor("lgsc", [ROWS, D], BF16).ap()
    mx_in = nc.dram_tensor("mx_in", [32, 128], F32).ap()
    mx_out = nc.dram_tensor("mx_out", [32, 128], F32).ap()
    rs_in = nc.dram_tensor("rs_in", [RSR, D], BF16).ap()
    rs_out = nc.dram_tensor("rs_out", [513, D], BF16).ap()
    eosc = nc.dram_tensor("eosc", [512, D], BF16).ap()
    eog = nc.dram_tensor("eog", [4096, D], BF16, addr_space="Shared").ap()
    dsc = nc.dram_tensor("dsc", [32, 128], F32).ap()

    es = ExitStack()
    t = lambda name, shape, dt: es.enter_context(nc.sbuf_tensor(name, shape, dt))
    WGU = t("WGUd", [128, 8 * 2048], BF16)
    WUU = t("WUUd", [128, 8 * 2048], BF16)
    WDD = t("WDDd", [128, 16 * 1024], BF16)
    sefm = t("sefmd", [128, 8192], BF16)
    XF = t("XFd", [128, 8192], BF16)
    XNB8 = t("XNB8d", [128, 8192], BF16)
    PB4 = t("PB4d", [128, 4096], F32)
    BGB = t("BGBd", [128, 2048], BF16)
    BUB = t("BUBd", [128, 2048], BF16)
    BDB = t("BDBd", [128, 1024], BF16)
    scl = t("scld", [128, 1], F32)
    class _WD:
        def __init__(self, ap): self._ap = ap
        def __getitem__(self, key): return self._ap if key == slice(None) else self._ap[key]
    XT = [t(f"XTd{i}", [128, 1024], F32) for i in range(2)]
    SEt = XT
    XNc = [t(f"XNcd{i}", [128, 1024], BF16) for i in range(2)]
    LGR = [t(f"LGRd{i}", [128, 1024], BF16) for i in range(2)]
    scr = t("scrD", [128, 1024], F32)
    stat = t("statD", [128, 4], F32)
    MP = t("MPd", [128, 32], F32)
    MRrow = t("MRrowd", [1, 128], F32)
    PSL = t("PSLd", [128, 1024], BF16)
    DENT = t("DENTd", [128, 32], F32)
    DET = t("DETd", [32, 128], F32)
    DR4 = [t(f"DR4d{i}", [1, 128], F32) for i in range(4)]
    onescol = t("onescold", [128, 1], BF16)
    SLB = t("SLBd", [128, 1024], BF16)
    slfm = t("slfmd", [128, 4096], BF16)
    DRC = t("DRCd", [128, 4], F32)


    rsum = t("rsumd", [128, 2], F32)
    NMb = [t(f"NMbd{i}", [128, 1], F32) for i in range(2)]
    RSb = [t(f"RSbd{i}", [128, 1], F32) for i in range(2)]
    X4T = [_WD(PB4[:][:, 0:1024]), _WD(PB4[:][:, 1024:2048])]
    MB = _WD(sefm[:][:, 0:2048].bitcast(F32))
    GT = _WD(sefm[:][0:64, 2048:4096])
    GUB = _WD(sefm[:][0:64, 4096:6144])
    gufm = _WD(sefm[:][:, 6144:7168])
    EOB = _WD(sefm[:][:, 7168:8192])
    P2T = [_WD(slfm[:][:, 0:1024])]
    idm = t("idmD", [128, 128], F32)
    identb = t("identbD", [128, 128], BF16)
    identf = t("identfD", [128, 128], F32)
    ones1_128f = t("o1_128fD", [1, 128], F32)
    cb_eps = t("cb_epsD", [128, 1], F32)

    pA = es.enter_context(nc.psum_tensor("pAd", [128, 512], F32))
    pB = es.enter_context(nc.psum_tensor("pBd", [128, 512], F32))
    pC = es.enter_context(nc.psum_tensor("pCd", [128, 512], F32))
    pD = es.enter_context(nc.psum_tensor("pDd", [128, 512], F32))
    pT1 = es.enter_context(nc.psum_tensor("pT1d", [128, 512], BF16))
    pT2 = es.enter_context(nc.psum_tensor("pT2d", [128, 512], BF16))
    pE = es.enter_context(nc.psum_tensor("pEd", [128, 512], F32))

    S = {}
    for name in ["d_io", "d_pl", "d_vr", "d_w", "d_we", "d_sel", "d_c", "d_r", "d_sec",
                 "d_set", "d_sete", "d_x3l", "d_xnc", "d_xnt", "d_xnte",
                 "d_xns", "d_lg", "d_lge", "d_lgs", "d_lgt", "d_mx",
                 "d_mpt", "d_mpte", "d_mxs", "d_ar", "d_mrl", "d_mb", "d_mbe",
                 "d_lgl", "d_pex", "d_psub", "d_sxl", "d_sl", "d_sle",
                 "d_den", "d_dene", "d_dt", "d_dte", "d_pss", "d_dens",
                 "d_rs", "d_rsl", "d_drl", "d_drt", "d_drte", "d_slc", "d_slt",
                 "d_slte", "d_w2", "d_g", "d_ge", "d_gu", "d_gut", "d_gute",
                 "d_eo", "d_eoe", "d_eos", "d_ag", "d_lgl2", "d_cm", "d_cme",
                 "d_p2", "d_p2t", "d_p2te", "d_eofl", "d_nm", "d_x3l4", "d_x4c", "d_x4s"]:
        S[name] = Sem(es, nc, name)

    with es:
        block = es.enter_context(nc.Block())

        @block.sync
        def _(sync):
            S["d_pl"].bump(sync.dma_start(out=PB4[:], in_=pb_h[:]), 16)
            S["d_pl"].bump(sync.dma_start(out=scl[:], in_=scal_h[:]), 16)
            for st in range(8):
                p = st % 2
                if st >= 2:
                    sync.wait_ge(S["d_sec"].h, st - 1)
                S["d_sel"].bump(sync.dma_start(
                    out=SEt[p][:], in_=semb_h[128 * st:128 * (st + 1), :]), 16)
            sync.wait_ge(S["d_sec"].h, 8)
            for b in range(4):
                for nt in range(8):
                    i = b * 8 + nt
                    p = i % 2
                    if i >= 2:
                        sync.wait_ge(S["d_xnc"].h, i - 1)
                    S["d_x3l"].bump(sync.dma_start(
                        out=XT[p][:], in_=x3_h[128 * i:128 * (i + 1), :]), 16)
                    sync.wait_ge(S["d_xnc"].h, i + 1)
                    S["d_xns"].bump(sync.dma_start(
                        out=xnsc[128 * i:128 * (i + 1), :], in_=XNc[p][:]), 16)
                for nt in range(8):
                    i = b * 8 + nt
                    sync.wait_ge(S["d_lge"].h, 2 * (i + 1))
                    S["d_lgs"].bump(sync.dma_start(
                        out=lgsc[128 * i:128 * (i + 1), :], in_=LGR[i % 2][:]), 16)
            sync.wait_ge(S["d_mpte"].h, 1)
            S["d_mxs"].bump(sync.dma_start(out=mx_in[:], in_=DET[:]), 16)
            sync.wait_ge(S["d_lgs"].h, 16 * 32)
            sync.wait_ge(S["d_xns"].h, 16 * 32)
            sync.wait_ge(S["d_ar"].h, 1)
            for b in range(4):
                for er in range(8):
                    if b + er >= 1:
                        sync.wait_ge(S["d_mb"].h, b * 8 + er)
                    S["d_mrl"].bump(sync.dma_start(
                        out=MRrow[:], in_=mx_out[b * 8 + er: b * 8 + er + 1, :]), 16)
                for nt in range(8):
                    i = b * 8 + nt
                    p = nt % 2
                    if i >= 2:
                        sync.wait_ge(S["d_pex"].h, i - 1)
                    S["d_lgl"].bump(sync.dma_start(
                        out=LGR[p][:], in_=lgsc[128 * i:128 * (i + 1), :]), 16)
                    if nt == 0 and b >= 1:
                        sync.wait_ge(S["d_sl"].h, 8 * b)
                    S["d_sxl"].bump(sync.dma_start(
                        out=XNB8[:, nt * 1024:(nt + 1) * 1024],
                        in_=xnsc[128 * i:128 * (i + 1), :]), 16)
                for er in range(8):
                    sync.wait_ge(S["d_sle"].h, (b * 8 + er + 1) * 2)
                    S["d_pss"].bump(sync.dma_start(
                        out=rs_in[513 * er + b * 128: 513 * er + (b + 1) * 128, :],
                        in_=PSL[:]), 16)
            sync.wait_ge(S["d_rs"].h, 1)
            for b in range(4):
                S["d_rsl"].bump(sync.dma_start(
                    out=SLB[:], in_=rs_out[b * 128:(b + 1) * 128, :]), 16)
                sync.wait_ge(S["d_slt"].h, 8 * (b + 1))
            for e in range(2):
                for b in range(4):
                    eb = e * 4 + b
                    sync.wait_ge(S["d_eoe"].h, 2 * (eb + 1))
                    S["d_eos"].bump(sync.dma_start(
                        out=eosc[b * 128 + e * 64: b * 128 + e * 64 + 64, :],
                        in_=EOB[e * 64:(e + 1) * 64, :]), 16)
            sync.wait_ge(S["d_ag"].h, 1)
            for b in range(4):
                for er in range(8):
                    S["d_eofl"].bump(sync.dma_start(
                        out=XNB8[:, er * 1024:(er + 1) * 1024],
                        in_=eog[er * 512 + b * 128: er * 512 + (b + 1) * 128, :]), 16)
                for nt in range(8):
                    i = b * 8 + nt
                    p = nt % 2
                    if nt >= 2:
                        sync.wait_ge(S["d_x4c"].h, i - 1)
                    S["d_lgl2"].bump(sync.dma_start(
                        out=LGR[p][:], in_=lgsc[128 * i:128 * (i + 1), :]), 16)
                    S["d_x3l4"].bump(sync.dma_start(
                        out=XT[p][:], in_=x3_h[128 * i:128 * (i + 1), :]), 16)
                    sync.wait_ge(S["d_x4c"].h, i + 1)
                    S["d_x4s"].bump(sync.dma_start(
                        out=x4_h[128 * i:128 * (i + 1), :], in_=X4T[p][:]), 16)
            sync.wait_ge(S["d_x4s"].h, 16 * 32)

        @block.gpsimd
        def _(gpsimd):
            gpsimd.iota(out=idm[:], pattern=[[1, 128]], base=0, channel_multiplier=-1,
                        allow_small_or_imprecise_dtypes=True)
            S["d_io"].bump(gpsimd.engine_nop())
            S["d_w"].bump(gpsimd.dma_start(out=BGB[:], in_=bgb_h[:, 0:2048]), 16)
            S["d_w"].bump(gpsimd.dma_start(out=BUB[:], in_=bub_h[:, 0:2048]), 16)
            S["d_w"].bump(gpsimd.dma_start(out=BDB[:], in_=bdb_h[:, 0:1024]), 16)
            for wh, wsb in [(wg_h, WGU), (wu_h, WUU)]:
                S["d_w"].bump(gpsimd.dma_start(
                    out=wsb[:].rearrange("p (dc f) -> p dc f", dc=8),
                    in_=wh[0:1024, :].rearrange("(dc p) f -> p dc f", p=128)), 16)
            S["d_w"].bump(gpsimd.dma_start(
                out=WDD[:].rearrange("p (fc d) -> p fc d", fc=16),
                in_=wd_h[0:2048, :].rearrange("(fc p) d -> p fc d", p=128)), 16)
            gpsimd.wait_ge(S["d_mxs"].h, 16)
            gpsimd.collective_compute(
                "AllReduce", ALU.max, replica_groups=[list(range(NCORE))],
                ins=[mx_in[:]], outs=[mx_out[:]]).then_inc(S["d_ar"].h, 1)
            gpsimd.wait_ge(S["d_dte"].h, 1)
            S["d_dens"].bump(gpsimd.dma_start(out=dsc[:], in_=DET[:]), 16)
            gpsimd.wait_ge(S["d_dens"].h, 16)
            for er in range(8):
                for b in range(4):
                    S["d_dens"].bump(gpsimd.dma_start(
                        out=rs_in[513 * er + 512: 513 * er + 513,
                                  b * 128:(b + 1) * 128],
                        in_=dsc[b * 8 + er: b * 8 + er + 1, :]), 16)
            gpsimd.wait_ge(S["d_pss"].h, 16 * 32)
            gpsimd.wait_ge(S["d_dens"].h, 16 * 33)
            gpsimd.collective_compute(
                "ReduceScatter", ALU.add, replica_groups=[list(range(NCORE))],
                ins=[rs_in[:]], outs=[rs_out[:]]).then_inc(S["d_rs"].h, 1)
            gpsimd.wait_ge(S["d_rs"].h, 1)
            for b in range(4):
                S["d_drl"].bump(gpsimd.dma_start(
                    out=DR4[b][:],
                    in_=rs_out[512:513, b * 128:(b + 1) * 128]), 16)
            gpsimd.wait_ge(S["d_eo"].h, 8)
            S["d_w2"].bump(gpsimd.dma_start(out=BGB[:], in_=bgb_h[:, 2048:4096]), 16)
            S["d_w2"].bump(gpsimd.dma_start(out=BUB[:], in_=bub_h[:, 2048:4096]), 16)
            S["d_w2"].bump(gpsimd.dma_start(out=BDB[:], in_=bdb_h[:, 1024:2048]), 16)
            for wh, wsb in [(wg_h, WGU), (wu_h, WUU)]:
                S["d_w2"].bump(gpsimd.dma_start(
                    out=wsb[:].rearrange("p (dc f) -> p dc f", dc=8),
                    in_=wh[1024:2048, :].rearrange("(dc p) f -> p dc f", p=128)), 16)
            S["d_w2"].bump(gpsimd.dma_start(
                out=WDD[:].rearrange("p (fc d) -> p fc d", fc=16),
                in_=wd_h[2048:4096, :].rearrange("(fc p) d -> p fc d", p=128)), 16)
            gpsimd.wait_ge(S["d_eos"].h, 16 * 8)
            gpsimd.collective_compute(
                "AllGather", ALU.bypass, replica_groups=[list(range(NCORE))],
                ins=[eosc[:]], outs=[eog[:]]).then_inc(S["d_ag"].h, 1)

        @block.tensor
        def _(tensor):
            tensor.wait_ge(S["d_vr"].h, 1)
            for st in range(8):
                p = st % 2
                tensor.wait_ge(S["d_sec"].h, st + 1)
                for dc in range(8):
                    n_ = 8 * st + dc
                    if n_ >= 2:
                        tensor.wait_ge(S["d_sete"].h, n_ - 1)
                    S["d_set"].bump(tensor.transpose(
                        out=(pT1 if dc % 2 == 0 else pT2)[:, 0:128],
                        in_=XNc[0][:, dc * 128:(dc + 1) * 128],
                        identity=identb[:]))
            for b in range(4):
                for nt in range(8):
                    i = b * 8 + nt
                    p = i % 2
                    tensor.wait_ge(S["d_xnc"].h, i + 1)
                    for dc in range(8):
                        n_ = 8 * i + dc
                        if n_ >= 2:
                            tensor.wait_ge(S["d_xnte"].h, n_ - 1)
                        S["d_xnt"].bump(tensor.transpose(
                            out=(pT1 if dc % 2 == 0 else pT2)[:, 0:128],
                            in_=XNc[p][:, dc * 128:(dc + 1) * 128],
                            identity=identb[:]))
                tensor.wait_ge(S["d_xnte"].h, 64 * (b + 1))
                for nt in range(8):
                    i = b * 8 + nt
                    if i >= 1:
                        tensor.wait_ge(S["d_lge"].h, 2 * i)
                    for hf in range(2):
                        for dc in range(8):
                            mm = tensor.matmul(
                                out=(pA if hf == 0 else pB)[:],
                                lhsT=XF[:, dc * 1024 + nt * 128:
                                        dc * 1024 + (nt + 1) * 128],
                                rhs=sefm[:, dc * 1024 + hf * 512:
                                         dc * 1024 + (hf + 1) * 512],
                                start=(dc == 0), stop=(dc == 7))
                    S["d_lg"].bump(mm)
                for er in range(8):
                    ber = b * 8 + er
                    if ber >= 1:
                        tensor.wait_ge(S["d_mx"].h, ber)
                    for hf in range(2):
                        for dc in range(8):
                            mm = tensor.matmul(
                                out=(pC if hf == 0 else pD)[:],
                                lhsT=sefm[:, dc * 1024 + er * 128:
                                          dc * 1024 + (er + 1) * 128],
                                rhs=XF[:, dc * 1024 + hf * 512:
                                       dc * 1024 + (hf + 1) * 512],
                                start=(dc == 0), stop=(dc == 7))
                    S["d_lgt"].bump(mm)
            tensor.wait_ge(S["d_mx"].h, 32)
            S["d_mpt"].bump(tensor.transpose(out=pE[0:32, 0:128],
                                             in_=MP[:], identity=identf[:]))
            tensor.wait_ge(S["d_ar"].h, 1)
            for b in range(4):
                for er in range(8):
                    ber = b * 8 + er
                    tensor.wait_ge(S["d_mrl"].h, 16 * (ber + 1))
                    tensor.wait_ge(S["d_mpte"].h, 1)
                    if ber >= 1:
                        tensor.wait_ge(S["d_mbe"].h, ber)
                    S["d_mb"].bump(tensor.matmul(
                        out=pE[0:128, 0:128], lhsT=ones1_128f[:], rhs=MRrow[:],
                        start=True, stop=True))
                for er in range(8):
                    ber = b * 8 + er
                    tensor.wait_ge(S["d_pex"].h, 8 * (b + 1))
                    tensor.wait_ge(S["d_sxl"].h, 16 * 8 * (b + 1))
                    if ber >= 1:
                        tensor.wait_ge(S["d_sle"].h, 2 * ber)
                        tensor.wait_ge(S["d_den"].h, ber)
                    for nt in range(8):
                        for hf in range(2):
                            tensor.matmul(
                                out=(pA if hf == 0 else pB)[:],
                                lhsT=XF[:, nt * 1024 + er * 128:
                                        nt * 1024 + (er + 1) * 128],
                                rhs=XNB8[:, nt * 1024 + hf * 512:
                                         nt * 1024 + (hf + 1) * 512],
                                start=(nt == 0), stop=(nt == 7))
                        mm = tensor.matmul(
                            out=pE[0:128, 128:129],
                            lhsT=XF[:, nt * 1024 + er * 128:
                                    nt * 1024 + (er + 1) * 128],
                            rhs=onescol[:], start=(nt == 0), stop=(nt == 7))
                    S["d_sl"].bump(mm)
            tensor.wait_ge(S["d_dt"].h, 1)
            S["d_mpt"].bump(tensor.transpose(out=pE[0:32, 0:128], in_=DENT[:],
                                             identity=identf[:]))
            for b in range(4):
                tensor.wait_ge(S["d_drl"].h, 16 * (b + 1))
                if b >= 1:
                    tensor.wait_ge(S["d_drte"].h, b)
                S["d_drt"].bump(tensor.transpose(
                    out=pE[0:128, 128:129], in_=DR4[b][:],
                    identity=identf[0:1, 0:1]))
                tensor.wait_ge(S["d_slc"].h, b + 1)
                for dc in range(8):
                    n_ = 8 * b + dc
                    if n_ >= 2:
                        tensor.wait_ge(S["d_slte"].h, n_ - 1)
                    S["d_slt"].bump(tensor.transpose(
                        out=(pT1 if dc % 2 == 0 else pT2)[:, 0:128],
                        in_=SLB[:, dc * 128:(dc + 1) * 128], identity=identb[:]))
            for e in range(2):
                if e == 0:
                    tensor.wait_ge(S["d_w"].h, 96)
                else:
                    tensor.wait_ge(S["d_w2"].h, 96)
                for b in range(4):
                    eb = e * 4 + b
                    tensor.wait_ge(S["d_slte"].h, 8 * (b + 1))
                    for fc in range(4):
                        n_ = eb * 4 + fc
                        if n_ >= 1:
                            tensor.wait_ge(S["d_ge"].h, 2 * n_)
                            tensor.wait_ge(S["d_gu"].h, n_)
                        for dc in range(8):
                            mmg = tensor.matmul(
                                out=pA[0:64, :],
                                lhsT=slfm[:, b * 1024 + dc * 128 + e * 64:
                                          b * 1024 + dc * 128 + e * 64 + 64],
                                rhs=WGU[:, dc * 2048 + fc * 512:
                                        dc * 2048 + (fc + 1) * 512],
                                start=(dc == 0), stop=(dc == 7))
                        for dc in range(8):
                            mmu = tensor.matmul(
                                out=pB[0:64, :],
                                lhsT=slfm[:, b * 1024 + dc * 128 + e * 64:
                                          b * 1024 + dc * 128 + e * 64 + 64],
                                rhs=WUU[:, dc * 2048 + fc * 512:
                                        dc * 2048 + (fc + 1) * 512],
                                start=(dc == 0), stop=(dc == 7))
                        S["d_g"].bump(mmu)
                    for fc16 in range(16):
                        n_ = eb * 16 + fc16
                        tensor.wait_ge(S["d_gu"].h, 4 * (eb + 1))
                        if n_ >= 2:
                            tensor.wait_ge(S["d_gute"].h, n_ - 1)
                        S["d_gut"].bump(tensor.transpose(
                            out=(pT1 if fc16 % 2 == 0 else pT2)[:, 0:64],
                            in_=GUB[:, fc16 * 128:(fc16 + 1) * 128],
                            identity=identb[0:64, 0:64]))
                    tensor.wait_ge(S["d_gute"].h, 16 * (eb + 1))
                    if eb >= 1:
                        tensor.wait_ge(S["d_eoe"].h, 2 * eb)
                    for hf in range(2):
                        for fc16 in range(16):
                            mm = tensor.matmul(
                                out=(pC if hf == 0 else pD)[0:64, :],
                                lhsT=gufm[:, fc16 * 64:(fc16 + 1) * 64],
                                rhs=WDD[:, fc16 * 1024 + hf * 512:
                                        fc16 * 1024 + (hf + 1) * 512],
                                start=(fc16 == 0), stop=(fc16 == 15))
                        S["d_eo"].bump(mm)
            tensor.wait_ge(S["d_ag"].h, 1)
            for b in range(4):
                tensor.wait_ge(S["d_eofl"].h, 16 * 8 * (b + 1))
                for nt in range(8):
                    i = b * 8 + nt
                    tensor.wait_ge(S["d_p2"].h, i + 1)
                    for dc in range(8):
                        n_ = 8 * i + dc
                        if n_ >= 2:
                            tensor.wait_ge(S["d_p2te"].h, n_ - 1)
                        S["d_p2t"].bump(tensor.transpose(
                            out=(pT1 if dc % 2 == 0 else pT2)[:, 0:128],
                            in_=LGR[nt % 2][:, dc * 128:(dc + 1) * 128],
                            identity=identb[:]))
                    tensor.wait_ge(S["d_p2te"].h, 8 * (i + 1))
                    if i >= 1:
                        tensor.wait_ge(S["d_x4c"].h, i)
                    for hf in range(2):
                        for er in range(8):
                            mm = tensor.matmul(
                                out=(pC if hf == 0 else pD)[:],
                                lhsT=P2T[0][:, er * 128:(er + 1) * 128],
                                rhs=XNB8[:, er * 1024 + hf * 512:
                                         er * 1024 + (hf + 1) * 512],
                                start=(er == 0), stop=(er == 7))
                        S["d_cm"].bump(mm)

        @block.scalar
        def _(scalar):
            def ln_scalar(k):
                scalar.wait_ge(S["d_c"].h, k + 1)
                scalar.activation(out=PSL[:], in_=scr[:], func=AF.Square,
                                  accum_out=stat[:, 2:3])
                scalar.drain()
                scalar.activation(out=stat[:, 3:4], in_=stat[:, 2:3], func=AF.Ln,
                                  scale=1.0 / 1024, bias=cb_eps[:])
                scalar.drain()
                S["d_r"].bump(scalar.activation(out=stat[:, 3:4], in_=stat[:, 3:4],
                                                func=AF.Exp, scale=-0.5))
            for k in range(40):
                ln_scalar(k)
            for b in range(4):
                for nt in range(8):
                    i = b * 8 + nt
                    scalar.wait_ge(S["d_psub"].h, i + 1)
                    if b >= 1 and nt == 0:
                        scalar.wait_ge(S["d_sl"].h, 8 * b)
                    S["d_pex"].bump(scalar.activation(
                        out=XF[:, nt * 1024:(nt + 1) * 1024], in_=scr[:],
                        func=AF.Exp))
            for e in range(2):
                for b in range(4):
                    eb = e * 4 + b
                    for fc in range(4):
                        n_ = eb * 4 + fc
                        scalar.wait_ge(S["d_g"].h, n_ + 1)
                        scalar.wait_ge(S["d_ge"].h, 2 * n_ + 1)
                        S["d_ge"].bump(scalar.activation(
                            out=GT[:, fc * 512:(fc + 1) * 512],
                            in_=GT[:, fc * 512:(fc + 1) * 512], func=AF.Gelu))
            for b in range(4):
                for nt in range(8):
                    i = b * 8 + nt
                    p = nt % 2
                    scalar.wait_ge(S["d_nm"].h, i + 1)
                    if i >= 2:
                        scalar.wait_ge(S["d_p2t"].h, 8 * (i - 1))
                    S["d_p2"].bump(scalar.activation(
                        out=LGR[p][:], in_=LGR[p][:], func=AF.Exp,
                        bias=NMb[p][:], accum_out=RSb[p][:]))

        @block.vector
        def _(vector):
            vector.wait_ge(S["d_io"].h, 1)
            vector.tensor_scalar(out=identb[:], in0=idm[:], scalar1=0.0, scalar2=None,
                                 op0=ALU.is_equal)
            vector.tensor_scalar(out=identf[:], in0=idm[:], scalar1=0.0, scalar2=None,
                                 op0=ALU.is_equal)
            vector.memset(ones1_128f[:], 1.0)
            vector.memset(onescol[:], 1.0)
            vector.memset(cb_eps[:], LNEPS)
            S["d_vr"].bump(vector.engine_nop())

            def ln_event(xin_ap, out_ap, scale_w=None, bias_w=None):
                vector.tensor_reduce(out=stat[:, 0:1], in_=xin_ap, axis=AX.X, op=ALU.add)
                vector.drain()
                vector.tensor_scalar(out=stat[:, 1:2], in0=stat[:, 0:1],
                                     scalar1=1.0 / 1024, scalar2=None, op0=ALU.mult)
                vector.drain()
                v = S["d_c"].bump(vector.tensor_scalar(
                    out=scr[:], in0=xin_ap, scalar1=stat[:, 1:2], scalar2=None,
                    op0=ALU.subtract))
                vector.wait_ge(S["d_r"].h, v)
                vector.tensor_scalar(out=out_ap, in0=scr[:], scalar1=stat[:, 3:4],
                                     scalar2=None, op0=ALU.mult)
                if scale_w is not None:
                    vector.tensor_tensor(out=out_ap, in0=out_ap, in1=scale_w, op=ALU.mult)
                    vector.tensor_tensor(out=out_ap, in0=out_ap, in1=bias_w, op=ALU.add)

            vector.wait_ge(S["d_pl"].h, 32)
            for st in range(8):
                p = st % 2
                vector.wait_ge(S["d_sel"].h, 16 * (st + 1))
                ln_event(SEt[p][:], SEt[p][:], scale_w=PB4[:, 2048:3072],
                         bias_w=PB4[:, 3072:4096])
                vector.drain()
                vector.tensor_scalar(out=SEt[p][:], in0=SEt[p][:], scalar1=scl[:],
                                     scalar2=None, op0=ALU.mult)
                S["d_sec"].bump(vector.tensor_copy(out=XNc[0][:], in_=SEt[p][:]))
                for dc in range(8):
                    vector.wait_ge(S["d_set"].h, 8 * st + dc + 1)
                    S["d_sete"].bump(vector.tensor_copy(
                        out=sefm[:, dc * 1024 + st * 128: dc * 1024 + (st + 1) * 128],
                        in_=(pT1 if dc % 2 == 0 else pT2)[:, 0:128]))
            for b in range(4):
                for nt in range(8):
                    i = b * 8 + nt
                    p = i % 2
                    vector.wait_ge(S["d_x3l"].h, 16 * (i + 1))
                    if i >= 2:
                        vector.wait_ge(S["d_xns"].h, 16 * (i - 1))
                    ln_event(XT[p][:], XNc[p][:], scale_w=PB4[:, 0:1024],
                             bias_w=PB4[:, 1024:2048])
                    S["d_xnc"].bump(vector.engine_nop())
                    for dc in range(8):
                        vector.wait_ge(S["d_xnt"].h, 8 * i + dc + 1)
                        S["d_xnte"].bump(vector.tensor_copy(
                            out=XF[:, dc * 1024 + nt * 128: dc * 1024 + (nt + 1) * 128],
                            in_=(pT1 if dc % 2 == 0 else pT2)[:, 0:128]))
                for nt in range(8):
                    i = b * 8 + nt
                    vector.wait_ge(S["d_lg"].h, i + 1)
                    if i >= 2:
                        vector.wait_ge(S["d_lgs"].h, 16 * (i - 1))
                    vector.tensor_copy(out=LGR[i % 2][:, 0:512], in_=pA[:])
                    S["d_lge"].bump(vector.tensor_copy(
                        out=LGR[i % 2][:, 512:1024], in_=pB[:]))
                    S["d_lge"].bump(vector.engine_nop())
                for er in range(8):
                    ber = b * 8 + er
                    vector.wait_ge(S["d_lgt"].h, ber + 1)
                    vector.tensor_reduce(out=stat[:, 0:1], in_=pC[:], axis=AX.X,
                                         op=ALU.max)
                    vector.tensor_reduce(out=stat[:, 1:2], in_=pD[:], axis=AX.X,
                                         op=ALU.max)
                    vector.drain()
                    S["d_mx"].bump(vector.tensor_tensor(
                        out=MP[:, ber:ber + 1], in0=stat[:, 0:1], in1=stat[:, 1:2],
                        op=ALU.max))
            vector.wait_ge(S["d_mpt"].h, 1)
            S["d_mpte"].bump(vector.tensor_copy(out=DET[:], in_=pE[0:32, 0:128]))
            for b in range(4):
                for er in range(8):
                    ber = b * 8 + er
                    vector.wait_ge(S["d_mb"].h, ber + 1)
                    S["d_mbe"].bump(vector.tensor_copy(
                        out=MB[:, er * 128:(er + 1) * 128], in_=pE[0:128, 0:128]))
                for nt in range(8):
                    i = b * 8 + nt
                    p = nt % 2
                    vector.wait_ge(S["d_lgl"].h, 16 * (i + 1))
                    if i >= 1:
                        vector.wait_ge(S["d_pex"].h, i)
                    S["d_psub"].bump(vector.tensor_tensor(
                        out=scr[:], in0=LGR[p][:], in1=MB[:], op=ALU.subtract))
                for er in range(8):
                    ber = b * 8 + er
                    vector.wait_ge(S["d_sl"].h, ber + 1)
                    if ber >= 1:
                        vector.wait_ge(S["d_pss"].h, 16 * ber)
                    vector.tensor_copy(out=PSL[:, 0:512], in_=pA[:])
                    S["d_sle"].bump(vector.tensor_copy(out=PSL[:, 512:1024], in_=pB[:]),
                                    2)
                    S["d_den"].bump(vector.tensor_copy(
                        out=DENT[:, ber:ber + 1], in_=pE[0:128, 128:129]))
            vector.wait_ge(S["d_den"].h, 32)
            S["d_dt"].bump(vector.engine_nop())
            vector.wait_ge(S["d_mpt"].h, 2)
            vector.wait_ge(S["d_mxs"].h, 16)
            S["d_dte"].bump(vector.tensor_copy(out=DET[:], in_=pE[0:32, 0:128]))
            for b in range(4):
                vector.wait_ge(S["d_rsl"].h, 16 * (b + 1))
                vector.wait_ge(S["d_drt"].h, b + 1)
                vector.tensor_copy(out=DRC[:, b:b + 1], in_=pE[0:128, 128:129])
                vector.drain()
                vector.reciprocal(out=DRC[:, b:b + 1], in_=DRC[:, b:b + 1])
                vector.drain()
                vector.tensor_scalar(out=scr[:], in0=SLB[:], scalar1=DRC[:, b:b + 1],
                                     scalar2=None, op0=ALU.mult)
                S["d_drte"].bump(vector.engine_nop())
                S["d_slc"].bump(vector.tensor_copy(out=SLB[:], in_=scr[:]))
                for dc in range(8):
                    vector.wait_ge(S["d_slt"].h, 8 * b + dc + 1)
                    S["d_slte"].bump(vector.tensor_copy(
                        out=slfm[:, b * 1024 + dc * 128: b * 1024 + (dc + 1) * 128],
                        in_=(pT1 if dc % 2 == 0 else pT2)[:, 0:128]))
            for e in range(2):
                for b in range(4):
                    eb = e * 4 + b
                    for fc in range(4):
                        n_ = eb * 4 + fc
                        vector.wait_ge(S["d_g"].h, n_ + 1)
                        S["d_ge"].bump(vector.tensor_tensor(
                            out=GT[:, fc * 512:(fc + 1) * 512], in0=pA[0:64, :],
                            in1=BGB[0:64, fc * 512:(fc + 1) * 512], op=ALU.add))
                        vector.wait_ge(S["d_ge"].h, 2 * n_ + 2)
                        vector.tensor_tensor(
                            out=scr[0:64, 0:512], in0=pB[0:64, :],
                            in1=BUB[0:64, fc * 512:(fc + 1) * 512], op=ALU.add)
                        vector.drain()
                        S["d_gu"].bump(vector.tensor_tensor(
                            out=GUB[:, fc * 512:(fc + 1) * 512],
                            in0=GT[:, fc * 512:(fc + 1) * 512],
                            in1=scr[0:64, 0:512], op=ALU.mult))
                    for fc16 in range(16):
                        n_ = eb * 16 + fc16
                        vector.wait_ge(S["d_gut"].h, n_ + 1)
                        S["d_gute"].bump(vector.tensor_copy(
                            out=gufm[:, fc16 * 64:(fc16 + 1) * 64],
                            in_=(pT1 if fc16 % 2 == 0 else pT2)[:, 0:64]))
                    vector.wait_ge(S["d_eo"].h, 2 * (eb + 1))
                    if eb >= 2:
                        vector.wait_ge(S["d_eos"].h, 16 * (eb - 1))
                    vector.tensor_tensor(out=EOB[e * 64:(e + 1) * 64, 0:512],
                                         in0=pC[0:64, :],
                                         in1=BDB[0:64, 0:512], op=ALU.add)
                    S["d_eoe"].bump(vector.tensor_tensor(
                        out=EOB[e * 64:(e + 1) * 64, 512:1024], in0=pD[0:64, :],
                        in1=BDB[0:64, 512:1024], op=ALU.add), 2)
            for b in range(4):
                for nt in range(8):
                    i = b * 8 + nt
                    p = nt % 2
                    vector.wait_ge(S["d_lgl2"].h, 16 * (i + 1))
                    if i >= 2:
                        vector.wait_ge(S["d_p2"].h, i - 1)
                    vector.tensor_reduce(out=NMb[p][:], in_=LGR[p][:], axis=AX.X,
                                         op=ALU.max, negate=True)
                    S["d_nm"].bump(vector.engine_nop())
                    for dc in range(8):
                        vector.wait_ge(S["d_p2t"].h, 8 * i + dc + 1)
                        S["d_p2te"].bump(vector.tensor_copy(
                            out=P2T[0][:, dc * 128:(dc + 1) * 128],
                            in_=(pT1 if dc % 2 == 0 else pT2)[:, 0:128]))
                    vector.wait_ge(S["d_cm"].h, 2 * (i + 1))
                    vector.drain()
                    vector.reciprocal(out=rsum[:, 1:2], in_=RSb[p][:])
                    vector.drain()
                    vector.wait_ge(S["d_x3l4"].h, 16 * (i + 1))
                    if i >= 2:
                        vector.wait_ge(S["d_x4s"].h, 16 * (i - 1))
                    vector.scalar_tensor_tensor(
                        out=X4T[p][:, 0:512], in0=pC[:], scalar=rsum[:, 1:2],
                        in1=XT[p][:, 0:512], op0=ALU.mult, op1=ALU.add)
                    S["d_x4c"].bump(vector.scalar_tensor_tensor(
                        out=X4T[p][:, 512:1024], in0=pD[:], scalar=rsum[:, 1:2],
                        in1=XT[p][:, 512:1024], op0=ALU.mult, op1=ALU.add))

    return nc


def run_stage_d(inputs, x3_full):
    x3 = np.ascontiguousarray(x3_full.reshape(NB, NSEQ, D))
    sg = inputs["L1_ffn_slot_g"]; sb = inputs["L1_ffn_slot_b"]
    ng = inputs["L1_ffn_norm_g"]; nb = inputs["L1_ffn_norm_b"]
    pball = np.ascontiguousarray(np.broadcast_to(
        np.concatenate([ng, nb, sg, sb])[None, :], (128, 4096)).astype(np.float32))
    common = {
        "semb": np.ascontiguousarray(inputs["L1_ffn_slot_embeds"].reshape(1024, D)),
        "pball": pball,
        "sclb": np.ascontiguousarray(
            np.broadcast_to(inputs["L1_ffn_scale"].reshape(1, 1), (128, 1)).astype(np.float32)),
    }
    in_maps = []
    for j in range(NCORE):
        m = dict(common)
        m["x3"] = np.ascontiguousarray(x3[:, j * NLOC:(j + 1) * NLOC, :]).reshape(ROWS, D)
        e0, e1 = 2 * j, 2 * j + 1
        m["wgd"] = np.ascontiguousarray(inputs["L1_ffn_Wg"][[e0, e1]].reshape(2 * D, 2048))
        m["wud"] = np.ascontiguousarray(inputs["L1_ffn_Wu"][[e0, e1]].reshape(2 * D, 2048))
        m["wdd"] = np.ascontiguousarray(inputs["L1_ffn_Wd"][[e0, e1]].reshape(2 * 2048, D))
        m["bgball"] = np.ascontiguousarray(np.broadcast_to(
            inputs["L1_ffn_bg"][[e0, e1]].reshape(1, 4096), (128, 4096)).astype(np.float32))
        m["buball"] = np.ascontiguousarray(np.broadcast_to(
            inputs["L1_ffn_bu"][[e0, e1]].reshape(1, 4096), (128, 4096)).astype(np.float32))
        m["bdball"] = np.ascontiguousarray(np.broadcast_to(
            inputs["L1_ffn_bd"][[e0, e1]].reshape(1, 2048), (128, 2048)).astype(np.float32))
        in_maps.append(m)
    nc = build_stage_d()
    res = run_bass_kernel_spmd(nc, in_maps, core_ids=list(range(NCORE)))
    x4 = np.zeros((NB, NSEQ, D), np.float32)
    for j in range(NCORE):
        x4[:, j * NLOC:(j + 1) * NLOC, :] = res.results[j]["x4"].reshape(NB, NLOC, D)
    return x4


def kernel(**inputs):
    x1 = run_stage_a(inputs)
    x2 = run_stage_b(inputs, x1)
    x3 = run_stage_c(inputs, x2)
    x4 = run_stage_d(inputs, x3)
    return x4.reshape(NB, NSEQ, D)


# version 03 (final)
# speedup vs baseline: 1.3635x; 1.3635x over previous
import numpy as np
import ml_dtypes
from contextlib import ExitStack
import concourse.bass as bass
import concourse.mybir as mybir
from concourse.bass_utils import run_bass_kernel_spmd

F32 = mybir.dt.float32
BF16 = mybir.dt.bfloat16
U32 = mybir.dt.uint32
AF = mybir.ActivationFunctionType
ALU = mybir.AluOpType
AX = mybir.AxisListType
IOA = bass.IndirectOffsetOnAxis

NB, NSEQ, D = 4, 8192, 1024
NCORE = 8
NLOC = NSEQ // NCORE
ROWS = NB * NLOC
LNEPS = 1e-5
SCAP = 30.0


class Sem:
    def __init__(self, es, nc, name):
        self.h = es.enter_context(nc.semaphore(name))
        self.v = 0

    def bump(self, instr, n=1):
        instr.then_inc(self.h, n)
        self.v += n
        return self.v


def build_stage_a():
    nc = bass.Bass("TRN2", target_bir_lowering=False)
    x0_h = nc.dram_tensor("x0", [ROWS, D], F32, kind="ExternalInput").ap()
    wq_h = nc.dram_tensor("wq", [D, D], F32, kind="ExternalInput").ap()
    wk_h = nc.dram_tensor("wk", [D, D], F32, kind="ExternalInput").ap()
    wv_h = nc.dram_tensor("wv", [D, D], F32, kind="ExternalInput").ap()
    x1_h = nc.dram_tensor("x1", [ROWS, D], F32, kind="ExternalOutput").ap()
    ysc = nc.dram_tensor("ysc", [ROWS, D], F32).ap()
    cc_in = nc.dram_tensor("cc_in", [1, 512], F32).ap()
    cc_out = nc.dram_tensor("cc_out", [1, 512], F32).ap()

    es = ExitStack()
    t = lambda name, shape, dt: es.enter_context(nc.sbuf_tensor(name, shape, dt))
    wq = t("wqs", [128, 8 * D], BF16)
    wk = t("wks", [128, 8 * D], BF16)
    wv = t("wvs", [128, 8 * D], BF16)
    X = t("X", [128, 4096], F32)
    H = t("H", [128, 4096], F32)
    Hb = t("Hb", [128, 4096], BF16)
    hfm = t("hfm", [128, 4096], BF16)
    Qfm = t("Qfm", [128, 4096], BF16)
    Kfm = t("Kfm", [128, 4096], BF16)
    Qdup = t("Qdup", [128, 8192], BF16)
    Vtm = t("Vtm", [128, 4096], BF16)
    Vaug = t("Vaug", [128, 64 * 65], BF16)
    PT = [t(f"PT{i}", [128, 1024], BF16) for i in range(2)]
    onsb = t("onsb", [64, 512], F32)
    outTsb = t("outTsb", [64, 512], BF16)
    rec = t("rec", [1, 512], F32)
    Yslab = t("Yslab", [128, 4096], F32)
    scr = t("scr", [128, 1024], F32)
    sqd = t("sqd", [128, 1024], BF16)
    stat = t("stat", [128, 4], F32)
    SS = t("SS", [128, 8], F32)
    MR = t("MR", [128, 8], F32)
    stg = t("stg", [1, 8], F32)
    stg512 = t("stg512", [1, 512], F32)
    idm = t("idm", [128, 128], F32)
    ident = t("ident", [128, 128], BF16)
    ones1_64 = t("ones1_64", [1, 64], F32)
    ones1_128 = t("ones1_128", [1, 128], F32)
    ones128_1 = t("ones128_1", [128, 1], F32)
    cb_eps = t("cb_eps", [128, 1], F32)
    cb_cap = t("cb_cap", [128, 1], F32)
    XA = [t(f"XA{i}", [128, 1024], F32) for i in range(2)]
    YA = [t(f"YA{i}", [128, 1024], F32) for i in range(2)]
    X1T = [t(f"X1T{i}", [128, 1024], F32) for i in range(2)]

    pSpair = [es.enter_context(nc.psum_tensor(f"pSp{i}", [128, 1024], F32)) for i in range(2)]
    pAV = es.enter_context(nc.psum_tensor("pAV", [128, 512], F32))
    pMisc = es.enter_context(nc.psum_tensor("pMisc", [128, 512], F32))
    pTp = [es.enter_context(nc.psum_tensor(f"pTp{i}", [128, 128], BF16)) for i in range(2)]

    S = {}
    for name in ["s_w", "s_ld", "s_pre", "s_hb", "s_tp", "s_te", "s_mm", "s_ev",
                 "s_qkv", "s_S", "s_exp", "s_av", "s_rec", "s_bc", "s_outT",
                 "s_tp2", "s_yadd", "s_ysd", "s_yst", "s_st", "s_cc", "s_mr",
                 "s_x1l", "s_x1c", "s_x1s", "s_c", "s_r", "s_ccr", "s_ccc"]:
        S[name] = Sem(es, nc, name)

    with es:
        block = es.enter_context(nc.Block())

        @block.sync
        def _(sync):
            for g in range(8):
                b, sb2 = g // 2, g % 2
                row0 = b * 1024 + sb2 * 512
                if g >= 1:
                    sync.wait_ge(S["s_hb"].h, g)
                for tt_ in range(4):
                    S["s_ld"].bump(sync.dma_start(
                        out=X[:, tt_ * 1024:(tt_ + 1) * 1024],
                        in_=x0_h[row0 + 128 * tt_: row0 + 128 * (tt_ + 1), :]), 16)
                sync.wait_ge(S["s_yst"].h, g + 1)
                for tt_ in range(4):
                    S["s_ysd"].bump(sync.dma_start(
                        out=ysc[row0 + 128 * tt_: row0 + 128 * (tt_ + 1), :],
                        in_=Yslab[:, tt_ * 1024:(tt_ + 1) * 1024]), 16)
            sync.wait_ge(S["s_ysd"].h, 16 * 32)
            for i in range(32):
                p = i % 2
                if i >= 2:
                    sync.wait_ge(S["s_x1c"].h, i - 1)
                S["s_x1l"].bump(sync.dma_start(
                    out=XA[p][:], in_=x0_h[128 * i:128 * (i + 1), :]), 16)
                S["s_x1l"].bump(sync.dma_start(
                    out=YA[p][:], in_=ysc[128 * i:128 * (i + 1), :]), 16)
                sync.wait_ge(S["s_x1c"].h, i + 1)
                S["s_x1s"].bump(sync.dma_start(
                    out=x1_h[128 * i:128 * (i + 1), :], in_=X1T[p][:]), 16)
            sync.wait_ge(S["s_x1s"].h, 16 * 32)

        @block.gpsimd
        def _(gpsimd):
            S["s_w"].bump(gpsimd.dma_start(
                out=wq[:].rearrange("p (dc c) -> p dc c", dc=8),
                in_=wq_h.rearrange("(dc p) c -> p dc c", p=128)), 16)
            S["s_w"].bump(gpsimd.dma_start(
                out=wk[:].rearrange("p (dc c) -> p dc c", dc=8),
                in_=wk_h.rearrange("(dc p) c -> p dc c", p=128)), 16)
            S["s_w"].bump(gpsimd.dma_start(
                out=wv[:].rearrange("p (dc c) -> p dc c", dc=8),
                in_=wv_h.rearrange("(dc p) c -> p dc c", p=128)), 16)
            gpsimd.iota(out=idm[:], pattern=[[1, 128]], base=0, channel_multiplier=-1,
                        allow_small_or_imprecise_dtypes=True)
            S["s_pre"].bump(gpsimd.memset(Vaug[:], 1.0))
            gpsimd.wait_ge(S["s_st"].h, 3)
            S["s_cc"].bump(gpsimd.dma_start(out=cc_in[:], in_=stg512[:]), 16)
            gpsimd.wait_ge(S["s_cc"].h, 16)
            gpsimd.collective_compute(
                "AllReduce", ALU.add, replica_groups=[list(range(NCORE))],
                ins=[cc_in[:]], outs=[cc_out[:]]).then_inc(S["s_ccc"].h, 1)
            gpsimd.wait_ge(S["s_ccc"].h, 1)
            S["s_ccr"].bump(gpsimd.dma_start(out=stg512[:], in_=cc_out[:]), 16)

        @block.vector
        def _(vector):
            vector.wait_ge(S["s_pre"].h, 1)
            vector.tensor_scalar(out=ident[:], in0=idm[:], scalar1=0.0, scalar2=None,
                                 op0=ALU.is_equal)
            vector.memset(ones1_64[:], 1.0)
            vector.memset(cb_eps[:], LNEPS)
            vector.memset(cb_cap[:], -SCAP)
            vector.memset(ones1_128[:], 1.0)
            vector.memset(stg512[:], 0.0)
            vector.memset(ones128_1[:], 1.0)
            S["s_pre"].bump(vector.memset(SS[:], 0.0))

            for g in range(8):
                b = g // 2
                if g >= 1:
                    vector.wait_ge(S["s_ysd"].h, 64 * g)
                for tt_ in range(4):
                    vector.wait_ge(S["s_ld"].h, 16 * (4 * g + tt_ + 1))
                    xap = X[:, tt_ * 1024:(tt_ + 1) * 1024]
                    vector.tensor_reduce(out=stat[:, 0:1], in_=xap, axis=AX.X, op=ALU.add)
                    vector.drain()
                    vector.tensor_scalar(out=stat[:, 1:2], in0=stat[:, 0:1],
                                         scalar1=1.0 / 1024, scalar2=None, op0=ALU.mult)
                    vector.drain()
                    v = S["s_c"].bump(vector.tensor_scalar(
                        out=scr[:], in0=xap, scalar1=stat[:, 1:2], scalar2=None,
                        op0=ALU.subtract))
                    vector.wait_ge(S["s_r"].h, v)
                    vector.tensor_scalar(out=H[:, tt_ * 1024:(tt_ + 1) * 1024],
                                         in0=scr[:], scalar1=stat[:, 3:4], scalar2=None,
                                         op0=ALU.mult)
                S["s_hb"].bump(vector.tensor_copy(out=Hb[:], in_=H[:]))
                for i in range(32):
                    rt_, dc = i // 8, i % 8
                    vector.wait_ge(S["s_tp"].h, 32 * g + i + 1)
                    S["s_te"].bump(vector.tensor_copy(
                        out=hfm[:, dc * 512 + rt_ * 128: dc * 512 + (rt_ + 1) * 128],
                        in_=pTp[i % 2][:]))
                for i in range(24):
                    vector.wait_ge(S["s_mm"].h, 24 * g + i + 1)
                    pp = pSpair[i % 2][:, 0:512]
                    if i < 8:
                        dst = Qfm[:, i * 512:(i + 1) * 512]
                    elif i < 16:
                        dst = Kfm[:, (i - 8) * 512:(i - 7) * 512]
                    else:
                        k = i - 16
                        rt_, ch = k // 2, k % 2
                        dst = Vtm[:, rt_ * 1024 + ch * 512: rt_ * 1024 + (ch + 1) * 512]
                    S["s_ev"].bump(vector.tensor_copy(out=dst, in_=pp))
                for sq in range(16):
                    src = Qfm[(sq % 2) * 64:(sq % 2) * 64 + 64,
                              (sq // 2) * 512:(sq // 2 + 1) * 512]
                    vector.tensor_copy(out=Qdup[0:64, sq * 512:(sq + 1) * 512], in_=src)
                    vector.tensor_copy(out=Qdup[64:128, sq * 512:(sq + 1) * 512], in_=src)
                for s_ in range(16):
                    for rt_ in range(4):
                        cid = s_ * 4 + rt_
                        vector.tensor_copy(
                            out=Vaug[:, cid * 65: cid * 65 + 64],
                            in_=Vtm[:, rt_ * 1024 + s_ * 64: rt_ * 1024 + (s_ + 1) * 64])
                S["s_qkv"].bump(vector.engine_nop())
                for sq in range(16):
                    qi = 16 * g + sq
                    vector.wait_ge(S["s_av"].h, qi + 1)
                    vector.reciprocal(out=rec[:], in_=pAV[64:65, :])
                    S["s_rec"].bump(vector.tensor_copy(out=onsb[:], in_=pAV[0:64, :]))
                    vector.wait_ge(S["s_bc"].h, qi + 1)
                    vector.drain()
                    S["s_outT"].bump(vector.tensor_tensor(
                        out=outTsb[:], in0=onsb[:], in1=pMisc[0:64, :], op=ALU.mult))
                    for rt_ in range(4):
                        vector.wait_ge(S["s_tp2"].h, 4 * qi + rt_ + 1)
                        col = rt_ * 1024 + sq * 64
                        S["s_yadd"].bump(vector.tensor_tensor(
                            out=Yslab[:, col: col + 64],
                            in0=pTp[rt_ % 2][:, 0:64], in1=H[:, col: col + 64],
                            op=ALU.add))
                for tt_ in range(4):
                    yap = Yslab[:, tt_ * 1024:(tt_ + 1) * 1024]
                    vector.tensor_reduce(out=stat[:, 0:1], in_=yap, axis=AX.X, op=ALU.add)
                    vector.tensor_tensor(out=scr[:], in0=yap, in1=yap, op=ALU.mult)
                    vector.tensor_reduce(out=stat[:, 1:2], in_=scr[:], axis=AX.X, op=ALU.add)
                    vector.tensor_tensor(out=SS[:, b:b + 1], in0=SS[:, b:b + 1],
                                         in1=stat[:, 0:1], op=ALU.add)
                    vector.tensor_tensor(out=SS[:, 4 + b:5 + b], in0=SS[:, 4 + b:5 + b],
                                         in1=stat[:, 1:2], op=ALU.add)
                    vector.drain()
                S["s_yst"].bump(vector.engine_nop())

            vector.wait_ge(S["s_st"].h, 2)
            vector.tensor_copy(out=stg[:], in_=pMisc[0:1, 0:8])
            vector.drain()
            S["s_st"].bump(vector.tensor_copy(out=stg512[:, 0:8], in_=stg[:]))
            vector.wait_ge(S["s_ccr"].h, 16)
            vector.tensor_copy(out=stg[:], in_=stg512[:, 0:8])
            ninv = 1.0 / (NSEQ * D)
            vector.tensor_scalar(out=stg[:, 0:4], in0=stg[:, 0:4], scalar1=ninv,
                                 scalar2=None, op0=ALU.mult)
            vector.tensor_scalar(out=stg[:, 4:8], in0=stg[:, 4:8], scalar1=ninv,
                                 scalar2=None, op0=ALU.mult)
            vector.drain()
            vector.tensor_tensor(out=scr[0:1, 0:4], in0=stg[:, 0:4], in1=stg[:, 0:4],
                                 op=ALU.mult)
            vector.drain()
            S["s_st"].bump(vector.tensor_tensor(
                out=stg[:, 4:8], in0=stg[:, 4:8], in1=scr[0:1, 0:4],
                op=ALU.subtract))
            vector.wait_ge(S["s_mr"].h, 2)
            S["s_mr"].bump(vector.tensor_copy(out=MR[:], in_=pMisc[0:128, 0:8]))
            vector.drain()
            for i in range(32):
                p = i % 2
                b = i // 8
                vector.wait_ge(S["s_x1l"].h, 32 * (i + 1))
                if i >= 2:
                    vector.wait_ge(S["s_x1s"].h, 16 * (i - 1))
                vector.tensor_scalar(out=X1T[p][:], in0=YA[p][:],
                                     scalar1=MR[:, b:b + 1], scalar2=MR[:, 4 + b:5 + b],
                                     op0=ALU.subtract, op1=ALU.mult)
                S["s_x1c"].bump(vector.tensor_tensor(
                    out=X1T[p][:], in0=X1T[p][:], in1=XA[p][:], op=ALU.add))

        @block.scalar
        def _(scalar):
            for g in range(8):
                for tt_ in range(4):
                    scalar.wait_ge(S["s_c"].h, 4 * g + tt_ + 1)
                    scalar.activation(out=sqd[:], in_=scr[:], func=AF.Square,
                                      accum_out=stat[:, 2:3])
                    scalar.drain()
                    scalar.activation(out=stat[:, 3:4], in_=stat[:, 2:3], func=AF.Ln,
                                      scale=1.0 / 1024, bias=cb_eps[:])
                    scalar.drain()
                    S["s_r"].bump(scalar.activation(out=stat[:, 3:4], in_=stat[:, 3:4],
                                                    func=AF.Exp, scale=-0.5))
                for sq in range(16):
                    for p_ in range(32):
                        idx = (16 * g + sq) * 32 + p_
                        scalar.wait_ge(S["s_S"].h, idx + 1)
                        S["s_exp"].bump(scalar.activation(
                            out=PT[p_ % 2][:], in_=pSpair[p_ % 2][:], func=AF.Exp,
                            bias=cb_cap[:]))
            scalar.wait_ge(S["s_st"].h, 4)
            scalar.activation(out=stg[:, 4:8], in_=stg[:, 4:8], func=AF.Ln,
                              bias=cb_eps[0:1, 0:1])
            scalar.drain()
            S["s_st"].bump(scalar.activation(out=stg[:, 4:8], in_=stg[:, 4:8],
                                             func=AF.Exp, scale=-0.5))

        @block.tensor
        def _(tensor):
            tensor.wait_ge(S["s_w"].h, 48)
            tensor.wait_ge(S["s_pre"].h, 2)
            for g in range(8):
                tensor.wait_ge(S["s_hb"].h, g + 1)
                if g >= 1:
                    tensor.wait_ge(S["s_yadd"].h, 64 * g)
                    tensor.wait_ge(S["s_exp"].h, 512 * g)
                for i in range(32):
                    rt_, dc = i // 8, i % 8
                    if 32 * g + i >= 2:
                        tensor.wait_ge(S["s_te"].h, 32 * g + i - 1)
                    S["s_tp"].bump(tensor.transpose(
                        out=pTp[i % 2][:],
                        in_=Hb[:, rt_ * 1024 + dc * 128: rt_ * 1024 + (dc + 1) * 128],
                        identity=ident[:]))
                tensor.wait_ge(S["s_te"].h, 32 * (g + 1))
                for i in range(24):
                    mmi = 24 * g + i
                    if mmi >= 2:
                        tensor.wait_ge(S["s_ev"].h, mmi - 1)
                    pdst = pSpair[i % 2][:, 0:512]
                    for dc in range(8):
                        if i < 8:
                            lh = wq[:, dc * 1024 + i * 128: dc * 1024 + (i + 1) * 128]
                            rh = hfm[:, dc * 512:(dc + 1) * 512]
                        elif i < 16:
                            cc = i - 8
                            lh = wk[:, dc * 1024 + cc * 128: dc * 1024 + (cc + 1) * 128]
                            rh = hfm[:, dc * 512:(dc + 1) * 512]
                        else:
                            k = i - 16
                            rt_, ch = k // 2, k % 2
                            lh = hfm[:, dc * 512 + rt_ * 128: dc * 512 + (rt_ + 1) * 128]
                            rh = wv[:, dc * 1024 + ch * 512: dc * 1024 + ch * 512 + 512]
                        mm = tensor.matmul(out=pdst, lhsT=lh, rhs=rh,
                                           start=(dc == 0), stop=(dc == 7))
                    S["s_mm"].bump(mm)
                tensor.wait_ge(S["s_qkv"].h, g + 1)
                for sq in range(16):
                    qi = 16 * g + sq
                    if qi >= 1:
                        tensor.wait_ge(S["s_rec"].h, qi)
                        tensor.wait_ge(S["s_outT"].h, qi)
                    for p_ in range(33):
                        if p_ < 32:
                            cc, rt_ = p_ // 4, p_ % 4
                            idx = qi * 32 + p_
                            if idx >= 2:
                                tensor.wait_ge(S["s_exp"].h, idx - 1)
                            kA = Kfm[0:64, cc * 512 + rt_ * 128:
                                     cc * 512 + (rt_ + 1) * 128]
                            kB = Kfm[64:128, cc * 512 + rt_ * 128:
                                     cc * 512 + (rt_ + 1) * 128]
                            tensor.matmul(out=pSpair[p_ % 2][:, 0:512], lhsT=kA,
                                          rhs=Qdup[0:64, sq * 512:(sq + 1) * 512],
                                          start=True, stop=True, tile_position=(0, 0))
                            S["s_S"].bump(tensor.matmul(
                                out=pSpair[p_ % 2][:, 512:1024], lhsT=kB,
                                rhs=Qdup[64:128, sq * 512:(sq + 1) * 512],
                                start=True, stop=True, tile_position=(64, 0)))
                        if p_ >= 1:
                            pm = p_ - 1
                            ccm, rtm = pm // 4, pm % 4
                            tensor.wait_ge(S["s_exp"].h, qi * 32 + pm + 1)
                            cidA = (2 * ccm) * 4 + rtm
                            cidB = (2 * ccm + 1) * 4 + rtm
                            tensor.matmul(out=pAV[0:65, :],
                                          lhsT=Vaug[:, cidA * 65: cidA * 65 + 65],
                                          rhs=PT[pm % 2][:, 0:512],
                                          start=(pm == 0), stop=False)
                            mm = tensor.matmul(out=pAV[0:65, :],
                                               lhsT=Vaug[:, cidB * 65: cidB * 65 + 65],
                                               rhs=PT[pm % 2][:, 512:1024],
                                               start=False, stop=(pm == 31))
                    S["s_av"].bump(mm)
                    tensor.wait_ge(S["s_rec"].h, qi + 1)
                    S["s_bc"].bump(tensor.matmul(
                        out=pMisc[0:64, :], lhsT=ones1_64[:], rhs=rec[:],
                        start=True, stop=True))
                    tensor.wait_ge(S["s_outT"].h, qi + 1)
                    for rt_ in range(4):
                        if 4 * qi + rt_ >= 2:
                            tensor.wait_ge(S["s_yadd"].h, 4 * qi + rt_ - 1)
                        S["s_tp2"].bump(tensor.transpose(
                            out=pTp[rt_ % 2][:, 0:64],
                            in_=outTsb[:, rt_ * 128:(rt_ + 1) * 128],
                            identity=ident[0:64, 0:64]))
            tensor.wait_ge(S["s_yst"].h, 8)
            S["s_st"].bump(tensor.matmul(out=pMisc[0:1, 0:8], lhsT=ones128_1[:],
                                         rhs=SS[:], start=True, stop=True), 2)
            tensor.wait_ge(S["s_st"].h, 5)
            S["s_mr"].bump(tensor.matmul(out=pMisc[0:128, 0:8], lhsT=ones1_128[:],
                                         rhs=stg[:], start=True, stop=True), 2)

    return nc


def run_stage_a(inputs, x0_full=None):
    x0 = np.ascontiguousarray((x0_full if x0_full is not None
                               else inputs["L0_attn_x"]).reshape(NB, NSEQ, D))
    wqT = np.ascontiguousarray(inputs["L0_attn_Wq"].T)
    wkT = np.ascontiguousarray(inputs["L0_attn_Wk"].T)
    wvT = np.ascontiguousarray(inputs["L0_attn_Wv"].T)
    in_maps = []
    for j in range(NCORE):
        xs = np.ascontiguousarray(x0[:, j * NLOC:(j + 1) * NLOC, :]).reshape(ROWS, D)
        in_maps.append({"x0": xs, "wq": wqT, "wk": wkT, "wv": wvT})
    nc = build_stage_a()
    res = run_bass_kernel_spmd(nc, in_maps, core_ids=list(range(NCORE)))
    x1 = np.zeros((NB, NSEQ, D), np.float32)
    for j in range(NCORE):
        x1[:, j * NLOC:(j + 1) * NLOC, :] = res.results[j]["x1"].reshape(NB, NLOC, D)
    return x1


AGR = 4104

def build_stage_b():
    nc = bass.Bass("TRN2", target_bir_lowering=False)
    x1_h = nc.dram_tensor("x1", [ROWS, D], F32, kind="ExternalInput").ap()
    eid_h = nc.dram_tensor("eid", [128, 1], F32, kind="ExternalInput").ap()
    wg_h = nc.dram_tensor("wg", [D, 8], F32, kind="ExternalInput").ap()
    w1_h = nc.dram_tensor("w1", [D, 4096], F32, kind="ExternalInput").ap()
    w2_h = nc.dram_tensor("w2", [4096, D], F32, kind="ExternalInput").ap()
    x2_h = nc.dram_tensor("x2", [ROWS, D], F32, kind="ExternalOutput").ap()
    agin = nc.dram_tensor("agin", [AGR, D], BF16).ap()
    agout = nc.dram_tensor("agout", [8 * AGR, D], BF16, addr_space="Shared").ap()
    ei_h = nc.dram_tensor("ei", [4096, D], BF16).ap()
    eo_h = nc.dram_tensor("eo", [4096, D], BF16).ap()
    egout = nc.dram_tensor("egout", [8 * 4096, D], BF16, addr_space="Shared").ap()
    offsc = nc.dram_tensor("offsc", [256, 8], F32).ap()

    es = ExitStack()
    t = lambda name, shape, dt: es.enter_context(nc.sbuf_tensor(name, shape, dt))
    w1s = t("w1s", [128, 8 * 4096], BF16)
    w2s = t("w2s", [128, 32 * 1024], BF16)
    wgs = t("wgs", [128, 64], F32)
    XT = [t(f"XT{i}", [128, 1024], F32) for i in range(2)]
    T1 = [t(f"T1{i}", [128, 1024], F32) for i in range(2)]
    T1b = [t(f"T1b{i}", [128, 1024], BF16) for i in range(2)]
    t1fm_s = t("t1fm", [128, 1024], F32)
    t1fm = [t1fm_s, t1fm_s]
    Lg = [t(f"Lg{i}", [128, 8], F32) for i in range(2)]
    EGsb = [t(f"EGsb{i}", [128, 2], BF16) for i in range(2)]
    EIDXGATE = t("EIDXGATE", [2, 4096], BF16)
    scr = t("scrB", [128, 1024], F32)
    stat = t("statB", [128, 4], F32)
    m8 = t("m8", [128, 2], F32)
    ohr = t("ohrB", [128, 8], F32)
    e8 = t("e8B", [128, 8], F32)
    eidxf = t("eidxf", [128, 1], F32)
    gatef = t("gatef", [128, 1], F32)
    iota8 = t("iota8", [128, 8], F32)
    IOTA64 = t("IOTA64", [128, 64], F32)
    lane6 = t("lane6", [128, 1], F32)
    lane7 = t("lane7", [128, 1], F32)
    idm = t("idmB", [128, 128], F32)
    identf = t("identf", [128, 128], F32)
    identb = t("identb", [128, 128], BF16)
    lstrb = t("lstrb", [128, 128], BF16)
    lstrf = t("lstrf", [128, 128], F32)
    ones1_128f = t("o1_128f", [1, 128], F32)
    ones128_1b = t("o128_1b", [128, 1], BF16)
    cb_eps = t("cb_epsB", [128, 1], F32)
    eid = t("eid_sb", [128, 1], F32)
    mg8 = t("mg8", [128, 1], F32)
    KEEP8 = t("KEEP8", [128, 8], F32)
    IDX2f = t("IDX2f", [128, 32], F32)
    GAT2f = t("GAT2f", [128, 32], F32)
    EIDXall = t("EIDXall", [128, 1024], BF16)
    GATEall = t("GATEall", [128, 1024], BF16)
    OHall = t("OHall", [128, 2048], BF16)
    CNT = t("CNT", [8, 256], F32)
    Ctile = t("Ctile", [64, 8], F32)
    OFFA4 = t("OFFA4", [64, 8], F32)
    OFFROW = t("OFFROW", [1, 512], F32)
    TRr = [t(f"TRr{i}", [128, 1024], BF16) for i in range(2)]
    IDXu = [t(f"IDXu{i}", [128, 8], U32) for i in range(2)]
    POSF64 = t("POSF64", [128, 64], F32)
    OHF = t("OHF", [128, 64], F32)
    PS8 = t("PS8", [128, 8], F32)
    KK8 = t("KK8", [128, 8], F32)
    AA8 = t("AA8", [128, 8], F32)
    BB8 = t("BB8", [128, 8], F32)
    IDX2u = t("IDX2u", [128, 32], U32)
    GAT2 = t("GAT2", [128, 32], F32)
    EGL = t("EGL", [8, 1024], BF16)
    EIT = [t(f"EIT{i}", [128, 1024], BF16) for i in range(2)]
    zeros = EIT[0]
    eifm = [t(f"eifm{i}", [128, 1024], BF16) for i in range(2)]
    HBt = t("HBt", [128, 2048], BF16)
    hfmB = t("hfmB", [128, 2048], BF16)
    EOT = T1b
    GOT = TRr
    X2T = T1

    pA = es.enter_context(nc.psum_tensor("pA", [128, 512], F32))
    pB = es.enter_context(nc.psum_tensor("pB", [128, 512], F32))
    pC = es.enter_context(nc.psum_tensor("pC", [128, 512], F32))
    pD = es.enter_context(nc.psum_tensor("pD", [128, 512], F32))
    pG = es.enter_context(nc.psum_tensor("pG", [128, 512], F32))
    pT1 = es.enter_context(nc.psum_tensor("pT1B", [128, 512], BF16))
    pT2 = es.enter_context(nc.psum_tensor("pT2B", [128, 512], BF16))

    S = {}
    for name in ["s_w", "s_pre", "s_x1ld", "s_c", "s_r", "s_t1c", "s_m8", "s_rt",
                 "s_tp", "s_tpe", "s_lg", "s_lge", "s_eg", "s_egt", "s_ege",
                 "s_agw", "s_ag1", "s_eiz", "s_egl", "s_egx", "s_egxe",
                 "s_oh", "s_cnt", "s_cnte", "s_ctp", "s_ctpe", "s_off", "s_offe", "s_offd", "s_offl",
                 "s_pos", "s_pose", "s_idx", "s_sc", "s_trl",
                 "s_eild", "s_eitp", "s_eitpe", "s_h", "s_hg", "s_htp", "s_htpe",
                 "s_eo", "s_eoe", "s_eos", "s_ag2", "s_gz", "s_ga",
                 "s_x2l", "s_x2c", "s_x2s"]:
        S[name] = Sem(es, nc, name)

    NT = 32

    with es:
        block = es.enter_context(nc.Block())

        @block.sync
        def _(sync):
            sync.dma_start(out=wgs[:].rearrange("p (dc e) -> p dc e", dc=8),
                           in_=wg_h.rearrange("(dc p) e -> p dc e", p=128)
                           ).then_inc(S["s_w"].h, 16)
            sync.dma_start(out=eid[:], in_=eid_h[:]).then_inc(S["s_w"].h, 16)
            for i in range(NT):
                p = i % 2
                if i >= 2:
                    sync.wait_ge(S["s_t1c"].h, i - 1)
                S["s_x1ld"].bump(sync.dma_start(
                    out=XT[p][:], in_=x1_h[128 * i:128 * (i + 1), :]), 16)
                sync.wait_ge(S["s_t1c"].h, i + 1)
                S["s_agw"].bump(sync.dma_start(
                    out=agin[128 * i:128 * (i + 1), :], in_=T1b[p][:]), 16)
            sync.wait_ge(S["s_ege"].h, NT)
            S["s_agw"].bump(sync.dma_start(
                out=agin[4096:4104, :].rearrange("(a b) c -> a (b c)", a=2),
                in_=EIDXGATE[:]), 16)
            sync.wait_ge(S["s_ag1"].h, 1)
            for r in range(8):
                if r >= 1:
                    sync.wait_ge(S["s_egxe"].h, 8 * r)
                S["s_egl"].bump(sync.dma_start(
                    out=EGL[:], in_=agout[r * AGR + 4096: r * AGR + 4104, :]), 16)
            for b_ in range(4):
                sync.wait_ge(S["s_offe"].h, b_ + 1)
                S["s_offd"].bump(sync.dma_start(
                    out=offsc[b_ * 64:(b_ + 1) * 64, :], in_=OFFA4[:]), 16)
            for gi in range(32):
                b, g8 = gi // 8, gi % 8
                if g8 == 0:
                    sync.wait_ge(S["s_offd"].h, 16 * (b + 1))
                    if b >= 1:
                        sync.wait_ge(S["s_pose"].h, 8 * b)
                    S["s_offl"].bump(sync.dma_start(
                        out=OFFROW[:],
                        in_=offsc[b * 64:(b + 1) * 64, :].rearrange(
                            "r c -> (r c)").rearrange("(a f) -> a f", a=1)), 16)
                for tloc in range(8):
                    kk_ = gi * 8 + tloc
                    if kk_ >= 2:
                        sync.wait_ge(S["s_sc"].h, 16 * (kk_ - 1))
                    S["s_trl"].bump(sync.dma_start(
                        out=TRr[kk_ % 2][:],
                        in_=agout[g8 * AGR + b * 1024 + tloc * 128:
                                  g8 * AGR + b * 1024 + (tloc + 1) * 128, :]), 16)
            sync.wait_ge(S["s_sc"].h, 16 * 256)
            for st in range(32):
                p = st % 2
                if st >= 2:
                    sync.wait_ge(S["s_eitpe"].h, 8 * (st - 1))
                S["s_eild"].bump(sync.dma_start(
                    out=EIT[p][:], in_=ei_h[128 * st:128 * (st + 1), :]), 16)
                sync.wait_ge(S["s_eoe"].h, 2 * (st + 1))
                S["s_eos"].bump(sync.dma_start(
                    out=eo_h[128 * st:128 * (st + 1), :], in_=EOT[p][:]), 16)
            sync.wait_ge(S["s_ag2"].h, 1)
            for i in range(NT):
                p = i % 2
                if i >= 2:
                    sync.wait_ge(S["s_x2c"].h, i - 1)
                S["s_x2l"].bump(sync.dma_start(
                    out=XT[p][:], in_=x1_h[128 * i:128 * (i + 1), :]), 16)
                sync.wait_ge(S["s_x2c"].h, i + 1)
                S["s_x2s"].bump(sync.dma_start(
                    out=x2_h[128 * i:128 * (i + 1), :], in_=X2T[p][:]), 16)
            sync.wait_ge(S["s_x2s"].h, 16 * NT)

        @block.gpsimd
        def _(gpsimd):
            S["s_w"].bump(gpsimd.dma_start(
                out=w1s[:].rearrange("p (dc h) -> p dc h", dc=8),
                in_=w1_h.rearrange("(dc p) h -> p dc h", p=128)), 16)
            S["s_w"].bump(gpsimd.dma_start(
                out=w2s[:].rearrange("p (fc d) -> p fc d", fc=32),
                in_=w2_h.rearrange("(fc p) d -> p fc d", p=128)), 16)
            gpsimd.iota(out=idm[:], pattern=[[1, 128]], base=0, channel_multiplier=-1,
                        allow_small_or_imprecise_dtypes=True)
            gpsimd.iota(out=iota8[:], pattern=[[1, 8]], base=0, channel_multiplier=0,
                        allow_small_or_imprecise_dtypes=True)
            gpsimd.iota(out=IOTA64[:], pattern=[[0, 8], [1, 8]], base=0,
                        channel_multiplier=0, allow_small_or_imprecise_dtypes=True)
            gpsimd.iota(out=lane6[:], pattern=[[0, 1]], base=1000000,
                        channel_multiplier=1, allow_small_or_imprecise_dtypes=True)
            S["s_pre"].bump(gpsimd.iota(
                out=lane7[:], pattern=[[0, 1]], base=10000000, channel_multiplier=1,
                allow_small_or_imprecise_dtypes=True))
            gpsimd.wait_ge(S["s_pre"].h, 2)
            for st in range(32):
                S["s_eiz"].bump(gpsimd.dma_start(
                    out=ei_h[128 * st:128 * (st + 1), :], in_=zeros[:]), 16)
            gpsimd.wait_ge(S["s_agw"].h, 16 * (NT + 1))
            gpsimd.collective_compute(
                "AllGather", ALU.bypass, replica_groups=[list(range(NCORE))],
                ins=[agin[:]], outs=[agout[:]]).then_inc(S["s_ag1"].h, 1)
            gpsimd.wait_ge(S["s_eiz"].h, 16 * 32)
            for gi in range(32):
                for tloc in range(8):
                    kk_ = gi * 8 + tloc
                    gpsimd.wait_ge(S["s_trl"].h, 16 * (kk_ + 1))
                    gpsimd.wait_ge(S["s_idx"].h, gi + 1)
                    S["s_sc"].bump(gpsimd.indirect_dma_start(
                        out=ei_h[:], out_offset=IOA(IDXu[gi % 2][:, tloc:tloc + 1], 0),
                        in_=TRr[kk_ % 2][:], in_offset=None, oob_is_err=False), 16)
            gpsimd.wait_ge(S["s_eos"].h, 16 * 32)
            gpsimd.collective_compute(
                "AllGather", ALU.bypass, replica_groups=[list(range(NCORE))],
                ins=[eo_h[:]], outs=[egout[:]]).then_inc(S["s_ag2"].h, 1)
            gpsimd.wait_ge(S["s_ag2"].h, 1)
            for i in range(NT):
                p = i % 2
                gpsimd.wait_ge(S["s_gz"].h, i + 1)
                S["s_ga"].bump(gpsimd.indirect_dma_start(
                    out=GOT[p][:], out_offset=None, in_=egout[:],
                    in_offset=IOA(IDX2u[:, i:i + 1], 0), oob_is_err=False), 16)

        @block.scalar
        def _(scalar):
            for i in range(NT):
                p = i % 2
                scalar.wait_ge(S["s_c"].h, i + 1)
                scalar.activation(out=OHall[:, 0:1024], in_=scr[:], func=AF.Square,
                                  accum_out=stat[:, 2:3])
                scalar.drain()
                scalar.activation(out=stat[:, 3:4], in_=stat[:, 2:3], func=AF.Ln,
                                  scale=1.0 / 1024, bias=cb_eps[:])
                scalar.drain()
                S["s_r"].bump(scalar.activation(out=stat[:, 3:4], in_=stat[:, 3:4],
                                                func=AF.Exp, scale=-0.5))
                scalar.wait_ge(S["s_m8"].h, i + 1)
                S["s_rt"].bump(scalar.activation(out=e8[:], in_=Lg[p][:], func=AF.Exp,
                                                 bias=m8[:, 1:2]))
            for st in range(32):
                for hc in range(8):
                    scalar.wait_ge(S["s_h"].h, 8 * st + hc + 1)
                    S["s_hg"].bump(scalar.activation(
                        out=HBt[:, (hc % 4) * 512:(hc % 4 + 1) * 512],
                        in_=(pA if hc % 2 == 0 else pB)[:],
                        func=AF.Gelu_apprx_tanh))

        @block.vector
        def _(vector):
            vector.wait_ge(S["s_pre"].h, 1)
            vector.tensor_scalar(out=identf[:], in0=idm[:], scalar1=0.0, scalar2=None,
                                 op0=ALU.is_equal)
            vector.tensor_scalar(out=identb[:], in0=idm[:], scalar1=0.0, scalar2=None,
                                 op0=ALU.is_equal)
            vector.tensor_scalar(out=lstrb[:], in0=idm[:], scalar1=0.0, scalar2=None,
                                 op0=ALU.is_gt)
            vector.tensor_scalar(out=lstrf[:], in0=idm[:], scalar1=0.0, scalar2=None,
                                 op0=ALU.is_gt)
            vector.memset(ones1_128f[:], 1.0)
            vector.memset(ones128_1b[:], 1.0)
            vector.memset(cb_eps[:], LNEPS)
            S["s_pre"].bump(vector.memset(zeros[:], 0.0))
            for i in range(NT):
                p = i % 2
                vector.wait_ge(S["s_x1ld"].h, 16 * (i + 1))
                xap = XT[p][:]
                vector.tensor_reduce(out=stat[:, 0:1], in_=xap, axis=AX.X, op=ALU.add)
                vector.drain()
                vector.tensor_scalar(out=stat[:, 1:2], in0=stat[:, 0:1],
                                     scalar1=1.0 / 1024, scalar2=None, op0=ALU.mult)
                vector.drain()
                v = S["s_c"].bump(vector.tensor_scalar(
                    out=scr[:], in0=xap, scalar1=stat[:, 1:2], scalar2=None,
                    op0=ALU.subtract))
                vector.wait_ge(S["s_r"].h, v)
                vector.tensor_scalar(out=T1[p][:], in0=scr[:], scalar1=stat[:, 3:4],
                                     scalar2=None, op0=ALU.mult)
                S["s_t1c"].bump(vector.tensor_copy(out=T1b[p][:], in_=T1[p][:]))
                for dc in range(8):
                    vector.wait_ge(S["s_tp"].h, 8 * i + dc + 1)
                    S["s_tpe"].bump(vector.tensor_copy(
                        out=t1fm[p][:, dc * 128:(dc + 1) * 128],
                        in_=(pA if dc % 2 == 0 else pB)[:, 0:128]))
                vector.wait_ge(S["s_lg"].h, i + 1)
                S["s_lge"].bump(vector.tensor_copy(out=Lg[p][:],
                                                   in_=(pC if p == 0 else pD)[:, 0:8]))
                vector.drain()
                vector.tensor_reduce(out=m8[:, 0:1], in_=Lg[p][:], axis=AX.X, op=ALU.max)
                S["s_m8"].bump(vector.tensor_reduce(out=m8[:, 1:2], in_=Lg[p][:],
                                                    axis=AX.X, op=ALU.max, negate=True))
                vector.drain()
                vector.tensor_scalar(out=ohr[:], in0=Lg[p][:], scalar1=m8[:, 0:1],
                                     scalar2=None, op0=ALU.is_equal)
                vector.drain()
                vector.tensor_scalar(out=ohr[:], in0=ohr[:], scalar1=1000.0,
                                     scalar2=None, op0=ALU.mult)
                vector.drain()
                vector.tensor_tensor(out=ohr[:], in0=iota8[:], in1=ohr[:],
                                     op=ALU.subtract)
                vector.drain()
                vector.tensor_reduce(out=eidxf[:], in_=ohr[:], axis=AX.X, op=ALU.min)
                vector.drain()
                vector.tensor_scalar(out=eidxf[:], in0=eidxf[:], scalar1=1000.0,
                                     scalar2=None, op0=ALU.add)
                vector.drain()
                vector.wait_ge(S["s_rt"].h, i + 1)
                vector.tensor_reduce(out=gatef[:], in_=e8[:], axis=AX.X, op=ALU.add)
                vector.drain()
                vector.reciprocal(out=gatef[:], in_=gatef[:])
                vector.drain()
                if i >= 2:
                    vector.wait_ge(S["s_egt"].h, i - 1)
                vector.tensor_copy(out=EGsb[p][:, 0:1], in_=eidxf[:])
                S["s_eg"].bump(vector.tensor_copy(out=EGsb[p][:, 1:2], in_=gatef[:]))
                vector.wait_ge(S["s_egt"].h, i + 1)
                S["s_ege"].bump(vector.tensor_copy(
                    out=EIDXGATE[:, i * 128:(i + 1) * 128], in_=pT1[0:2, 0:128]))
            for r in range(8):
                for tloc in range(8):
                    vector.wait_ge(S["s_egx"].h, 8 * r + tloc + 1)
                    pt = pT1 if tloc % 2 == 0 else pT2
                    k = r * 8 + tloc
                    vector.tensor_copy(out=EIDXall[:, k * 4:(k + 1) * 4],
                                       in_=pt[:, 0:4])
                    S["s_egxe"].bump(vector.tensor_copy(
                        out=GATEall[:, k * 4:(k + 1) * 4], in_=pt[:, 4:8]))
            EIv = EIDXall[:].rearrange("p (k four) -> p k four", four=4)
            GAv = GATEall[:].rearrange("p (k four) -> p k four", four=4)
            for gi in range(32):
                b, g8 = gi // 8, gi % 8
                eg8 = EIv[:, g8 * 8:(g8 + 1) * 8, b]
                vector.tensor_tensor(
                    out=OHall[:, gi * 64:(gi + 1) * 64].rearrange(
                        "p (t e) -> p t e", t=8),
                    in0=IOTA64[:].rearrange("p (t e) -> p t e", t=8),
                    in1=eg8.unsqueeze(2).broadcast_to([128, 8, 8]),
                    op=ALU.is_equal)
                S["s_oh"].bump(vector.engine_nop())
            vector.wait_ge(S["s_cnt"].h, 256)
            S["s_cnte"].bump(vector.tensor_copy(out=CNT[:], in_=pG[0:8, 0:256]))
            for b in range(4):
                vector.wait_ge(S["s_ctp"].h, b + 1)
                S["s_ctpe"].bump(vector.tensor_copy(out=Ctile[:],
                                                    in_=pG[0:64, 256:264]))
                vector.wait_ge(S["s_off"].h, b + 1)
                if b >= 1:
                    vector.wait_ge(S["s_offd"].h, 16 * b)
                S["s_offe"].bump(vector.tensor_copy(
                    out=OFFA4[:], in_=pG[0:64, 264:272]))
            for gi in range(32):
                b, g8 = gi // 8, gi % 8
                vector.wait_ge(S["s_pos"].h, gi + 1)
                ppos = pA if gi % 2 == 0 else pB
                vector.tensor_copy(out=POSF64[:], in_=ppos[:, 0:64])
                vector.tensor_copy(out=OHF[:], in_=OHall[:, gi * 64:(gi + 1) * 64])
                vector.drain()
                vector.tensor_tensor(out=OHF[:], in0=OHF[:], in1=POSF64[:],
                                     op=ALU.mult)
                vector.drain()
                S["s_pose"].bump(vector.tensor_reduce(
                    out=PS8[:], in_=OHF[:].rearrange("p (t e) -> p t e", t=8),
                    axis=AX.X, op=ALU.add))
                vector.drain()
                vector.tensor_scalar(out=KEEP8[:], in0=PS8[:], scalar1=1024.0,
                                     scalar2=None, op0=ALU.is_lt)
                eg8 = EIv[:, g8 * 8:(g8 + 1) * 8, b]
                vector.tensor_scalar(out=AA8[:], in0=eg8, scalar1=eid[:],
                                     scalar2=None, op0=ALU.is_equal)
                vector.drain()
                vector.tensor_tensor(out=KK8[:], in0=KEEP8[:], in1=AA8[:], op=ALU.mult)
                vector.drain()
                vector.tensor_scalar(out=AA8[:], in0=PS8[:], scalar1=1.0,
                                     scalar2=float(b * 1024), op0=ALU.mult,
                                     op1=ALU.add)
                vector.drain()
                vector.tensor_tensor(out=BB8[:], in0=AA8[:],
                                     in1=lane6[:].broadcast_to([128, 8]),
                                     op=ALU.subtract)
                vector.drain()
                vector.tensor_tensor(out=BB8[:], in0=BB8[:], in1=KK8[:], op=ALU.mult)
                vector.drain()
                vector.tensor_tensor(out=BB8[:], in0=BB8[:],
                                     in1=lane6[:].broadcast_to([128, 8]), op=ALU.add)
                vector.drain()
                if gi >= 2:
                    vector.wait_ge(S["s_sc"].h, 16 * 8 * (gi - 1))
                S["s_idx"].bump(vector.tensor_copy(out=IDXu[gi % 2][:], in_=BB8[:]))
                vector.tensor_scalar(out=mg8[:], in0=eid[:], scalar1=float(g8),
                                     scalar2=None, op0=ALU.is_equal)
                vector.tensor_scalar(out=BB8[:], in0=eg8, scalar1=4096.0,
                                     scalar2=None, op0=ALU.mult)
                vector.drain()
                vector.tensor_tensor(out=AA8[:], in0=AA8[:], in1=BB8[:], op=ALU.add)
                vector.drain()
                vector.tensor_tensor(out=BB8[:], in0=AA8[:],
                                     in1=lane7[:].broadcast_to([128, 8]),
                                     op=ALU.subtract)
                vector.drain()
                vector.tensor_tensor(out=BB8[:], in0=BB8[:], in1=KEEP8[:], op=ALU.mult)
                vector.drain()
                vector.tensor_tensor(out=BB8[:], in0=BB8[:],
                                     in1=lane7[:].broadcast_to([128, 8]), op=ALU.add)
                vector.drain()
                vector.tensor_scalar(out=BB8[:], in0=BB8[:], scalar1=mg8[:],
                                     scalar2=None, op0=ALU.mult)
                vector.drain()
                if g8 == 0:
                    vector.tensor_copy(out=IDX2f[:, b * 8:(b + 1) * 8], in_=BB8[:])
                else:
                    vector.tensor_tensor(out=IDX2f[:, b * 8:(b + 1) * 8],
                                         in0=IDX2f[:, b * 8:(b + 1) * 8], in1=BB8[:],
                                         op=ALU.add)
                vector.tensor_scalar(out=AA8[:], in0=GAv[:, g8 * 8:(g8 + 1) * 8, b],
                                     scalar1=mg8[:], scalar2=None, op0=ALU.mult)
                vector.drain()
                if g8 == 0:
                    vector.tensor_copy(out=GAT2f[:, b * 8:(b + 1) * 8], in_=AA8[:])
                else:
                    vector.tensor_tensor(out=GAT2f[:, b * 8:(b + 1) * 8],
                                         in0=GAT2f[:, b * 8:(b + 1) * 8], in1=AA8[:],
                                         op=ALU.add)
                vector.drain()
            vector.tensor_copy(out=IDX2u[:], in_=IDX2f[:])
            vector.tensor_copy(out=GAT2[:], in_=GAT2f[:])
            vector.drain()
            for st in range(32):
                p = st % 2
                for dc in range(8):
                    vector.wait_ge(S["s_eitp"].h, 8 * st + dc + 1)
                    S["s_eitpe"].bump(vector.tensor_copy(
                        out=eifm[p][:, dc * 128:(dc + 1) * 128],
                        in_=(pT1 if dc % 2 == 0 else pT2)[:, 0:128]))
                for fc in range(32):
                    vector.wait_ge(S["s_htp"].h, 32 * st + fc + 1)
                    S["s_htpe"].bump(vector.tensor_copy(
                        out=hfmB[:, (fc % 16) * 128:(fc % 16 + 1) * 128],
                        in_=(pT1 if fc % 2 == 0 else pT2)[:, 0:128]))
                if st >= 2:
                    vector.wait_ge(S["s_eos"].h, 16 * (st - 1))
                for dh in range(2):
                    vector.wait_ge(S["s_eo"].h, 2 * st + dh + 1)
                    S["s_eoe"].bump(vector.tensor_copy(
                        out=EOT[p][:, dh * 512:(dh + 1) * 512],
                        in_=(pC if dh == 0 else pD)[:]))
            for i in range(NT):
                p = i % 2
                S["s_gz"].bump(vector.memset(GOT[p][:], 0.0))
                vector.wait_ge(S["s_ga"].h, 16 * (i + 1))
                vector.wait_ge(S["s_x2l"].h, 16 * (i + 1))
                if i >= 2:
                    vector.wait_ge(S["s_x2s"].h, 16 * (i - 1))
                S["s_x2c"].bump(vector.scalar_tensor_tensor(
                    out=X2T[p][:], in0=GOT[p][:], scalar=GAT2[:, i:i + 1],
                    in1=XT[p][:], op0=ALU.mult, op1=ALU.add))

        @block.tensor
        def _(tensor):
            tensor.wait_ge(S["s_pre"].h, 2)
            tensor.wait_ge(S["s_w"].h, 64)
            for i in range(NT):
                p = i % 2
                tensor.wait_ge(S["s_t1c"].h, i + 1)
                for dc in range(8):
                    cnt = 8 * i + dc
                    if cnt >= 2:
                        tensor.wait_ge(S["s_tpe"].h, cnt - 1)
                    S["s_tp"].bump(tensor.transpose(
                        out=(pA if dc % 2 == 0 else pB)[:, 0:128],
                        in_=T1[p][:, dc * 128:(dc + 1) * 128], identity=identf[:]))
                tensor.wait_ge(S["s_tpe"].h, 8 * (i + 1))
                if i >= 2:
                    tensor.wait_ge(S["s_lge"].h, i - 1)
                for dc in range(8):
                    mm = tensor.matmul(out=(pC if p == 0 else pD)[:, 0:8],
                                       lhsT=t1fm[p][:, dc * 128:(dc + 1) * 128],
                                       rhs=wgs[:, dc * 8:(dc + 1) * 8],
                                       start=(dc == 0), stop=(dc == 7))
                S["s_lg"].bump(mm)
                tensor.wait_ge(S["s_eg"].h, i + 1)
                if i >= 1:
                    tensor.wait_ge(S["s_ege"].h, i)
                S["s_egt"].bump(tensor.transpose(
                    out=pT1[0:2, 0:128], in_=EGsb[p][:], identity=identb[:]))
            for r in range(8):
                tensor.wait_ge(S["s_egl"].h, 16 * (r + 1))
                for tloc in range(8):
                    n_ = 8 * r + tloc
                    if n_ >= 2:
                        tensor.wait_ge(S["s_egxe"].h, n_ - 1)
                    S["s_egx"].bump(tensor.transpose(
                        out=(pT1 if tloc % 2 == 0 else pT2)[0:128, 0:8],
                        in_=EGL[:, tloc * 128:(tloc + 1) * 128],
                        identity=identb[0:8, 0:8]))
            for gi in range(32):
                b, g8 = gi // 8, gi % 8
                tensor.wait_ge(S["s_oh"].h, gi + 1)
                for tloc in range(8):
                    k = g8 * 8 + tloc
                    col = b * 64 + k
                    S["s_cnt"].bump(tensor.matmul(
                        out=pG[0:8, col:col + 1],
                        lhsT=OHall[:, gi * 64 + tloc * 8: gi * 64 + (tloc + 1) * 8],
                        rhs=ones128_1b[:], start=True, stop=True))
            tensor.wait_ge(S["s_cnte"].h, 1)
            for b in range(4):
                if b >= 1:
                    tensor.wait_ge(S["s_ctpe"].h, b)
                S["s_ctp"].bump(tensor.transpose(
                    out=pG[0:64, 256:264], in_=CNT[:, b * 64:(b + 1) * 64],
                    identity=identf[0:8, 0:8]))
                tensor.wait_ge(S["s_ctpe"].h, b + 1)
                if b >= 1:
                    tensor.wait_ge(S["s_offe"].h, b)
                S["s_off"].bump(tensor.matmul(
                    out=pG[0:64, 264:272], lhsT=lstrf[0:64, 0:64], rhs=Ctile[:],
                    start=True, stop=True))
            for gi in range(32):
                b, g8 = gi // 8, gi % 8
                ppos = pA if gi % 2 == 0 else pB
                tensor.wait_ge(S["s_offl"].h, 16 * (b + 1))
                if gi >= 2:
                    tensor.wait_ge(S["s_pose"].h, gi - 1)
                for tloc in range(8):
                    tensor.matmul(out=ppos[:, tloc * 8:(tloc + 1) * 8],
                                  lhsT=lstrb[:],
                                  rhs=OHall[:, gi * 64 + tloc * 8:
                                            gi * 64 + (tloc + 1) * 8],
                                  start=(tloc == 0), stop=False)
                mm = tensor.matmul(out=ppos[:, 0:64], lhsT=ones1_128f[:],
                                   rhs=OFFROW[0:1, g8 * 64:(g8 + 1) * 64],
                                   start=False, stop=True)
                S["s_pos"].bump(mm)
            for st in range(32):
                p = st % 2
                tensor.wait_ge(S["s_eild"].h, 16 * (st + 1))
                for dc in range(8):
                    n_ = 8 * st + dc
                    if n_ >= 2:
                        tensor.wait_ge(S["s_eitpe"].h, n_ - 1)
                    S["s_eitp"].bump(tensor.transpose(
                        out=(pT1 if dc % 2 == 0 else pT2)[:, 0:128],
                        in_=EIT[p][:, dc * 128:(dc + 1) * 128], identity=identb[:]))
                tensor.wait_ge(S["s_eitpe"].h, 8 * (st + 1))
                if st >= 1:
                    tensor.wait_ge(S["s_eoe"].h, 2 * st)
                for half in range(2):
                    for hcl in range(4):
                        hc = half * 4 + hcl
                        n_ = 8 * st + hc
                        if n_ >= 2:
                            tensor.wait_ge(S["s_hg"].h, n_ - 1)
                        if half == 1 and hcl == 0:
                            pass
                        for dc in range(8):
                            mm = tensor.matmul(
                                out=(pA if hc % 2 == 0 else pB)[:],
                                lhsT=eifm[p][:, dc * 128:(dc + 1) * 128],
                                rhs=w1s[:, dc * 4096 + hc * 512: dc * 4096 + (hc + 1) * 512],
                                start=(dc == 0), stop=(dc == 7))
                        S["s_h"].bump(mm)
                    tensor.wait_ge(S["s_hg"].h, 8 * st + 4 * (half + 1))
                    for fcl in range(16):
                        fc = half * 16 + fcl
                        n_ = 32 * st + fc
                        if n_ >= 2:
                            tensor.wait_ge(S["s_htpe"].h, n_ - 1)
                        S["s_htp"].bump(tensor.transpose(
                            out=(pT1 if fc % 2 == 0 else pT2)[:, 0:128],
                            in_=HBt[:, fcl * 128:(fcl + 1) * 128], identity=identb[:]))
                    tensor.wait_ge(S["s_htpe"].h, 32 * st + 16 * (half + 1))
                    for dh in range(2):
                        for fcl in range(16):
                            fc = half * 16 + fcl
                            mm = tensor.matmul(
                                out=(pC if dh == 0 else pD)[:],
                                lhsT=hfmB[:, fcl * 128:(fcl + 1) * 128],
                                rhs=w2s[:, fc * 1024 + dh * 512: fc * 1024 + (dh + 1) * 512],
                                start=(half == 0 and fcl == 0),
                                stop=(half == 1 and fcl == 15))
                        if half == 1:
                            S["s_eo"].bump(mm)

    return nc


def run_stage_b(inputs, x1_full):
    x1 = np.ascontiguousarray(x1_full.reshape(NB, NSEQ, D))
    in_maps = []
    for j in range(NCORE):
        xs = np.ascontiguousarray(x1[:, j * NLOC:(j + 1) * NLOC, :]).reshape(ROWS, D)
        in_maps.append({"x1": xs,
                        "eid": np.full((128, 1), float(j), np.float32),
                        "wg": np.ascontiguousarray(inputs["L0_ffn_Wg"]),
                        "w1": np.ascontiguousarray(inputs["L0_ffn_W1"][j]),
                        "w2": np.ascontiguousarray(inputs["L0_ffn_W2"][j])})
    nc = build_stage_b()
    res = run_bass_kernel_spmd(nc, in_maps, core_ids=list(range(NCORE)))
    x2 = np.zeros((NB, NSEQ, D), np.float32)
    for j in range(NCORE):
        x2[:, j * NLOC:(j + 1) * NLOC, :] = res.results[j]["x2"].reshape(NB, NLOC, D)
    return x2



def build_stage_c():
    nc = bass.Bass("TRN2", target_bir_lowering=False)
    x2_h = nc.dram_tensor("x2", [ROWS, D], F32, kind="ExternalInput").ap()
    wts_h = {}
    for w in ["wr", "ww", "wk", "wv", "wa", "wg", "wo", "wf"]:
        wts_h[w] = nc.dram_tensor(w, [D, D], F32, kind="ExternalInput").ap()
    vec_h = {}
    for vname in ["mu_r", "mu_w", "mu_k", "mu_v", "mu_a", "mu_g", "w0", "a0",
                  "k_k", "k_a", "r_k", "lnx_w", "lnx_b", "bf", "norm_w",
                  "norm_b", "lnf_w", "lnf_b"]:
        vec_h[vname] = nc.dram_tensor(vname, [D], F32, kind="ExternalInput").ap()
    muarr_h = nc.dram_tensor("muarr", [128, 48], F32, kind="ExternalInput").ap()
    x3_h = nc.dram_tensor("x3", [ROWS, D], F32, kind="ExternalOutput").ap()
    psc = nc.dram_tensor("psc", [6 * ROWS, D], BF16).ap()
    att_sc = nc.dram_tensor("att_sc", [ROWS, D], BF16).ap()

    es = ExitStack()
    t = lambda name, shape, dt: es.enter_context(nc.sbuf_tensor(name, shape, dt))
    WS = [t(f"WSc{i}", [128, 8192], BF16) for i in range(2)]
    ufm4 = [t(f"ufm{i}", [128, 8192], BF16) for i in range(4)]
    PB = [t(f"PBc{i}", [128, 1024], F32) for i in range(7)]
    mu6 = t("mu6", [128, 48], F32)
    om6 = t("om6", [128, 48], F32)
    prow = t("prowC", [1, 512], F32)
    XCH = t("XCHc", [128, 8192], BF16)
    XT2 = [t(f"XT2c{i}", [128, 1024], F32) for i in range(2)]
    UT = [t(f"UTc{i}", [128, 1024], F32) for i in range(2)]
    UB = [t(f"UBc{i}", [128, 1024], BF16) for i in range(2)]
    PJT = [t(f"PJTc{i}", [128, 1024], BF16) for i in range(2)]
    WZ = t("WZc", [128, 512], F32)
    scr = t("scrC", [128, 1024], F32)
    scr2 = t("scr2C", [128, 1024], F32)
    stat = t("statC", [128, 4], F32)
    _v = lambda tn, a, b: tn[:][:, a:b]
    _vf = lambda tn, a, b: tn[:][:, a:b].bitcast(F32)
    KKN = type("A", (), {})()
    class _W:
        def __init__(self, ap): self._ap = ap
        def __getitem__(self, key): return self._ap if key == slice(None) else self._ap[key]
    KKN = _W(_vf(ufm4[0], 0, 2048))
    KP = _W(_vf(ufm4[0], 2048, 4096))
    KKA = _W(_vf(ufm4[0], 4096, 6144))
    Y = _W(_vf(ufm4[0], 6144, 8192))
    P3 = [_W(_vf(ufm4[1], 0, 2048)), _W(_vf(ufm4[1], 2048, 4096)),
          _W(_vf(ufm4[1], 4096, 6144))]
    ATTb = _W(_v(ufm4[1], 6144, 7168))
    Rb = [_W(_v(ufm4[2], 0, 1024)), _W(_v(ufm4[2], 1024, 2048))]
    Wb = [_W(_v(ufm4[2], 2048, 3072)), _W(_v(ufm4[2], 3072, 4096))]
    Kb = [_W(_v(ufm4[2], 4096, 5120)), _W(_v(ufm4[2], 5120, 6144))]
    Ab = [_W(_v(ufm4[2], 6144, 7168)), _W(_v(ufm4[2], 7168, 8192))]
    Gb = [_W(_v(ufm4[3], 0, 1024)), _W(_v(ufm4[3], 1024, 2048))]
    VT = [_W(_v(ufm4[3], 2048 + 1024 * k, 3072 + 1024 * k)) for k in range(4)]
    YGb = _W(_v(ufm4[3], 6144, 7168))
    ygfm = _W(_v(ufm4[3], 7168, 8192))
    H1b = _W(_v(XCH, 2048, 3072))
    h1fm = _W(_v(XCH, 3072, 4096))
    ATTl = [_W(_v(XCH, 4096, 5120)), _W(_v(XCH, 5120, 6144))]
    ST = t("STc", [128, 64], F32)
    CT = t("CTc", [128, 64], F32)
    X3T = [t(f"X3Tc{i}", [128, 1024], F32) for i in range(2)]
    idm = t("idmC", [128, 128], F32)
    identb = t("identbC", [128, 128], BF16)
    ones1_128f = t("o1_128fC", [1, 128], F32)
    cb_eps = t("cb_epsC", [128, 1], F32)
    cb_gn = t("cb_gnC", [128, 1], F32)

    pA = es.enter_context(nc.psum_tensor("pAc", [128, 512], F32))
    pB = es.enter_context(nc.psum_tensor("pBc", [128, 512], F32))
    pC = es.enter_context(nc.psum_tensor("pCc", [128, 512], F32))
    pD = es.enter_context(nc.psum_tensor("pDc", [128, 512], F32))
    pT1 = es.enter_context(nc.psum_tensor("pT1c", [128, 512], BF16))
    pT2 = es.enter_context(nc.psum_tensor("pT2c", [128, 512], BF16))
    pP = es.enter_context(nc.psum_tensor("pPc", [128, 512], F32))

    S = {}
    for name in ["c_mu", "c_pr", "c_pb", "c_pbe", "c_pre", "c_ld", "c_c", "c_r",
                 "c_ucast", "c_ut", "c_ute", "c_ws", "c_xch", "c_pj", "c_pje",
                 "c_pjs", "c_wz", "c_wr", "c_sld", "c_kst", "c_ksr", "c_gn1",
                 "c_gn2", "c_yg", "c_ygt", "c_ygte", "c_wo", "c_att", "c_atts",
                 "c_wol", "c_x2l4", "c_attl", "c_h1c", "c_h1t", "c_h1te",
                 "c_wf", "c_wfe", "c_x3c", "c_x3s"]:
        S[name] = Sem(es, nc, name)

    PBUILD1 = [(0, "w0"), (1, "a0"), (2, "k_k"), (3, "k_a"), (4, "r_k"),
               (5, "lnx_w"), (6, "lnx_b")]
    PBUILD2 = [(0, "bf"), (1, "norm_w"), (2, "norm_b"), (3, "lnf_w"), (4, "lnf_b")]
    CHI = ["wr", "ww", "wk", "wv", "wa", "wg"]

    with es:
        block = es.enter_context(nc.Block())

        @block.sync
        def _(sync):
            S["c_mu"].bump(sync.dma_start(out=mu6[:], in_=muarr_h[:]), 16)
            for bi, (slot, pn) in enumerate(PBUILD1):
                for hf in range(2):
                    sync.wait_ge(S["c_pb"].h, 2 * bi + hf)
                    S["c_pr"].bump(sync.dma_start(
                        out=prow[:], in_=vec_h[pn][hf * 512:(hf + 1) * 512]
                        .rearrange("(a f) -> a f", a=1)), 16)
            for i in range(32):
                p = i % 2
                if i >= 2:
                    sync.wait_ge(S["c_ucast"].h, i - 1)
                S["c_ld"].bump(sync.dma_start(
                    out=XT2[p][:], in_=x2_h[128 * i:128 * (i + 1), :]), 16)
            for ci in range(6):
                for tt_ in range(4):
                    for nt in range(8):
                        for hf in range(2):
                            n_ = ((ci * 4 + tt_) * 8 + nt) * 2 + hf
                            sync.wait_ge(S["c_pje"].h, n_ + 1)
                            row = ci * ROWS + tt_ * 1024 + nt * 128
                            S["c_pjs"].bump(sync.dma_start(
                                out=psc[row:row + 128, hf * 512:(hf + 1) * 512],
                                in_=PJT[n_ % 2][:, hf * 512:(hf + 1) * 512]), 16)
            for i in range(8):
                for tt_ in range(4):
                    it = i * 4 + tt_
                    pp = tt_ % 2
                    if it >= 1:
                        sync.wait_ge(S["c_yg"].h, it)
                    for ci, dst in [(0, Rb[pp]), (1, Wb[pp]), (2, Kb[pp]),
                                    (4, Ab[pp]), (5, Gb[pp])]:
                        S["c_sld"].bump(sync.dma_start(
                            out=dst[:],
                            in_=psc[ci * ROWS + tt_ * 1024 + i * 128:
                                    ci * ROWS + tt_ * 1024 + i * 128 + 128, :]), 16)
                    S["c_sld"].bump(sync.dma_start(
                        out=VT[tt_][:],
                        in_=psc[3 * ROWS + tt_ * 1024 + i * 128:
                                3 * ROWS + tt_ * 1024 + i * 128 + 128, :]), 16)
                    sync.wait_ge(S["c_att"].h, it + 1)
                    S["c_atts"].bump(sync.dma_start(
                        out=att_sc[tt_ * 1024 + i * 128: tt_ * 1024 + i * 128 + 128, :],
                        in_=ATTb[:]), 16)
            for bi, (slot, pn) in enumerate(PBUILD2):
                for hf in range(2):
                    sync.wait_ge(S["c_pb"].h, 14 + 2 * bi + hf)
                    S["c_pr"].bump(sync.dma_start(
                        out=prow[:], in_=vec_h[pn][hf * 512:(hf + 1) * 512]
                        .rearrange("(a f) -> a f", a=1)), 16)
            for i in range(32):
                p = i % 2
                if i >= 2:
                    sync.wait_ge(S["c_x3c"].h, i - 1)
                S["c_x2l4"].bump(sync.dma_start(
                    out=XT2[p][:], in_=x2_h[128 * i:128 * (i + 1), :]), 16)
                S["c_attl"].bump(sync.dma_start(
                    out=ATTl[p][:], in_=att_sc[128 * i:128 * (i + 1), :]), 16)
                sync.wait_ge(S["c_x3c"].h, i + 1)
                S["c_x3s"].bump(sync.dma_start(
                    out=x3_h[128 * i:128 * (i + 1), :], in_=X3T[p][:]), 16)
            sync.wait_ge(S["c_x3s"].h, 16 * 32)

        @block.gpsimd
        def _(gpsimd):
            gpsimd.iota(out=idm[:], pattern=[[1, 128]], base=0, channel_multiplier=-1,
                        allow_small_or_imprecise_dtypes=True)
            S["c_pre"].bump(gpsimd.engine_nop())
            for ci, wname in enumerate(CHI):
                if ci >= 2:
                    gpsimd.wait_ge(S["c_pj"].h, (ci - 1) * 64)
                S["c_ws"].bump(gpsimd.dma_start(
                    out=WS[ci % 2][:].rearrange("p (dc c) -> p dc c", dc=8),
                    in_=wts_h[wname].rearrange("(dc p) c -> p dc c", p=128)), 16)
            gpsimd.wait_ge(S["c_pj"].h, 6 * 64)
            S["c_wol"].bump(gpsimd.dma_start(
                out=WS[0][:].rearrange("p (dc c) -> p dc c", dc=8),
                in_=wts_h["wo"].rearrange("(dc p) c -> p dc c", p=128)), 16)
            S["c_wol"].bump(gpsimd.dma_start(
                out=WS[1][:].rearrange("p (dc c) -> p dc c", dc=8),
                in_=wts_h["wf"].rearrange("(dc p) c -> p dc c", p=128)), 16)

        @block.tensor
        def _(tensor):
            tensor.wait_ge(S["c_pre"].h, 2)
            for bi in range(len(PBUILD1)):
                for hf in range(2):
                    n_ = 2 * bi + hf
                    tensor.wait_ge(S["c_pr"].h, 16 * (n_ + 1))
                    if n_ >= 1:
                        tensor.wait_ge(S["c_pbe"].h, n_)
                    S["c_pb"].bump(tensor.matmul(
                        out=pP[:], lhsT=ones1_128f[:], rhs=prow[:],
                        start=True, stop=True))
            for i in range(32):
                p = i % 2
                tensor.wait_ge(S["c_ucast"].h, i + 1)
                for dc in range(8):
                    n_ = 8 * i + dc
                    if n_ >= 2:
                        tensor.wait_ge(S["c_ute"].h, n_ - 1)
                    S["c_ut"].bump(tensor.transpose(
                        out=(pT1 if dc % 2 == 0 else pT2)[:, 0:128],
                        in_=UB[p][:, dc * 128:(dc + 1) * 128], identity=identb[:]))
            for ci in range(6):
                tensor.wait_ge(S["c_ws"].h, 16 * (ci + 1))
                for tt_ in range(4):
                    tensor.wait_ge(S["c_xch"].h, 4 * ci + tt_ + 1)
                    for nt in range(8):
                        for hf in range(2):
                            n_ = ((ci * 4 + tt_) * 8 + nt) * 2 + hf
                            if n_ >= 2:
                                tensor.wait_ge(S["c_pje"].h, n_ - 1)
                            for dc in range(8):
                                mm = tensor.matmul(
                                    out=(pA if n_ % 2 == 0 else pB)[:],
                                    lhsT=XCH[:, dc * 1024 + nt * 128:
                                             dc * 1024 + (nt + 1) * 128],
                                    rhs=WS[ci % 2][:, dc * 1024 + hf * 512:
                                                   dc * 1024 + (hf + 1) * 512],
                                    start=(dc == 0), stop=(dc == 7))
                            S["c_pj"].bump(mm)
            tensor.wait_ge(S["c_wol"].h, 32)
            for i in range(8):
                for tt_ in range(4):
                    it = i * 4 + tt_
                    tensor.wait_ge(S["c_yg"].h, it + 1)
                    for dc in range(8):
                        n_ = 8 * it + dc
                        if n_ >= 2:
                            tensor.wait_ge(S["c_ygte"].h, n_ - 1)
                        S["c_ygt"].bump(tensor.transpose(
                            out=(pT1 if dc % 2 == 0 else pT2)[:, 0:128],
                            in_=YGb[:, dc * 128:(dc + 1) * 128], identity=identb[:]))
                    tensor.wait_ge(S["c_ygte"].h, 8 * (it + 1))
                    if it >= 1:
                        tensor.wait_ge(S["c_att"].h, it)
                    for hf in range(2):
                        for dc in range(8):
                            mm = tensor.matmul(
                                out=(pC if hf == 0 else pD)[:],
                                lhsT=ygfm[:, dc * 128:(dc + 1) * 128],
                                rhs=WS[0][:, dc * 1024 + hf * 512:
                                          dc * 1024 + (hf + 1) * 512],
                                start=(dc == 0), stop=(dc == 7))
                        S["c_wo"].bump(mm)
            for bi in range(len(PBUILD2)):
                for hf in range(2):
                    n_ = 14 + 2 * bi + hf
                    tensor.wait_ge(S["c_pr"].h, 16 * (n_ + 1))
                    tensor.wait_ge(S["c_pbe"].h, n_)
                    S["c_pb"].bump(tensor.matmul(
                        out=pP[:], lhsT=ones1_128f[:], rhs=prow[:],
                        start=True, stop=True))
            for i in range(32):
                tensor.wait_ge(S["c_h1c"].h, i + 1)
                for dc in range(8):
                    n_ = 8 * i + dc
                    if n_ >= 2:
                        tensor.wait_ge(S["c_h1te"].h, n_ - 1)
                    S["c_h1t"].bump(tensor.transpose(
                        out=(pT1 if dc % 2 == 0 else pT2)[:, 0:128],
                        in_=H1b[:, dc * 128:(dc + 1) * 128], identity=identb[:]))
                tensor.wait_ge(S["c_h1te"].h, 8 * (i + 1))
                if i >= 1:
                    tensor.wait_ge(S["c_wfe"].h, 2 * i)
                for hf in range(2):
                    for dc in range(8):
                        mm = tensor.matmul(
                            out=(pC if hf == 0 else pD)[:],
                            lhsT=h1fm[:, dc * 128:(dc + 1) * 128],
                            rhs=WS[1][:, dc * 1024 + hf * 512:
                                      dc * 1024 + (hf + 1) * 512],
                            start=(dc == 0), stop=(dc == 7))
                    S["c_wf"].bump(mm)

        @block.scalar
        def _(scalar):
            def ln_scalar(k):
                scalar.wait_ge(S["c_c"].h, k + 1)
                scalar.activation(out=XCH[:, 0:1024], in_=scr[:], func=AF.Square,
                                  accum_out=stat[:, 2:3])
                scalar.drain()
                scalar.activation(out=stat[:, 3:4], in_=stat[:, 2:3], func=AF.Ln,
                                  scale=1.0 / 1024, bias=cb_eps[:])
                scalar.drain()
                S["c_r"].bump(scalar.activation(out=stat[:, 3:4], in_=stat[:, 3:4],
                                                func=AF.Exp, scale=-0.5))
            for i in range(32):
                ln_scalar(i)
            nwz = 0
            for ci in range(6):
                if ci not in (1, 4, 5):
                    continue
                for tt_ in range(4):
                    for nt in range(8):
                        for hf in range(2):
                            n_ = ((ci * 4 + tt_) * 8 + nt) * 2 + hf
                            if ci == 1:
                                nwz += 1
                                scalar.wait_ge(S["c_wz"].h, nwz)
                                if n_ >= 2:
                                    scalar.wait_ge(S["c_pjs"].h, 16 * (n_ - 1))
                                scalar.activation(out=WZ[:], in_=WZ[:], func=AF.Sigmoid)
                                scalar.drain()
                                S["c_pje"].bump(scalar.activation(
                                    out=PJT[n_ % 2][:, hf * 512:(hf + 1) * 512],
                                    in_=WZ[:], func=AF.Exp, scale=-0.60653065971))
                            elif ci == 4:
                                nwz += 1
                                scalar.wait_ge(S["c_wz"].h, nwz)
                                if n_ >= 2:
                                    scalar.wait_ge(S["c_pjs"].h, 16 * (n_ - 1))
                                S["c_pje"].bump(scalar.activation(
                                    out=PJT[n_ % 2][:, hf * 512:(hf + 1) * 512],
                                    in_=WZ[:], func=AF.Sigmoid))
                            else:
                                scalar.wait_ge(S["c_pj"].h,
                                               ci * 64 + tt_ * 16 + nt * 2 + hf + 1)
                                if n_ >= 2:
                                    scalar.wait_ge(S["c_pjs"].h, 16 * (n_ - 1))
                                S["c_pje"].bump(scalar.activation(
                                    out=PJT[n_ % 2][:, hf * 512:(hf + 1) * 512],
                                    in_=(pA if n_ % 2 == 0 else pB)[:], func=AF.Sigmoid))
            for it in range(32):
                scalar.wait_ge(S["c_kst"].h, it + 1)
                S["c_ksr"].bump(scalar.activation(out=ST[:, 16:32], in_=ST[:, 0:16],
                                                  func=AF.Sqrt))
                scalar.wait_ge(S["c_gn1"].h, it + 1)
                scalar.activation(out=ST[:, 32:48], in_=ST[:, 16:32], func=AF.Ln,
                                  scale=1.0 / 64, bias=cb_gn[:])
                scalar.drain()
                S["c_gn2"].bump(scalar.activation(out=ST[:, 32:48], in_=ST[:, 32:48],
                                                  func=AF.Exp, scale=-0.5))
            for i in range(32):
                ln_scalar(32 + 3 * i)
                ln_scalar(33 + 3 * i)
                scalar.wait_ge(S["c_wz"].h, 128 + i + 1)
                scalar.activation(out=scr2[:, 0:512], in_=scr[:, 0:512], func=AF.Silu)
                S["c_wr"].bump(scalar.activation(out=scr2[:, 512:1024],
                                                 in_=scr[:, 512:1024], func=AF.Silu))
                ln_scalar(34 + 3 * i)

        @block.vector
        def _(vector):
            vector.wait_ge(S["c_pre"].h, 1)
            vector.tensor_scalar(out=identb[:], in0=idm[:], scalar1=0.0, scalar2=None,
                                 op0=ALU.is_equal)
            vector.memset(ones1_128f[:], 1.0)
            vector.memset(cb_eps[:], LNEPS)
            vector.memset(cb_gn[:], 0.00064)
            vector.wait_ge(S["c_mu"].h, 16)
            vector.tensor_scalar(out=om6[:], in0=mu6[:], scalar1=-1.0, scalar2=1.0,
                                 op0=ALU.mult, op1=ALU.add)
            S["c_pre"].bump(vector.engine_nop())
            for bi, (slot, pn) in enumerate(PBUILD1):
                for hf in range(2):
                    n_ = 2 * bi + hf
                    vector.wait_ge(S["c_pb"].h, n_ + 1)
                    S["c_pbe"].bump(vector.tensor_copy(
                        out=PB[slot][:, hf * 512:(hf + 1) * 512], in_=pP[:]))

            def ln_event(xin_ap, out_ap, scale_w=None, bias_w=None):
                vector.tensor_reduce(out=stat[:, 0:1], in_=xin_ap, axis=AX.X, op=ALU.add)
                vector.drain()
                vector.tensor_scalar(out=stat[:, 1:2], in0=stat[:, 0:1],
                                     scalar1=1.0 / 1024, scalar2=None, op0=ALU.mult)
                vector.drain()
                v = S["c_c"].bump(vector.tensor_scalar(
                    out=scr[:], in0=xin_ap, scalar1=stat[:, 1:2], scalar2=None,
                    op0=ALU.subtract))
                vector.wait_ge(S["c_r"].h, v)
                vector.tensor_scalar(out=out_ap, in0=scr[:], scalar1=stat[:, 3:4],
                                     scalar2=None, op0=ALU.mult)
                if scale_w is not None:
                    vector.tensor_tensor(out=out_ap, in0=out_ap, in1=scale_w, op=ALU.mult)
                    vector.tensor_tensor(out=out_ap, in0=out_ap, in1=bias_w, op=ALU.add)

            for i in range(32):
                p = i % 2
                vector.wait_ge(S["c_ld"].h, 16 * (i + 1))
                ln_event(XT2[p][:], UT[p][:])
                S["c_ucast"].bump(vector.tensor_copy(out=UB[p][:], in_=UT[p][:]))
                tti, nt = i // 8, i % 8
                for dc in range(8):
                    vector.wait_ge(S["c_ut"].h, 8 * i + dc + 1)
                    S["c_ute"].bump(vector.tensor_copy(
                        out=ufm4[tti][:, dc * 1024 + nt * 128:
                                       dc * 1024 + (nt + 1) * 128],
                        in_=(pT1 if dc % 2 == 0 else pT2)[:, 0:128]))
            for ci in range(6):
                for tt_ in range(4):
                    if ci * 4 + tt_ >= 1:
                        vector.wait_ge(S["c_pj"].h, (ci * 4 + tt_) * 16)
                    for dc in range(8):
                        mcol = ci * 8 + dc
                        if tt_ == 0:
                            vector.tensor_scalar(
                                out=XCH[:, dc * 1024:(dc + 1) * 1024],
                                in0=ufm4[0][:, dc * 1024:(dc + 1) * 1024],
                                scalar1=om6[:, mcol:mcol + 1], scalar2=None,
                                op0=ALU.mult)
                        else:
                            vector.tensor_scalar(
                                out=scr[:].bitcast(BF16)[:, 0:1024],
                                in0=ufm4[tt_ - 1][:, dc * 1024:(dc + 1) * 1024],
                                scalar1=mu6[:, mcol:mcol + 1], scalar2=None,
                                op0=ALU.mult)
                            vector.scalar_tensor_tensor(
                                out=XCH[:, dc * 1024:(dc + 1) * 1024],
                                in0=ufm4[tt_][:, dc * 1024:(dc + 1) * 1024],
                                scalar=om6[:, mcol:mcol + 1],
                                in1=scr[:].bitcast(BF16)[:, 0:1024],
                                op0=ALU.mult, op1=ALU.add)
                    S["c_xch"].bump(vector.engine_nop())
                    for nt in range(8):
                        for hf in range(2):
                            n_ = ((ci * 4 + tt_) * 8 + nt) * 2 + hf
                            ps = (pA if n_ % 2 == 0 else pB)
                            pjv = ci * 64 + tt_ * 16 + nt * 2 + hf + 1
                            if ci in (0, 2, 3):
                                vector.wait_ge(S["c_pj"].h, pjv)
                                if n_ >= 2:
                                    vector.wait_ge(S["c_pjs"].h, 16 * (n_ - 1))
                                S["c_pje"].bump(vector.tensor_copy(
                                    out=PJT[n_ % 2][:, hf * 512:(hf + 1) * 512],
                                    in_=ps[:]))
                            elif ci in (1, 4):
                                vector.wait_ge(S["c_pj"].h, pjv)
                                vector.wait_ge(S["c_pje"].h, n_)
                                S["c_wz"].bump(vector.tensor_tensor(
                                    out=WZ[:], in0=ps[:],
                                    in1=PB[0 if ci == 1 else 1][:, hf * 512:(hf + 1) * 512],
                                    op=ALU.add))
            b3 = lambda ap16: ap16.unsqueeze(2).broadcast_to([128, 16, 64])
            r3 = lambda ap: ap.rearrange("p (h w) -> p h w", h=16)
            nsld = 0
            for i in range(8):
                for tt_ in range(4):
                    it = i * 4 + tt_
                    pp = tt_ % 2
                    nsld += 6
                    vector.wait_ge(S["c_sld"].h, 16 * nsld)
                    vector.tensor_tensor(out=scr[:], in0=Kb[pp][:], in1=PB[2][:],
                                         op=ALU.mult)
                    vector.tensor_tensor(out=scr2[:], in0=scr[:], in1=scr[:], op=ALU.mult)
                    S["c_kst"].bump(vector.tensor_reduce(
                        out=ST[:, 0:16], in_=r3(scr2[:]), axis=AX.X, op=ALU.add))
                    vector.wait_ge(S["c_ksr"].h, it + 1)
                    vector.tensor_scalar(out=ST[:, 16:32], in0=ST[:, 16:32],
                                         scalar1=1e-12, scalar2=None, op0=ALU.max)
                    vector.drain()
                    vector.reciprocal(out=ST[:, 48:64], in_=ST[:, 16:32])
                    vector.drain()
                    vector.tensor_tensor(out=r3(KKN[:]), in0=r3(scr[:]),
                                         in1=b3(ST[:, 48:64]), op=ALU.mult)
                    vector.tensor_scalar(out=scr[:], in0=Ab[pp][:], scalar1=1.0,
                                         scalar2=None, op0=ALU.subtract)
                    vector.tensor_tensor(out=scr[:], in0=scr[:], in1=PB[3][:], op=ALU.mult)
                    vector.tensor_tensor(out=scr[:], in0=Kb[pp][:], in1=scr[:], op=ALU.mult)
                    vector.tensor_tensor(out=KP[:], in0=Kb[pp][:], in1=scr[:], op=ALU.add)
                    vector.tensor_tensor(out=KKA[:], in0=KKN[:], in1=Ab[pp][:], op=ALU.mult)
                    for tau in range(tt_):
                        vector.tensor_tensor(out=scr[:], in0=KKN[:], in1=P3[tau][:],
                                             op=ALU.mult)
                        vector.tensor_reduce(out=ST[:, 0:16], in_=r3(scr[:]), axis=AX.X,
                                             op=ALU.add)
                        vector.drain()
                        vector.tensor_tensor(out=P3[tau][:], in0=P3[tau][:],
                                             in1=Wb[pp][:], op=ALU.mult)
                        vector.tensor_tensor(out=r3(scr[:]), in0=r3(KKA[:]),
                                             in1=b3(ST[:, 0:16]), op=ALU.mult)
                        vector.tensor_tensor(out=P3[tau][:], in0=P3[tau][:], in1=scr[:],
                                             op=ALU.subtract)
                    for tau in range(tt_):
                        vector.tensor_tensor(out=scr[:], in0=P3[tau][:], in1=Rb[pp][:],
                                             op=ALU.mult)
                        vector.tensor_reduce(out=CT[:, tau * 16:(tau + 1) * 16],
                                             in_=r3(scr[:]), axis=AX.X, op=ALU.add)
                    vector.tensor_tensor(out=scr[:], in0=KP[:], in1=Rb[pp][:], op=ALU.mult)
                    vector.tensor_reduce(out=CT[:, tt_ * 16:(tt_ + 1) * 16],
                                         in_=r3(scr[:]), axis=AX.X, op=ALU.add)
                    vector.drain()
                    if tt_ < 3:
                        vector.tensor_copy(out=P3[tt_][:], in_=KP[:])
                    vector.tensor_tensor(out=r3(Y[:]),
                                         in0=b3(CT[:, tt_ * 16:(tt_ + 1) * 16]),
                                         in1=r3(VT[tt_][:]), op=ALU.mult)
                    for tau in range(tt_):
                        vector.tensor_tensor(out=r3(scr[:]),
                                             in0=b3(CT[:, tau * 16:(tau + 1) * 16]),
                                             in1=r3(VT[tau][:]), op=ALU.mult)
                        vector.tensor_tensor(out=Y[:], in0=Y[:], in1=scr[:], op=ALU.add)
                    vector.tensor_reduce(out=ST[:, 0:16], in_=r3(Y[:]), axis=AX.X,
                                         op=ALU.add)
                    vector.drain()
                    vector.tensor_scalar(out=ST[:, 0:16], in0=ST[:, 0:16],
                                         scalar1=1.0 / 64, scalar2=None, op0=ALU.mult)
                    vector.drain()
                    vector.tensor_tensor(out=r3(scr[:]), in0=r3(Y[:]),
                                         in1=b3(ST[:, 0:16]), op=ALU.subtract)
                    vector.tensor_tensor(out=scr2[:], in0=scr[:], in1=scr[:], op=ALU.mult)
                    S["c_gn1"].bump(vector.tensor_reduce(
                        out=ST[:, 16:32], in_=r3(scr2[:]), axis=AX.X, op=ALU.add))
                    vector.wait_ge(S["c_gn2"].h, it + 1)
                    vector.tensor_tensor(out=r3(scr2[:]), in0=b3(ST[:, 32:48]),
                                         in1=r3(PB[5][:]), op=ALU.mult)
                    vector.drain()
                    vector.tensor_tensor(out=Y[:], in0=scr[:], in1=scr2[:], op=ALU.mult)
                    vector.tensor_tensor(out=Y[:], in0=Y[:], in1=PB[6][:], op=ALU.add)
                    vector.tensor_tensor(out=scr[:], in0=Rb[pp][:], in1=KP[:], op=ALU.mult)
                    vector.tensor_tensor(out=scr[:], in0=scr[:], in1=PB[4][:], op=ALU.mult)
                    vector.tensor_reduce(out=ST[:, 0:16], in_=r3(scr[:]), axis=AX.X,
                                         op=ALU.add)
                    vector.drain()
                    vector.tensor_tensor(out=r3(scr[:]), in0=b3(ST[:, 0:16]),
                                         in1=r3(VT[tt_][:]), op=ALU.mult)
                    vector.tensor_tensor(out=Y[:], in0=Y[:], in1=scr[:], op=ALU.add)
                    S["c_yg"].bump(vector.tensor_tensor(out=YGb[:], in0=Y[:],
                                                        in1=Gb[pp][:], op=ALU.mult))
                    for dc in range(8):
                        vector.wait_ge(S["c_ygt"].h, 8 * it + dc + 1)
                        S["c_ygte"].bump(vector.tensor_copy(
                            out=ygfm[:, dc * 128:(dc + 1) * 128],
                            in_=(pT1 if dc % 2 == 0 else pT2)[:, 0:128]))
                    vector.wait_ge(S["c_wo"].h, 2 * (it + 1))
                    if it >= 1:
                        vector.wait_ge(S["c_atts"].h, 16 * it)
                    vector.tensor_copy(out=ATTb[:, 0:512], in_=pC[:])
                    S["c_att"].bump(vector.tensor_copy(out=ATTb[:, 512:1024], in_=pD[:]))
            for bi, (slot, pn) in enumerate(PBUILD2):
                for hf in range(2):
                    n_ = 14 + 2 * bi + hf
                    vector.wait_ge(S["c_pb"].h, n_ + 1)
                    S["c_pbe"].bump(vector.tensor_copy(
                        out=PB[slot][:, hf * 512:(hf + 1) * 512], in_=pP[:]))
            for i in range(32):
                p = i % 2
                vector.wait_ge(S["c_x2l4"].h, 16 * (i + 1))
                ln_event(XT2[p][:], UT[0][:])
                vector.wait_ge(S["c_attl"].h, 16 * (i + 1))
                vector.tensor_tensor(out=scr2[:], in0=UT[0][:], in1=ATTl[p][:],
                                     op=ALU.add)
                ln_event(scr2[:], UT[1][:], scale_w=PB[1][:], bias_w=PB[2][:])
                S["c_h1c"].bump(vector.tensor_copy(out=H1b[:], in_=UT[1][:]))
                for dc in range(8):
                    vector.wait_ge(S["c_h1t"].h, 8 * i + dc + 1)
                    S["c_h1te"].bump(vector.tensor_copy(
                        out=h1fm[:, dc * 128:(dc + 1) * 128],
                        in_=(pT1 if dc % 2 == 0 else pT2)[:, 0:128]))
                vector.wait_ge(S["c_wf"].h, 2 * (i + 1))
                vector.tensor_tensor(out=scr[:, 0:512], in0=pC[:], in1=PB[0][:, 0:512],
                                     op=ALU.add)
                S["c_wz"].bump(vector.tensor_tensor(
                    out=scr[:, 512:1024], in0=pD[:], in1=PB[0][:, 512:1024], op=ALU.add))
                S["c_wfe"].bump(vector.engine_nop(), 2)
                vector.wait_ge(S["c_wr"].h, i + 1)
                ln_event(scr2[:], UT[1][:], scale_w=PB[3][:], bias_w=PB[4][:])
                if i >= 2:
                    vector.wait_ge(S["c_x3s"].h, 16 * (i - 1))
                S["c_x3c"].bump(vector.tensor_tensor(
                    out=X3T[p][:], in0=XT2[p][:], in1=UT[1][:], op=ALU.add))

    return nc


def run_stage_c(inputs, x2_full):
    x2 = np.ascontiguousarray(x2_full.reshape(NB, NSEQ, D))
    wmap = {"wr": "L1_attn_Wr", "ww": "L1_attn_Ww", "wk": "L1_attn_Wk",
            "wv": "L1_attn_Wv", "wa": "L1_attn_Wa", "wg": "L1_attn_Wg",
            "wo": "L1_attn_Wo", "wf": "L1_attn_Wf"}
    vmap = {"mu_r": "L1_attn_mu_r", "mu_w": "L1_attn_mu_w", "mu_k": "L1_attn_mu_k",
            "mu_v": "L1_attn_mu_v", "mu_a": "L1_attn_mu_a", "mu_g": "L1_attn_mu_g",
            "w0": "L1_attn_w0", "a0": "L1_attn_a0", "k_k": "L1_attn_k_k",
            "k_a": "L1_attn_k_a", "r_k": "L1_attn_r_k", "lnx_w": "L1_attn_lnx_w",
            "lnx_b": "L1_attn_lnx_b", "bf": "L1_attn_bf", "norm_w": "L1_attn_norm_w",
            "norm_b": "L1_attn_norm_b", "lnf_w": "L1_attn_lnf_w",
            "lnf_b": "L1_attn_lnf_b"}
    common = {}
    mu = np.stack([inputs[f"L1_attn_mu_{m}"] for m in
                   ["r", "w", "k", "v", "a", "g"]])
    common["muarr"] = np.ascontiguousarray(
        mu.reshape(6, 8, 128).transpose(2, 0, 1).reshape(128, 48))
    for k, src in wmap.items():
        common[k] = np.ascontiguousarray(inputs[src])
    for k, src in vmap.items():
        common[k] = np.ascontiguousarray(inputs[src])
    in_maps = []
    for j in range(NCORE):
        m = dict(common)
        m["x2"] = np.ascontiguousarray(
            x2[:, j * NLOC:(j + 1) * NLOC, :]).reshape(ROWS, D)
        in_maps.append(m)
    nc = build_stage_c()
    res = run_bass_kernel_spmd(nc, in_maps, core_ids=list(range(NCORE)))
    x3 = np.zeros((NB, NSEQ, D), np.float32)
    for j in range(NCORE):
        x3[:, j * NLOC:(j + 1) * NLOC, :] = res.results[j]["x3"].reshape(NB, NLOC, D)
    return x3


RSR = 8 * 513

def build_stage_d():
    nc = bass.Bass("TRN2", target_bir_lowering=False)
    x3_h = nc.dram_tensor("x3", [ROWS, D], F32, kind="ExternalInput").ap()
    semb_h = nc.dram_tensor("semb", [1024, D], F32, kind="ExternalInput").ap()
    wg_h = nc.dram_tensor("wgd", [2 * D, 2048], F32, kind="ExternalInput").ap()
    wu_h = nc.dram_tensor("wud", [2 * D, 2048], F32, kind="ExternalInput").ap()
    wd_h = nc.dram_tensor("wdd", [2 * 2048, D], F32, kind="ExternalInput").ap()
    pb_h = nc.dram_tensor("pball", [128, 4096], F32, kind="ExternalInput").ap()
    bgb_h = nc.dram_tensor("bgball", [128, 4096], F32, kind="ExternalInput").ap()
    bub_h = nc.dram_tensor("buball", [128, 4096], F32, kind="ExternalInput").ap()
    bdb_h = nc.dram_tensor("bdball", [128, 2048], F32, kind="ExternalInput").ap()
    scal_h = nc.dram_tensor("sclb", [128, 1], F32, kind="ExternalInput").ap()
    x4_h = nc.dram_tensor("x4", [ROWS, D], F32, kind="ExternalOutput").ap()
    xnsc = nc.dram_tensor("xnsc", [ROWS, D], BF16).ap()
    lgsc = nc.dram_tensor("lgsc", [ROWS, D], BF16).ap()
    mx_in = nc.dram_tensor("mx_in", [32, 128], F32).ap()
    mx_out = nc.dram_tensor("mx_out", [32, 128], F32).ap()
    rs_in = nc.dram_tensor("rs_in", [RSR, D], BF16).ap()
    rs_out = nc.dram_tensor("rs_out", [513, D], BF16).ap()
    eosc = nc.dram_tensor("eosc", [512, D], BF16).ap()
    eog = nc.dram_tensor("eog", [4096, D], BF16, addr_space="Shared").ap()
    dsc = nc.dram_tensor("dsc", [32, 128], F32).ap()

    es = ExitStack()
    t = lambda name, shape, dt: es.enter_context(nc.sbuf_tensor(name, shape, dt))
    WGU = t("WGUd", [128, 8 * 2048], BF16)
    WUU = t("WUUd", [128, 8 * 2048], BF16)
    WDD = t("WDDd", [128, 16 * 1024], BF16)
    sefm = t("sefmd", [128, 8192], BF16)
    XF = t("XFd", [128, 8192], BF16)
    XNB8 = t("XNB8d", [128, 8192], BF16)
    PB4 = t("PB4d", [128, 4096], F32)
    BGB = t("BGBd", [128, 2048], BF16)
    BUB = t("BUBd", [128, 2048], BF16)
    BDB = t("BDBd", [128, 1024], BF16)
    scl = t("scld", [128, 1], F32)
    class _WD:
        def __init__(self, ap): self._ap = ap
        def __getitem__(self, key): return self._ap if key == slice(None) else self._ap[key]
    XT = [t(f"XTd{i}", [128, 1024], F32) for i in range(2)]
    SEt = XT
    XNc = [t(f"XNcd{i}", [128, 1024], BF16) for i in range(2)]
    LGR = [t(f"LGRd{i}", [128, 1024], BF16) for i in range(2)]
    scr = t("scrD", [128, 1024], F32)
    stat = t("statD", [128, 4], F32)
    MP = t("MPd", [128, 32], F32)
    MRrow = t("MRrowd", [1, 128], F32)
    PSL = t("PSLd", [128, 1024], BF16)
    DENT = t("DENTd", [128, 32], F32)
    DET = t("DETd", [32, 128], F32)
    DR4 = [t(f"DR4d{i}", [1, 128], F32) for i in range(4)]
    onescol = t("onescold", [128, 1], BF16)
    SLB = t("SLBd", [128, 1024], BF16)
    slfm = t("slfmd", [128, 4096], BF16)
    DRC = t("DRCd", [128, 4], F32)


    rsum = t("rsumd", [128, 2], F32)
    NMb = [t(f"NMbd{i}", [128, 1], F32) for i in range(2)]
    RSb = [t(f"RSbd{i}", [128, 1], F32) for i in range(2)]
    X4T = [_WD(PB4[:][:, 0:1024]), _WD(PB4[:][:, 1024:2048])]
    MB = _WD(sefm[:][:, 0:2048].bitcast(F32))
    GT = _WD(sefm[:][0:64, 2048:4096])
    GUB = _WD(sefm[:][0:64, 4096:6144])
    gufm = _WD(sefm[:][:, 6144:7168])
    EOB = _WD(sefm[:][:, 7168:8192])
    P2T = [_WD(slfm[:][:, 0:1024])]
    idm = t("idmD", [128, 128], F32)
    identb = t("identbD", [128, 128], BF16)
    identf = t("identfD", [128, 128], F32)
    ones1_128f = t("o1_128fD", [1, 128], F32)
    cb_eps = t("cb_epsD", [128, 1], F32)

    pA = es.enter_context(nc.psum_tensor("pAd", [128, 512], F32))
    pB = es.enter_context(nc.psum_tensor("pBd", [128, 512], F32))
    pC = es.enter_context(nc.psum_tensor("pCd", [128, 512], F32))
    pD = es.enter_context(nc.psum_tensor("pDd", [128, 512], F32))
    pT1 = es.enter_context(nc.psum_tensor("pT1d", [128, 512], BF16))
    pT2 = es.enter_context(nc.psum_tensor("pT2d", [128, 512], BF16))
    pE = es.enter_context(nc.psum_tensor("pEd", [128, 512], F32))

    S = {}
    for name in ["d_io", "d_pl", "d_vr", "d_w", "d_we", "d_sel", "d_c", "d_r", "d_sec",
                 "d_set", "d_sete", "d_x3l", "d_xnc", "d_xnt", "d_xnte",
                 "d_xns", "d_lg", "d_lge", "d_lgs", "d_lgt", "d_mx",
                 "d_mpt", "d_mpte", "d_mxs", "d_ar", "d_mrl", "d_mb", "d_mbe",
                 "d_lgl", "d_pex", "d_psub", "d_sxl", "d_sl", "d_sle",
                 "d_den", "d_dene", "d_dt", "d_dte", "d_pss", "d_dens",
                 "d_rs", "d_rsl", "d_drl", "d_drt", "d_drte", "d_slc", "d_slt",
                 "d_slte", "d_w2", "d_g", "d_ge", "d_gu", "d_gut", "d_gute",
                 "d_eo", "d_eoe", "d_eos", "d_ag", "d_lgl2", "d_cm", "d_cme",
                 "d_p2", "d_p2t", "d_p2te", "d_eofl", "d_nm", "d_x3l4", "d_x4c", "d_x4s"]:
        S[name] = Sem(es, nc, name)

    with es:
        block = es.enter_context(nc.Block())

        @block.sync
        def _(sync):
            S["d_pl"].bump(sync.dma_start(out=PB4[:], in_=pb_h[:]), 16)
            S["d_pl"].bump(sync.dma_start(out=scl[:], in_=scal_h[:]), 16)
            for st in range(8):
                p = st % 2
                if st >= 2:
                    sync.wait_ge(S["d_sec"].h, st - 1)
                S["d_sel"].bump(sync.dma_start(
                    out=SEt[p][:], in_=semb_h[128 * st:128 * (st + 1), :]), 16)
            sync.wait_ge(S["d_sec"].h, 8)
            for b in range(4):
                for nt in range(8):
                    i = b * 8 + nt
                    p = i % 2
                    if i >= 2:
                        sync.wait_ge(S["d_xnc"].h, i - 1)
                    S["d_x3l"].bump(sync.dma_start(
                        out=XT[p][:], in_=x3_h[128 * i:128 * (i + 1), :]), 16)
                    sync.wait_ge(S["d_xnc"].h, i + 1)
                    S["d_xns"].bump(sync.dma_start(
                        out=xnsc[128 * i:128 * (i + 1), :], in_=XNc[p][:]), 16)
                for nt in range(8):
                    i = b * 8 + nt
                    sync.wait_ge(S["d_lge"].h, 2 * (i + 1))
                    S["d_lgs"].bump(sync.dma_start(
                        out=lgsc[128 * i:128 * (i + 1), :], in_=LGR[i % 2][:]), 16)
            sync.wait_ge(S["d_mpte"].h, 1)
            S["d_mxs"].bump(sync.dma_start(out=mx_in[:], in_=DET[:]), 16)
            sync.wait_ge(S["d_lgs"].h, 16 * 32)
            sync.wait_ge(S["d_xns"].h, 16 * 32)
            sync.wait_ge(S["d_ar"].h, 1)
            for b in range(4):
                for er in range(8):
                    if b + er >= 1:
                        sync.wait_ge(S["d_mb"].h, b * 8 + er)
                    S["d_mrl"].bump(sync.dma_start(
                        out=MRrow[:], in_=mx_out[b * 8 + er: b * 8 + er + 1, :]), 16)
                for nt in range(8):
                    i = b * 8 + nt
                    p = nt % 2
                    if i >= 2:
                        sync.wait_ge(S["d_pex"].h, i - 1)
                    S["d_lgl"].bump(sync.dma_start(
                        out=LGR[p][:], in_=lgsc[128 * i:128 * (i + 1), :]), 16)
                    if nt == 0 and b >= 1:
                        sync.wait_ge(S["d_sl"].h, 8 * b)
                    S["d_sxl"].bump(sync.dma_start(
                        out=XNB8[:, nt * 1024:(nt + 1) * 1024],
                        in_=xnsc[128 * i:128 * (i + 1), :]), 16)
                for er in range(8):
                    sync.wait_ge(S["d_sle"].h, (b * 8 + er + 1) * 2)
                    S["d_pss"].bump(sync.dma_start(
                        out=rs_in[513 * er + b * 128: 513 * er + (b + 1) * 128, :],
                        in_=PSL[:]), 16)
            sync.wait_ge(S["d_rs"].h, 1)
            for b in range(4):
                S["d_rsl"].bump(sync.dma_start(
                    out=SLB[:], in_=rs_out[b * 128:(b + 1) * 128, :]), 16)
                sync.wait_ge(S["d_slt"].h, 8 * (b + 1))
            for e in range(2):
                for b in range(4):
                    eb = e * 4 + b
                    sync.wait_ge(S["d_eoe"].h, 2 * (eb + 1))
                    S["d_eos"].bump(sync.dma_start(
                        out=eosc[b * 128 + e * 64: b * 128 + e * 64 + 64, :],
                        in_=EOB[e * 64:(e + 1) * 64, :]), 16)
            sync.wait_ge(S["d_ag"].h, 1)
            for b in range(4):
                for er in range(8):
                    S["d_eofl"].bump(sync.dma_start(
                        out=XNB8[:, er * 1024:(er + 1) * 1024],
                        in_=eog[er * 512 + b * 128: er * 512 + (b + 1) * 128, :]), 16)
                for nt in range(8):
                    i = b * 8 + nt
                    p = nt % 2
                    if nt >= 2:
                        sync.wait_ge(S["d_x4c"].h, i - 1)
                    S["d_lgl2"].bump(sync.dma_start(
                        out=LGR[p][:], in_=lgsc[128 * i:128 * (i + 1), :]), 16)
                    S["d_x3l4"].bump(sync.dma_start(
                        out=XT[p][:], in_=x3_h[128 * i:128 * (i + 1), :]), 16)
                    sync.wait_ge(S["d_x4c"].h, i + 1)
                    S["d_x4s"].bump(sync.dma_start(
                        out=x4_h[128 * i:128 * (i + 1), :], in_=X4T[p][:]), 16)
            sync.wait_ge(S["d_x4s"].h, 16 * 32)

        @block.gpsimd
        def _(gpsimd):
            gpsimd.iota(out=idm[:], pattern=[[1, 128]], base=0, channel_multiplier=-1,
                        allow_small_or_imprecise_dtypes=True)
            S["d_io"].bump(gpsimd.engine_nop())
            S["d_w"].bump(gpsimd.dma_start(out=BGB[:], in_=bgb_h[:, 0:2048]), 16)
            S["d_w"].bump(gpsimd.dma_start(out=BUB[:], in_=bub_h[:, 0:2048]), 16)
            S["d_w"].bump(gpsimd.dma_start(out=BDB[:], in_=bdb_h[:, 0:1024]), 16)
            for wh, wsb in [(wg_h, WGU), (wu_h, WUU)]:
                S["d_w"].bump(gpsimd.dma_start(
                    out=wsb[:].rearrange("p (dc f) -> p dc f", dc=8),
                    in_=wh[0:1024, :].rearrange("(dc p) f -> p dc f", p=128)), 16)
            S["d_w"].bump(gpsimd.dma_start(
                out=WDD[:].rearrange("p (fc d) -> p fc d", fc=16),
                in_=wd_h[0:2048, :].rearrange("(fc p) d -> p fc d", p=128)), 16)
            gpsimd.wait_ge(S["d_mxs"].h, 16)
            gpsimd.collective_compute(
                "AllReduce", ALU.max, replica_groups=[list(range(NCORE))],
                ins=[mx_in[:]], outs=[mx_out[:]]).then_inc(S["d_ar"].h, 1)
            gpsimd.wait_ge(S["d_dte"].h, 1)
            S["d_dens"].bump(gpsimd.dma_start(out=dsc[:], in_=DET[:]), 16)
            gpsimd.wait_ge(S["d_dens"].h, 16)
            for er in range(8):
                for b in range(4):
                    S["d_dens"].bump(gpsimd.dma_start(
                        out=rs_in[513 * er + 512: 513 * er + 513,
                                  b * 128:(b + 1) * 128],
                        in_=dsc[b * 8 + er: b * 8 + er + 1, :]), 16)
            gpsimd.wait_ge(S["d_pss"].h, 16 * 32)
            gpsimd.wait_ge(S["d_dens"].h, 16 * 33)
            gpsimd.collective_compute(
                "ReduceScatter", ALU.add, replica_groups=[list(range(NCORE))],
                ins=[rs_in[:]], outs=[rs_out[:]]).then_inc(S["d_rs"].h, 1)
            gpsimd.wait_ge(S["d_rs"].h, 1)
            for b in range(4):
                S["d_drl"].bump(gpsimd.dma_start(
                    out=DR4[b][:],
                    in_=rs_out[512:513, b * 128:(b + 1) * 128]), 16)
            gpsimd.wait_ge(S["d_eo"].h, 8)
            S["d_w2"].bump(gpsimd.dma_start(out=BGB[:], in_=bgb_h[:, 2048:4096]), 16)
            S["d_w2"].bump(gpsimd.dma_start(out=BUB[:], in_=bub_h[:, 2048:4096]), 16)
            S["d_w2"].bump(gpsimd.dma_start(out=BDB[:], in_=bdb_h[:, 1024:2048]), 16)
            for wh, wsb in [(wg_h, WGU), (wu_h, WUU)]:
                S["d_w2"].bump(gpsimd.dma_start(
                    out=wsb[:].rearrange("p (dc f) -> p dc f", dc=8),
                    in_=wh[1024:2048, :].rearrange("(dc p) f -> p dc f", p=128)), 16)
            S["d_w2"].bump(gpsimd.dma_start(
                out=WDD[:].rearrange("p (fc d) -> p fc d", fc=16),
                in_=wd_h[2048:4096, :].rearrange("(fc p) d -> p fc d", p=128)), 16)
            gpsimd.wait_ge(S["d_eos"].h, 16 * 8)
            gpsimd.collective_compute(
                "AllGather", ALU.bypass, replica_groups=[list(range(NCORE))],
                ins=[eosc[:]], outs=[eog[:]]).then_inc(S["d_ag"].h, 1)

        @block.tensor
        def _(tensor):
            tensor.wait_ge(S["d_vr"].h, 1)
            for st in range(8):
                p = st % 2
                tensor.wait_ge(S["d_sec"].h, st + 1)
                for dc in range(8):
                    n_ = 8 * st + dc
                    if n_ >= 2:
                        tensor.wait_ge(S["d_sete"].h, n_ - 1)
                    S["d_set"].bump(tensor.transpose(
                        out=(pT1 if dc % 2 == 0 else pT2)[:, 0:128],
                        in_=XNc[0][:, dc * 128:(dc + 1) * 128],
                        identity=identb[:]))
            for b in range(4):
                for nt in range(8):
                    i = b * 8 + nt
                    p = i % 2
                    tensor.wait_ge(S["d_xnc"].h, i + 1)
                    for dc in range(8):
                        n_ = 8 * i + dc
                        if n_ >= 2:
                            tensor.wait_ge(S["d_xnte"].h, n_ - 1)
                        S["d_xnt"].bump(tensor.transpose(
                            out=(pT1 if dc % 2 == 0 else pT2)[:, 0:128],
                            in_=XNc[p][:, dc * 128:(dc + 1) * 128],
                            identity=identb[:]))
                tensor.wait_ge(S["d_xnte"].h, 64 * (b + 1))
                for nt in range(8):
                    i = b * 8 + nt
                    if i >= 1:
                        tensor.wait_ge(S["d_lge"].h, 2 * i)
                    for hf in range(2):
                        for dc in range(8):
                            mm = tensor.matmul(
                                out=(pA if hf == 0 else pB)[:],
                                lhsT=XF[:, dc * 1024 + nt * 128:
                                        dc * 1024 + (nt + 1) * 128],
                                rhs=sefm[:, dc * 1024 + hf * 512:
                                         dc * 1024 + (hf + 1) * 512],
                                start=(dc == 0), stop=(dc == 7))
                    S["d_lg"].bump(mm)
                for er in range(8):
                    ber = b * 8 + er
                    if ber >= 1:
                        tensor.wait_ge(S["d_mx"].h, ber)
                    for hf in range(2):
                        for dc in range(8):
                            mm = tensor.matmul(
                                out=(pC if hf == 0 else pD)[:],
                                lhsT=sefm[:, dc * 1024 + er * 128:
                                          dc * 1024 + (er + 1) * 128],
                                rhs=XF[:, dc * 1024 + hf * 512:
                                       dc * 1024 + (hf + 1) * 512],
                                start=(dc == 0), stop=(dc == 7))
                    S["d_lgt"].bump(mm)
            tensor.wait_ge(S["d_mx"].h, 32)
            S["d_mpt"].bump(tensor.transpose(out=pE[0:32, 0:128],
                                             in_=MP[:], identity=identf[:]))
            tensor.wait_ge(S["d_ar"].h, 1)
            for b in range(4):
                for er in range(8):
                    ber = b * 8 + er
                    tensor.wait_ge(S["d_mrl"].h, 16 * (ber + 1))
                    tensor.wait_ge(S["d_mpte"].h, 1)
                    if ber >= 1:
                        tensor.wait_ge(S["d_mbe"].h, ber)
                    S["d_mb"].bump(tensor.matmul(
                        out=pE[0:128, 0:128], lhsT=ones1_128f[:], rhs=MRrow[:],
                        start=True, stop=True))
                for er in range(8):
                    ber = b * 8 + er
                    tensor.wait_ge(S["d_pex"].h, 8 * (b + 1))
                    tensor.wait_ge(S["d_sxl"].h, 16 * 8 * (b + 1))
                    if ber >= 1:
                        tensor.wait_ge(S["d_sle"].h, 2 * ber)
                        tensor.wait_ge(S["d_den"].h, ber)
                    for nt in range(8):
                        for hf in range(2):
                            tensor.matmul(
                                out=(pA if hf == 0 else pB)[:],
                                lhsT=XF[:, nt * 1024 + er * 128:
                                        nt * 1024 + (er + 1) * 128],
                                rhs=XNB8[:, nt * 1024 + hf * 512:
                                         nt * 1024 + (hf + 1) * 512],
                                start=(nt == 0), stop=(nt == 7))
                        mm = tensor.matmul(
                            out=pE[0:128, 128:129],
                            lhsT=XF[:, nt * 1024 + er * 128:
                                    nt * 1024 + (er + 1) * 128],
                            rhs=onescol[:], start=(nt == 0), stop=(nt == 7))
                    S["d_sl"].bump(mm)
            tensor.wait_ge(S["d_dt"].h, 1)
            S["d_mpt"].bump(tensor.transpose(out=pE[0:32, 0:128], in_=DENT[:],
                                             identity=identf[:]))
            for b in range(4):
                tensor.wait_ge(S["d_drl"].h, 16 * (b + 1))
                if b >= 1:
                    tensor.wait_ge(S["d_drte"].h, b)
                S["d_drt"].bump(tensor.transpose(
                    out=pE[0:128, 128:129], in_=DR4[b][:],
                    identity=identf[0:1, 0:1]))
                tensor.wait_ge(S["d_slc"].h, b + 1)
                for dc in range(8):
                    n_ = 8 * b + dc
                    if n_ >= 2:
                        tensor.wait_ge(S["d_slte"].h, n_ - 1)
                    S["d_slt"].bump(tensor.transpose(
                        out=(pT1 if dc % 2 == 0 else pT2)[:, 0:128],
                        in_=SLB[:, dc * 128:(dc + 1) * 128], identity=identb[:]))
            for e in range(2):
                if e == 0:
                    tensor.wait_ge(S["d_w"].h, 96)
                else:
                    tensor.wait_ge(S["d_w2"].h, 96)
                for b in range(4):
                    eb = e * 4 + b
                    tensor.wait_ge(S["d_slte"].h, 8 * (b + 1))
                    for fc in range(4):
                        n_ = eb * 4 + fc
                        if n_ >= 1:
                            tensor.wait_ge(S["d_ge"].h, 2 * n_)
                            tensor.wait_ge(S["d_gu"].h, n_)
                        for dc in range(8):
                            mmg = tensor.matmul(
                                out=pA[0:64, :],
                                lhsT=slfm[:, b * 1024 + dc * 128 + e * 64:
                                          b * 1024 + dc * 128 + e * 64 + 64],
                                rhs=WGU[:, dc * 2048 + fc * 512:
                                        dc * 2048 + (fc + 1) * 512],
                                start=(dc == 0), stop=(dc == 7))
                        for dc in range(8):
                            mmu = tensor.matmul(
                                out=pB[0:64, :],
                                lhsT=slfm[:, b * 1024 + dc * 128 + e * 64:
                                          b * 1024 + dc * 128 + e * 64 + 64],
                                rhs=WUU[:, dc * 2048 + fc * 512:
                                        dc * 2048 + (fc + 1) * 512],
                                start=(dc == 0), stop=(dc == 7))
                        S["d_g"].bump(mmu)
                    for fc16 in range(16):
                        n_ = eb * 16 + fc16
                        tensor.wait_ge(S["d_gu"].h, 4 * (eb + 1))
                        if n_ >= 2:
                            tensor.wait_ge(S["d_gute"].h, n_ - 1)
                        S["d_gut"].bump(tensor.transpose(
                            out=(pT1 if fc16 % 2 == 0 else pT2)[:, 0:64],
                            in_=GUB[:, fc16 * 128:(fc16 + 1) * 128],
                            identity=identb[0:64, 0:64]))
                    tensor.wait_ge(S["d_gute"].h, 16 * (eb + 1))
                    if eb >= 1:
                        tensor.wait_ge(S["d_eoe"].h, 2 * eb)
                    for hf in range(2):
                        for fc16 in range(16):
                            mm = tensor.matmul(
                                out=(pC if hf == 0 else pD)[0:64, :],
                                lhsT=gufm[:, fc16 * 64:(fc16 + 1) * 64],
                                rhs=WDD[:, fc16 * 1024 + hf * 512:
                                        fc16 * 1024 + (hf + 1) * 512],
                                start=(fc16 == 0), stop=(fc16 == 15))
                        S["d_eo"].bump(mm)
            tensor.wait_ge(S["d_ag"].h, 1)
            for b in range(4):
                tensor.wait_ge(S["d_eofl"].h, 16 * 8 * (b + 1))
                for nt in range(8):
                    i = b * 8 + nt
                    tensor.wait_ge(S["d_p2"].h, i + 1)
                    for dc in range(8):
                        n_ = 8 * i + dc
                        if n_ >= 2:
                            tensor.wait_ge(S["d_p2te"].h, n_ - 1)
                        S["d_p2t"].bump(tensor.transpose(
                            out=(pT1 if dc % 2 == 0 else pT2)[:, 0:128],
                            in_=LGR[nt % 2][:, dc * 128:(dc + 1) * 128],
                            identity=identb[:]))
                    tensor.wait_ge(S["d_p2te"].h, 8 * (i + 1))
                    if i >= 1:
                        tensor.wait_ge(S["d_x4c"].h, i)
                    for hf in range(2):
                        for er in range(8):
                            mm = tensor.matmul(
                                out=(pC if hf == 0 else pD)[:],
                                lhsT=P2T[0][:, er * 128:(er + 1) * 128],
                                rhs=XNB8[:, er * 1024 + hf * 512:
                                         er * 1024 + (hf + 1) * 512],
                                start=(er == 0), stop=(er == 7))
                        S["d_cm"].bump(mm)

        @block.scalar
        def _(scalar):
            def ln_scalar(k):
                scalar.wait_ge(S["d_c"].h, k + 1)
                scalar.activation(out=PSL[:], in_=scr[:], func=AF.Square,
                                  accum_out=stat[:, 2:3])
                scalar.drain()
                scalar.activation(out=stat[:, 3:4], in_=stat[:, 2:3], func=AF.Ln,
                                  scale=1.0 / 1024, bias=cb_eps[:])
                scalar.drain()
                S["d_r"].bump(scalar.activation(out=stat[:, 3:4], in_=stat[:, 3:4],
                                                func=AF.Exp, scale=-0.5))
            for k in range(40):
                ln_scalar(k)
            for b in range(4):
                for nt in range(8):
                    i = b * 8 + nt
                    scalar.wait_ge(S["d_psub"].h, i + 1)
                    if b >= 1 and nt == 0:
                        scalar.wait_ge(S["d_sl"].h, 8 * b)
                    S["d_pex"].bump(scalar.activation(
                        out=XF[:, nt * 1024:(nt + 1) * 1024], in_=scr[:],
                        func=AF.Exp))
            for e in range(2):
                for b in range(4):
                    eb = e * 4 + b
                    for fc in range(4):
                        n_ = eb * 4 + fc
                        scalar.wait_ge(S["d_g"].h, n_ + 1)
                        scalar.wait_ge(S["d_ge"].h, 2 * n_ + 1)
                        S["d_ge"].bump(scalar.activation(
                            out=GT[:, fc * 512:(fc + 1) * 512],
                            in_=GT[:, fc * 512:(fc + 1) * 512], func=AF.Gelu))
            for b in range(4):
                for nt in range(8):
                    i = b * 8 + nt
                    p = nt % 2
                    scalar.wait_ge(S["d_nm"].h, i + 1)
                    if i >= 2:
                        scalar.wait_ge(S["d_p2t"].h, 8 * (i - 1))
                    S["d_p2"].bump(scalar.activation(
                        out=LGR[p][:], in_=LGR[p][:], func=AF.Exp,
                        bias=NMb[p][:], accum_out=RSb[p][:]))

        @block.vector
        def _(vector):
            vector.wait_ge(S["d_io"].h, 1)
            vector.tensor_scalar(out=identb[:], in0=idm[:], scalar1=0.0, scalar2=None,
                                 op0=ALU.is_equal)
            vector.tensor_scalar(out=identf[:], in0=idm[:], scalar1=0.0, scalar2=None,
                                 op0=ALU.is_equal)
            vector.memset(ones1_128f[:], 1.0)
            vector.memset(onescol[:], 1.0)
            vector.memset(cb_eps[:], LNEPS)
            S["d_vr"].bump(vector.engine_nop())

            def ln_event(xin_ap, out_ap, scale_w=None, bias_w=None):
                vector.tensor_reduce(out=stat[:, 0:1], in_=xin_ap, axis=AX.X, op=ALU.add)
                vector.drain()
                vector.tensor_scalar(out=stat[:, 1:2], in0=stat[:, 0:1],
                                     scalar1=1.0 / 1024, scalar2=None, op0=ALU.mult)
                vector.drain()
                v = S["d_c"].bump(vector.tensor_scalar(
                    out=scr[:], in0=xin_ap, scalar1=stat[:, 1:2], scalar2=None,
                    op0=ALU.subtract))
                vector.wait_ge(S["d_r"].h, v)
                vector.tensor_scalar(out=out_ap, in0=scr[:], scalar1=stat[:, 3:4],
                                     scalar2=None, op0=ALU.mult)
                if scale_w is not None:
                    vector.tensor_tensor(out=out_ap, in0=out_ap, in1=scale_w, op=ALU.mult)
                    vector.tensor_tensor(out=out_ap, in0=out_ap, in1=bias_w, op=ALU.add)

            vector.wait_ge(S["d_pl"].h, 32)
            for st in range(8):
                p = st % 2
                vector.wait_ge(S["d_sel"].h, 16 * (st + 1))
                ln_event(SEt[p][:], SEt[p][:], scale_w=PB4[:, 2048:3072],
                         bias_w=PB4[:, 3072:4096])
                vector.drain()
                vector.tensor_scalar(out=SEt[p][:], in0=SEt[p][:], scalar1=scl[:],
                                     scalar2=None, op0=ALU.mult)
                S["d_sec"].bump(vector.tensor_copy(out=XNc[0][:], in_=SEt[p][:]))
                for dc in range(8):
                    vector.wait_ge(S["d_set"].h, 8 * st + dc + 1)
                    S["d_sete"].bump(vector.tensor_copy(
                        out=sefm[:, dc * 1024 + st * 128: dc * 1024 + (st + 1) * 128],
                        in_=(pT1 if dc % 2 == 0 else pT2)[:, 0:128]))
            for b in range(4):
                for nt in range(8):
                    i = b * 8 + nt
                    p = i % 2
                    vector.wait_ge(S["d_x3l"].h, 16 * (i + 1))
                    if i >= 2:
                        vector.wait_ge(S["d_xns"].h, 16 * (i - 1))
                    ln_event(XT[p][:], XNc[p][:], scale_w=PB4[:, 0:1024],
                             bias_w=PB4[:, 1024:2048])
                    S["d_xnc"].bump(vector.engine_nop())
                    for dc in range(8):
                        vector.wait_ge(S["d_xnt"].h, 8 * i + dc + 1)
                        S["d_xnte"].bump(vector.tensor_copy(
                            out=XF[:, dc * 1024 + nt * 128: dc * 1024 + (nt + 1) * 128],
                            in_=(pT1 if dc % 2 == 0 else pT2)[:, 0:128]))
                for nt in range(8):
                    i = b * 8 + nt
                    vector.wait_ge(S["d_lg"].h, i + 1)
                    if i >= 2:
                        vector.wait_ge(S["d_lgs"].h, 16 * (i - 1))
                    vector.tensor_copy(out=LGR[i % 2][:, 0:512], in_=pA[:])
                    S["d_lge"].bump(vector.tensor_copy(
                        out=LGR[i % 2][:, 512:1024], in_=pB[:]))
                    S["d_lge"].bump(vector.engine_nop())
                for er in range(8):
                    ber = b * 8 + er
                    vector.wait_ge(S["d_lgt"].h, ber + 1)
                    vector.tensor_reduce(out=stat[:, 0:1], in_=pC[:], axis=AX.X,
                                         op=ALU.max)
                    vector.tensor_reduce(out=stat[:, 1:2], in_=pD[:], axis=AX.X,
                                         op=ALU.max)
                    vector.drain()
                    S["d_mx"].bump(vector.tensor_tensor(
                        out=MP[:, ber:ber + 1], in0=stat[:, 0:1], in1=stat[:, 1:2],
                        op=ALU.max))
            vector.wait_ge(S["d_mpt"].h, 1)
            S["d_mpte"].bump(vector.tensor_copy(out=DET[:], in_=pE[0:32, 0:128]))
            for b in range(4):
                for er in range(8):
                    ber = b * 8 + er
                    vector.wait_ge(S["d_mb"].h, ber + 1)
                    S["d_mbe"].bump(vector.tensor_copy(
                        out=MB[:, er * 128:(er + 1) * 128], in_=pE[0:128, 0:128]))
                for nt in range(8):
                    i = b * 8 + nt
                    p = nt % 2
                    vector.wait_ge(S["d_lgl"].h, 16 * (i + 1))
                    if i >= 1:
                        vector.wait_ge(S["d_pex"].h, i)
                    S["d_psub"].bump(vector.tensor_tensor(
                        out=scr[:], in0=LGR[p][:], in1=MB[:], op=ALU.subtract))
                for er in range(8):
                    ber = b * 8 + er
                    vector.wait_ge(S["d_sl"].h, ber + 1)
                    if ber >= 1:
                        vector.wait_ge(S["d_pss"].h, 16 * ber)
                    vector.tensor_copy(out=PSL[:, 0:512], in_=pA[:])
                    S["d_sle"].bump(vector.tensor_copy(out=PSL[:, 512:1024], in_=pB[:]),
                                    2)
                    S["d_den"].bump(vector.tensor_copy(
                        out=DENT[:, ber:ber + 1], in_=pE[0:128, 128:129]))
            vector.wait_ge(S["d_den"].h, 32)
            S["d_dt"].bump(vector.engine_nop())
            vector.wait_ge(S["d_mpt"].h, 2)
            vector.wait_ge(S["d_mxs"].h, 16)
            S["d_dte"].bump(vector.tensor_copy(out=DET[:], in_=pE[0:32, 0:128]))
            for b in range(4):
                vector.wait_ge(S["d_rsl"].h, 16 * (b + 1))
                vector.wait_ge(S["d_drt"].h, b + 1)
                vector.tensor_copy(out=DRC[:, b:b + 1], in_=pE[0:128, 128:129])
                vector.drain()
                vector.reciprocal(out=DRC[:, b:b + 1], in_=DRC[:, b:b + 1])
                vector.drain()
                vector.tensor_scalar(out=scr[:], in0=SLB[:], scalar1=DRC[:, b:b + 1],
                                     scalar2=None, op0=ALU.mult)
                S["d_drte"].bump(vector.engine_nop())
                S["d_slc"].bump(vector.tensor_copy(out=SLB[:], in_=scr[:]))
                for dc in range(8):
                    vector.wait_ge(S["d_slt"].h, 8 * b + dc + 1)
                    S["d_slte"].bump(vector.tensor_copy(
                        out=slfm[:, b * 1024 + dc * 128: b * 1024 + (dc + 1) * 128],
                        in_=(pT1 if dc % 2 == 0 else pT2)[:, 0:128]))
            for e in range(2):
                for b in range(4):
                    eb = e * 4 + b
                    for fc in range(4):
                        n_ = eb * 4 + fc
                        vector.wait_ge(S["d_g"].h, n_ + 1)
                        S["d_ge"].bump(vector.tensor_tensor(
                            out=GT[:, fc * 512:(fc + 1) * 512], in0=pA[0:64, :],
                            in1=BGB[0:64, fc * 512:(fc + 1) * 512], op=ALU.add))
                        vector.wait_ge(S["d_ge"].h, 2 * n_ + 2)
                        vector.tensor_tensor(
                            out=scr[0:64, 0:512], in0=pB[0:64, :],
                            in1=BUB[0:64, fc * 512:(fc + 1) * 512], op=ALU.add)
                        vector.drain()
                        S["d_gu"].bump(vector.tensor_tensor(
                            out=GUB[:, fc * 512:(fc + 1) * 512],
                            in0=GT[:, fc * 512:(fc + 1) * 512],
                            in1=scr[0:64, 0:512], op=ALU.mult))
                    for fc16 in range(16):
                        n_ = eb * 16 + fc16
                        vector.wait_ge(S["d_gut"].h, n_ + 1)
                        S["d_gute"].bump(vector.tensor_copy(
                            out=gufm[:, fc16 * 64:(fc16 + 1) * 64],
                            in_=(pT1 if fc16 % 2 == 0 else pT2)[:, 0:64]))
                    vector.wait_ge(S["d_eo"].h, 2 * (eb + 1))
                    if eb >= 2:
                        vector.wait_ge(S["d_eos"].h, 16 * (eb - 1))
                    vector.tensor_tensor(out=EOB[e * 64:(e + 1) * 64, 0:512],
                                         in0=pC[0:64, :],
                                         in1=BDB[0:64, 0:512], op=ALU.add)
                    S["d_eoe"].bump(vector.tensor_tensor(
                        out=EOB[e * 64:(e + 1) * 64, 512:1024], in0=pD[0:64, :],
                        in1=BDB[0:64, 512:1024], op=ALU.add), 2)
            for b in range(4):
                for nt in range(8):
                    i = b * 8 + nt
                    p = nt % 2
                    vector.wait_ge(S["d_lgl2"].h, 16 * (i + 1))
                    if i >= 2:
                        vector.wait_ge(S["d_p2"].h, i - 1)
                    vector.tensor_reduce(out=NMb[p][:], in_=LGR[p][:], axis=AX.X,
                                         op=ALU.max, negate=True)
                    S["d_nm"].bump(vector.engine_nop())
                    for dc in range(8):
                        vector.wait_ge(S["d_p2t"].h, 8 * i + dc + 1)
                        S["d_p2te"].bump(vector.tensor_copy(
                            out=P2T[0][:, dc * 128:(dc + 1) * 128],
                            in_=(pT1 if dc % 2 == 0 else pT2)[:, 0:128]))
                    vector.wait_ge(S["d_cm"].h, 2 * (i + 1))
                    vector.drain()
                    vector.reciprocal(out=rsum[:, 1:2], in_=RSb[p][:])
                    vector.drain()
                    vector.wait_ge(S["d_x3l4"].h, 16 * (i + 1))
                    if i >= 2:
                        vector.wait_ge(S["d_x4s"].h, 16 * (i - 1))
                    vector.scalar_tensor_tensor(
                        out=X4T[p][:, 0:512], in0=pC[:], scalar=rsum[:, 1:2],
                        in1=XT[p][:, 0:512], op0=ALU.mult, op1=ALU.add)
                    S["d_x4c"].bump(vector.scalar_tensor_tensor(
                        out=X4T[p][:, 512:1024], in0=pD[:], scalar=rsum[:, 1:2],
                        in1=XT[p][:, 512:1024], op0=ALU.mult, op1=ALU.add))

    return nc


def run_stage_d(inputs, x3_full):
    x3 = np.ascontiguousarray(x3_full.reshape(NB, NSEQ, D))
    sg = inputs["L1_ffn_slot_g"]; sb = inputs["L1_ffn_slot_b"]
    ng = inputs["L1_ffn_norm_g"]; nb = inputs["L1_ffn_norm_b"]
    pball = np.ascontiguousarray(np.broadcast_to(
        np.concatenate([ng, nb, sg, sb])[None, :], (128, 4096)).astype(np.float32))
    common = {
        "semb": np.ascontiguousarray(inputs["L1_ffn_slot_embeds"].reshape(1024, D)),
        "pball": pball,
        "sclb": np.ascontiguousarray(
            np.broadcast_to(inputs["L1_ffn_scale"].reshape(1, 1), (128, 1)).astype(np.float32)),
    }
    in_maps = []
    for j in range(NCORE):
        m = dict(common)
        m["x3"] = np.ascontiguousarray(x3[:, j * NLOC:(j + 1) * NLOC, :]).reshape(ROWS, D)
        e0, e1 = 2 * j, 2 * j + 1
        m["wgd"] = np.ascontiguousarray(inputs["L1_ffn_Wg"][[e0, e1]].reshape(2 * D, 2048))
        m["wud"] = np.ascontiguousarray(inputs["L1_ffn_Wu"][[e0, e1]].reshape(2 * D, 2048))
        m["wdd"] = np.ascontiguousarray(inputs["L1_ffn_Wd"][[e0, e1]].reshape(2 * 2048, D))
        m["bgball"] = np.ascontiguousarray(np.broadcast_to(
            inputs["L1_ffn_bg"][[e0, e1]].reshape(1, 4096), (128, 4096)).astype(np.float32))
        m["buball"] = np.ascontiguousarray(np.broadcast_to(
            inputs["L1_ffn_bu"][[e0, e1]].reshape(1, 4096), (128, 4096)).astype(np.float32))
        m["bdball"] = np.ascontiguousarray(np.broadcast_to(
            inputs["L1_ffn_bd"][[e0, e1]].reshape(1, 2048), (128, 2048)).astype(np.float32))
        in_maps.append(m)
    nc = build_stage_d()
    res = run_bass_kernel_spmd(nc, in_maps, core_ids=list(range(NCORE)))
    x4 = np.zeros((NB, NSEQ, D), np.float32)
    for j in range(NCORE):
        x4[:, j * NLOC:(j + 1) * NLOC, :] = res.results[j]["x4"].reshape(NB, NLOC, D)
    return x4


def kernel(**inputs):
    x1 = run_stage_a(inputs)
    x2 = run_stage_b(inputs, x1)
    x3 = run_stage_c(inputs, x2)
    x4 = run_stage_d(inputs, x3)
    return x4.reshape(NB, NSEQ, D)
